# version 01
# baseline (speedup 1.0000x reference)
import numpy as np
import ml_dtypes
import concourse.bass as bass
import concourse.mybir as mybir
from concourse.bass_utils import run_bass_kernel_spmd

F32 = mybir.dt.float32
BF16 = mybir.dt.bfloat16
FP8 = mybir.dt.float8e4
I8 = mybir.dt.int8
AF = mybir.ActivationFunctionType
ALU = mybir.AluOpType
AXX = mybir.AxisListType.X
AP = bass.AP

D = 1024
DH = 4096
DS = 2816
H = 16
B = 2
N = 8192
CHUNK = 2048
NCORE = 8

OFFSETS = sorted(set(range(0, 33)) | {48, 64, 96, 128, 192, 256, 384, 512, 768, 1024, 1536})
FAR_SUB = [o // 16 for o in OFFSETS if o > 32]

NQ = 2080
NKV = 3616
NXW = 3712
XKV = 2
KVQ = 1536

NQH = 1040
NKH = 2576
W_NEAR = 160
W_FARM = 224
SUBQ = 65
W_FARH = 161

NEG = -30000.0
LN_EPS = 1e-5
RMAGIC = 12582912.0
RMS_EPS = 1e-6


class Sem:
    def __init__(self, nc, name):
        self.h = nc.alloc_semaphore(name=name)
        self.n = 0


class AD:
    _OPS = {"tensor_tensor", "tensor_scalar", "tensor_copy", "tensor_add",
            "tensor_sub", "tensor_mul", "tensor_max", "scalar_tensor_tensor",
            "tensor_reduce", "reduce_max", "reduce_sum", "reciprocal",
            "activation", "copy", "memset", "add", "mul", "sqrt", "square"}

    def __init__(self, eng):
        self._e = eng
        self._n = 0

    def __getattr__(self, k):
        a = getattr(self._e, k)
        if k in AD._OPS:
            def f(*args, **kw):
                self._e.drain()
                self._n += 1
                return a(*args, **kw)
            return f
        return a



def host_consts():
    ident = np.eye(128, dtype=np.float32)
    selq = np.zeros((128, 2), dtype=np.float32)
    selq[0:64, 0] = 1.0
    selq[64:128, 1] = 1.0
    sel2 = np.zeros((2, 128), dtype=np.float32)
    sel2[0, 0:64] = 1.0
    sel2[1, 64:128] = 1.0
    ind_near = np.zeros((45, 128, W_NEAR), dtype=np.float32)
    for i in range(128):
        for o in range(0, 33):
            c = i + 32 - o
            if 0 <= c < W_NEAR:
                ind_near[OFFSETS.index(o), i, c] = 1.0
    ind_near[44] = 1.0 - np.minimum(ind_near[0:44].sum(axis=0), 1.0)
    ind_far = np.zeros((45, 128, W_FARM), dtype=np.float32)
    for i in range(128):
        for os_ in FAR_SUB:
            c = i + 96 - os_
            if 0 <= c < W_FARM:
                ind_far[OFFSETS.index(os_ * 16), i, c] = 1.0
    ind_far[44] = 1.0 - np.minimum(ind_far[0:44].sum(axis=0), 1.0)
    selh = np.zeros((16, 8 * 128), dtype=np.float32)
    for c in range(8):
        selh[2 * c, c * 128:c * 128 + 64] = 1.0
        selh[2 * c + 1, c * 128 + 64:c * 128 + 128] = 1.0
    return dict(
        selh=selh,
        ident=ident.astype(ml_dtypes.bfloat16),
        identf=ident,
        selq=selq.astype(ml_dtypes.bfloat16),
        selqf=selq,
        sel2f=sel2,
        row1=np.ones((1, 128), dtype=ml_dtypes.bfloat16),
        row1f=np.ones((1, 128), dtype=np.float32),
        onescol=np.ones((128, 1), dtype=np.float32),
        ind_near=ind_near.reshape(45, -1).astype(ml_dtypes.bfloat16),
        ind_far=ind_far.reshape(45, -1).astype(ml_dtypes.bfloat16),
    )


_CONSTS = None


def host_core_inputs(core, inputs):
    global _CONSTS
    if _CONSTS is None:
        _CONSTS = host_consts()
    b, j = core // 4, core % 4
    s0 = j * CHUNK
    x = np.asarray(inputs["L0_attn_x"])
    lo, hi = s0 - 1554, s0 - 1554 + NXW
    pl, ph = max(0, -lo), max(0, hi - N)
    xw = np.zeros((NXW, D), dtype=np.float32)
    xw[pl:NXW - ph] = x[b, max(0, lo):min(N, hi)]
    cn = np.zeros((1, W_NEAR), dtype=np.float32)
    cf = np.zeros((1, W_FARM), dtype=np.float32)
    if j == 0:
        cn[0, 0:48] = NEG
        cf[0, 0:97] = NEG
    edge_l = np.full((128, 1), 0.0 if j == 0 else 1.0, dtype=np.float32)
    edge_r = np.full((128, 1), 0.0 if j == 3 else 1.0, dtype=np.float32)
    pb = np.concatenate([np.asarray(inputs["L0_attn_pos_bias"], dtype=np.float32),
                         np.full((1, H), NEG, dtype=np.float32)], axis=0)
    return dict(
        xw=xw, causal_n=cn, causal_f=cf, pb=pb, edge_l=edge_l, edge_r=edge_r,
        w_qkv=np.asarray(inputs["L0_attn_w_qkv"], dtype=np.float32),
        w_out=np.asarray(inputs["L0_attn_w_out"], dtype=np.float32),
        w_gate=np.asarray(inputs["L0_attn_w_gate"], dtype=np.float32),
        b_gate=np.ascontiguousarray(np.asarray(inputs["L0_attn_b_gate"], dtype=np.float32).reshape(8, 128).T),
        n1s=np.asarray(inputs["L0_attn_norm1_scale"], dtype=np.float32).reshape(1, D),
        n2s=np.asarray(inputs["L0_attn_norm2_scale"], dtype=np.float32).reshape(1, D),
        w_fg=np.asarray(inputs["L0_attn_w_ffn_gate"], dtype=np.float32),
        w_fu=np.asarray(inputs["L0_attn_w_ffn_up"], dtype=np.float32),
        w_fd=np.asarray(inputs["L0_attn_w_ffn_down"], dtype=np.float32),
        w1=np.asarray(inputs["L0_ffn_w1"], dtype=np.float32),
        w2=np.asarray(inputs["L0_ffn_w2"], dtype=np.float32),
        convw=np.ascontiguousarray(
            np.asarray(inputs["L0_ffn_conv_w"], dtype=np.float32)
            .reshape(32, 128, 3).transpose(1, 0, 2).reshape(128, 96)),
        convb=np.ascontiguousarray(
            np.asarray(inputs["L0_ffn_conv_b"], dtype=np.float32)
            .reshape(32, 128).T),
        **_CONSTS,
    )



class KB:
    def __init__(self, nc, debug=(), upto=None):
        self.nc = nc
        self.debug = set(debug)
        self.upto = upto
        self.io = {}
        self.sb = {}
        self._guards = []
        self.s = {k: Sem(nc, k) for k in
                  ["xld", "xldA", "xldB", "wld", "mm", "ev", "dv", "sc",
                   "sp", "spA", "spB", "gp", "ph"]}
        self.nbar = 0

    def sbuf(self, name, shape, dtype):
        g = self.nc.sbuf_tensor(name, shape, dtype)
        t = g.__enter__()
        self._guards.append(g)
        self._gmap = getattr(self, "_gmap", {})
        self._gmap[name] = g
        self.sb[name] = t
        return t

    def free(self, name):
        self._gmap[name].__exit__(None, None, None)
        del self.sb[name]

    def barrier(self, blk):
        self.nbar += 1
        tgt = self.nbar * 5
        ph = self.s["ph"]
        dma_waits = [(k, self.s[k].n) for k in
                     ["xld", "xldA", "xldB", "wld", "sp", "spA", "spB", "gp"]]
        for attr in ["sync", "tensor", "vector", "scalar", "gpsimd"]:
            @getattr(blk, attr)
            def _(eng, attr=attr):
                if attr == "sync":
                    for k, v in dma_waits:
                        eng.wait_ge(self.s[k].h, v)
                if attr == "gpsimd":
                    eng.sem_inc(ph.h, 1)
                else:
                    eng.drain().then_inc(ph.h, 1)
                eng.wait_ge(ph.h, tgt)
        ph.n = tgt


def declare_io(kb):
    nc = kb.nc

    def di(name, shape, dtype):
        kb.io[name] = nc.dram_tensor(name, shape, dtype, kind="ExternalInput").ap()

    def do(name, shape, dtype):
        kb.io[name] = nc.dram_tensor(name, shape, dtype, kind="ExternalOutput").ap()

    def ds(name, shape, dtype):
        kind = "ExternalOutput" if name in kb.debug else "Internal"
        kb.io[name] = nc.dram_tensor(name, shape, dtype, kind=kind).ap()

    di("xw", [NXW, D], F32)
    di("causal_n", [1, W_NEAR], F32)
    di("causal_f", [1, W_FARM], F32)
    di("edge_l", [128, 1], F32)
    di("edge_r", [128, 1], F32)
    di("pb", [45, H], F32)
    di("w_qkv", [D, 3 * D], F32)
    di("w_out", [D, D], F32)
    di("w_gate", [D, D], F32)
    di("b_gate", [128, 8], F32)
    di("n1s", [1, D], F32)
    di("n2s", [1, D], F32)
    di("w_fg", [D, DS], F32)
    di("w_fu", [D, DS], F32)
    di("w_fd", [DS, D], F32)
    di("w1", [DH, D], F32)
    di("w2", [D, DH], F32)
    di("convw", [128, 96], F32)
    di("convb", [128, 32], F32)
    di("ident", [128, 128], BF16)
    di("identf", [128, 128], F32)
    di("selq", [128, 2], BF16)
    di("selqf", [128, 2], F32)
    di("sel2f", [2, 128], F32)
    di("selh", [16, 8 * 128], F32)
    di("row1", [1, 128], BF16)
    di("row1f", [1, 128], F32)
    di("onescol", [128, 1], F32)
    di("ind_near", [45, 128 * W_NEAR], BF16)
    di("ind_far", [45, 128 * W_FARM], BF16)
    do("xout", [CHUNK, D], F32)
    ds("x1_scr", [NKV, D], F32)
    ds("k_scr", [128, 8 * NKV], BF16)
    ds("v_scr", [NKV, D], BF16)
    ds("q_scr", [128, 8 * NQ], BF16)
    ds("g_scr", [128, 8 * NQ], BF16)
    ds("ao_scr", [128, 8 * NQ], BF16)
    ds("x2_scr", [NQ, D], F32)
    ds("z0_scr", [NQ, D], F32)
    ds("gt_scr", [128, 22 * NQ], BF16)
    ds("x2b_scr", [NQ, D], F32)
    ds("mrow_scr", [H, 128 * (W_NEAR + W_FARM)], F32)
    ds("w1q_scr", [128, 8 * DH], FP8)
    ds("w2q_scr", [128, 32 * D], FP8)
    ds("dbg_h", [128, 32 * 130], F32)
    ds("dbg_ys", [128, 32 * 128], F32)
    ds("dbg_q2", [128, 32 * 128], F32)
    ds("dbg_q1", [128, 8 * 130], F32)
    ds("dbg_w1q", [128, 8 * DH], F32)
    ds("dbg_wsc", [128, 2], F32)

    sb = kb.sbuf
    sb("c_ident", [128, 128], BF16)
    sb("c_identf", [128, 128], F32)
    sb("c_row1", [1, 128], BF16)
    sb("c_row1f", [1, 128], F32)
    sb("c_ones", [128, 1], F32)
    sb("c_selq", [128, 2], BF16)
    sb("c_selqf", [128, 2], F32)
    sb("c_sel2f", [2, 128], F32)
    sb("c_selh", [16, 8 * 128], F32)
    sb("c_convw", [128, 32 * 3], F32)
    sb("c_convb", [128, 32], F32)
    sb("c_bg", [128, 8], F32)
    sb("c_edgel", [128, 1], F32)
    sb("c_edger", [128, 1], F32)
    sb("wsc", [128, 2], F32)
    sb("g1b", [128, D], BF16)
    sb("g2b", [128, D], BF16)



def quant_weight(kb, blk, src, nrow, ncol, dst, wsc_col, tag="qw"):
    nc, s, sb = kb.nc, kb.s, kb.sb
    rch, cch = nrow // 128, ncol // 128
    rps = max(1, 8192 // ncol)
    nst = (rch + rps - 1) // rps
    src3 = src.rearrange("(c p) d -> p c d", p=128)

    import contextlib
    with contextlib.ExitStack() as _es:
        ent = _es.enter_context
        stage = ent(nc.sbuf_tensor(tag + "_stage", [128, 8192], F32))
        dump = ent(nc.sbuf_tensor(tag + "_dump", [128, 8192], F32))
        wbf = ent(nc.sbuf_tensor(tag + "_bf", [128, 8192], BF16))
        acc = ent(nc.sbuf_tensor(tag + "_acc", [128, 8], F32))
        srow = ent(nc.sbuf_tensor(tag + "_srow", [1, 1], F32))
        pst = ent(nc.psum_tensor(tag + "_ps", [128, 1024], BF16))
        p1 = ent(nc.psum_tensor(tag + "_p1", [1, 1], F32))
        pbx = ent(nc.psum_tensor(tag + "_pb", [128, 1], F32))
        sc0, wld0 = s["sc"].n, s["wld"].n
        fence = {k: s[k].n for k in ["mm", "ev", "dv", "sc"]}

        @blk.sync
        def _(eng):
            for k, v in fence.items():
                eng.wait_ge(s[k].h, v)
            for st in range(nst):
                if st >= 1:
                    eng.wait_ge(s["sc"].h, sc0 + st)
                r0 = st * rps
                nr = min(rps, rch - r0)
                eng.dma_start(
                    out=stage.ap().rearrange("p (c d) -> p c d", c=rps)[:, 0:nr, 0:ncol]
                    if rps > 1 else stage[:, 0:ncol],
                    in_=src3[:, r0:r0 + nr, :]).then_inc(s["wld"].h, 16)
                s["wld"].n += 16

        @blk.scalar
        def _(eng):
            eng = AD(eng)
            for st in range(nst):
                eng.wait_ge(s["wld"].h, wld0 + 16 * (st + 1))
                nr = min(rps, rch - st * rps)
                eng.activation(dump[:, 0:nr * ncol], stage[:, 0:nr * ncol], AF.Abs,
                               accum_out=acc[:, st:st + 1]).then_inc(s["sc"].h, 1)
                s["sc"].n += 1

        @blk.vector
        def _(eng):
            eng = AD(eng)
            eng.wait_ge(s["sc"].h, sc0 + nst)
            for st in range(1, nst):
                eng.tensor_tensor(acc[:, 0:1], acc[:, 0:1], acc[:, st:st + 1], ALU.add)
            eng.drain().then_inc(s["dv"].h, 1)
            s["dv"].n += 1
        dv1 = s["dv"].n

        @blk.tensor
        def _(eng):
            eng.wait_ge(s["dv"].h, dv1)
            eng.matmul(p1[0:1, 0:1], sb["c_ones"][:, 0:1], acc[:, 0:1],
                       start=True, stop=True).then_inc(s["mm"].h, 1)
            s["mm"].n += 1
        mm1 = s["mm"].n

        @blk.vector
        def _(eng):
            eng = AD(eng)
            eng.wait_ge(s["mm"].h, mm1)
            eng.tensor_scalar(srow[0:1, 0:1], p1[0:1, 0:1], 1.0 / (nrow * ncol), 1e-5,
                              ALU.mult, ALU.max).then_inc(s["dv"].h, 1)
            s["dv"].n += 1
        dv2 = s["dv"].n

        @blk.tensor
        def _(eng):
            eng.wait_ge(s["dv"].h, dv2)
            eng.matmul(pbx[:, 0:1], sb["c_row1f"][0:1, :], srow[0:1, 0:1],
                       start=True, stop=True).then_inc(s["mm"].h, 1)
            s["mm"].n += 1
        mm2 = s["mm"].n

        @blk.vector
        def _(eng):
            eng = AD(eng)
            eng.wait_ge(s["mm"].h, mm2)
            eng.tensor_copy(sb["wsc"][:, wsc_col:wsc_col + 1], pbx[:, 0:1])
            eng.reciprocal(acc[:, 1:2], pbx[:, 0:1]).then_inc(s["dv"].h, 1)
            s["dv"].n += 1
        dv3 = s["dv"].n

        for st in range(nst):
            r0 = st * rps
            nr = min(rps, rch - r0)
            ev_pre = s["ev"].n

            @blk.sync
            def _(eng, st=st, r0=r0, nr=nr, ev_pre=ev_pre):
                eng.wait_ge(s["ev"].h, ev_pre)
                if st == 0:
                    eng.wait_ge(s["sc"].h, sc0 + nst)
                eng.dma_start(
                    out=stage.ap().rearrange("p (c d) -> p c d", c=rps)[:, 0:nr, 0:ncol]
                    if rps > 1 else stage[:, 0:ncol],
                    in_=src3[:, r0:r0 + nr, :]).then_inc(s["wld"].h, 16)
                s["wld"].n += 16
            wnow = s["wld"].n

            @blk.vector
            def _(eng, nr=nr, wnow=wnow, st=st):
                eng = AD(eng)
                eng.wait_ge(s["wld"].h, wnow)
                if st == 0:
                    eng.wait_ge(s["dv"].h, dv3)
                nel = nr * ncol
                eng.tensor_scalar(dump[:, 0:nel], stage[:, 0:nel], acc[:, 1:2],
                                  RMAGIC, ALU.mult, ALU.add)
                eng.tensor_scalar(dump[:, 0:nel], dump[:, 0:nel], RMAGIC, -1.0,
                                  ALU.subtract, ALU.max)
                eng.tensor_scalar(wbf[:, 0:nel], dump[:, 0:nel], 1.0, None,
                                  ALU.min).then_inc(s["dv"].h, 1)
                s["dv"].n += 1
            dnow = s["dv"].n
            evb = s["ev"].n

            @blk.tensor
            def _(eng, r0=r0, nr=nr, dnow=dnow, evb=evb):
                eng.wait_ge(s["dv"].h, dnow)
                for cc in range(cch):
                    if cc >= 1:
                        eng.wait_ge(s["ev"].h, evb + cc)
                    inst = None
                    for rl in range(nr):
                        inst = eng.transpose(pst[0:128, rl * 128:(rl + 1) * 128],
                                      wbf[:, rl * ncol + cc * 128:rl * ncol + cc * 128 + 128],
                                      sb["c_ident"][:, :])
                    inst.then_inc(s["mm"].h, 1)
                    s["mm"].n += 1
            mmb = s["mm"].n - cch

            @blk.vector
            def _(eng, r0=r0, nr=nr, mmb=mmb):
                eng = AD(eng)
                for cc in range(cch):
                    eng.wait_ge(s["mm"].h, mmb + cc + 1)
                    eng.tensor_copy(
                        AP(dst, cc * nrow + r0 * 128, [[dst.shape[1], 128], [1, nr * 128]]),
                        pst[:, 0:nr * 128]).then_inc(s["ev"].h, 1)
                    s["ev"].n += 1


def phase_p0(kb, blk):
    nc, s, io, sb = kb.nc, kb.s, kb.io, kb.sb
    kb.sbuf("w1q", [128, 8 * DH], FP8)
    kb.sbuf("w2q", [128, 32 * D], FP8)

    @blk.sync
    def _(eng):
        pairs = [
            (sb["c_ident"][:], io["ident"][:, :]),
            (sb["c_identf"][:], io["identf"][:, :]),
            (sb["c_row1"][:], io["row1"][:, :]),
            (sb["c_row1f"][:], io["row1f"][:, :]),
            (sb["c_ones"][:], io["onescol"][:, :]),
            (sb["c_selq"][:], io["selq"][:, :]),
            (sb["c_selqf"][:], io["selqf"][:, :]),
            (sb["c_sel2f"][:], io["sel2f"][:, :]),
            (sb["c_selh"][:], io["selh"][:, :]),
            (sb["c_convw"][:], io["convw"][:, :]),
            (sb["c_convb"][:], io["convb"][:, :]),
            (sb["c_bg"][:], io["b_gate"][:, :]),
            (sb["c_edgel"][:], io["edge_l"][:, :]),
            (sb["c_edger"][:], io["edge_r"][:, :]),
        ]
        for o_, i_ in pairs:
            eng.dma_start(out=o_, in_=i_).then_inc(s["xld"].h, 16)
            s["xld"].n += 16
    xc = s["xld"].n

    with (
        nc.sbuf_tensor("p0_grow", [1, D], BF16) as grow,
        nc.sbuf_tensor("p0_grow2", [1, D], BF16) as grow2,
        nc.psum_tensor("p0_gps", [128, 512], F32) as gps,
    ):
        @blk.gpsimd
        def _(eng):
            eng.wait_ge(s["gp"].h, s["gp"].n)
            eng.dma_start(out=grow[0:1, :], in_=io["n1s"][0:1, :]).then_inc(s["gp"].h, 16)
            s["gp"].n += 16
            eng.wait_ge(s["gp"].h, s["gp"].n)
            eng.dma_start(out=grow2[0:1, :], in_=io["n2s"][0:1, :]).then_inc(s["gp"].h, 16)
            s["gp"].n += 16
        gnow = s["gp"].n

        for gi, (rowt, dstn) in enumerate([(grow, "g1b"), (grow2, "g2b")]):
            evb = s["ev"].n

            @blk.tensor
            def _(eng, rowt=rowt, gnow=gnow, evb=evb):
                eng.wait_ge(s["gp"].h, gnow)
                eng.wait_ge(s["xld"].h, xc)
                eng.wait_ge(s["ev"].h, evb)
                for hf in range(2):
                    if hf == 1:
                        eng.wait_ge(s["ev"].h, evb + 1)
                    eng.matmul(gps[:, 0:512], sb["c_row1"][0:1, :],
                               rowt[0:1, hf * 512:hf * 512 + 512],
                               start=True, stop=True).then_inc(s["mm"].h, 1)
                    s["mm"].n += 1
            mmb = s["mm"].n - 2

            @blk.vector
            def _(eng, dstn=dstn, mmb=mmb):
                eng = AD(eng)
                for hf in range(2):
                    eng.wait_ge(s["mm"].h, mmb + hf + 1)
                    eng.tensor_copy(sb[dstn][:, hf * 512:hf * 512 + 512],
                                    gps[:, 0:512]).then_inc(s["ev"].h, 1)
                    s["ev"].n += 1

    quant_weight(kb, blk, io["w1"], DH, D, sb["w1q"], 0, tag="qw1")
    quant_weight(kb, blk, io["w2"], D, DH, sb["w2q"], 1, tag="qw2")
    if "dbg_w1q" in kb.debug:
        @blk.gpsimd
        def _(eng):
            eng.wait_ge(s["ev"].h, s["ev"].n)
            eng.dma_start(out=io["dbg_w1q"][:, :], in_=sb["w1q"][:, :]).then_inc(s["gp"].h, 16)
            eng.dma_start(out=io["dbg_wsc"][:, :], in_=sb["wsc"][:, :]).then_inc(s["gp"].h, 16)
            s["gp"].n += 32



GLEN = 128


def emit_bitffn(kb, blk, tag, ntile, groups, xin_dma, resid_dma, out_dma,
                glen_max=GLEN, edge_fix=None, pre_wait=None):
    nc, s, sb = kb.nc, kb.s, kb.sb
    GM = glen_max
    q1_cols = 128 * ntile
    import contextlib
    with contextlib.ExitStack() as _es:
        ent = _es.enter_context
        xs = ent(nc.sbuf_tensor(tag + "_xs", [128, 2 * D], F32))
        dmp = ent(nc.sbuf_tensor(tag + "_dmp", [128, D], F32))
        cl = ent(nc.sbuf_tensor(tag + "_cl", [128, 24], F32))
        xn = ent(nc.sbuf_tensor(tag + "_xn", [128, D], F32))
        q1b = ent(nc.sbuf_tensor(tag + "_q1b", [128, D], BF16))
        q1f = ent(nc.sbuf_tensor(tag + "_q1f", [128, 8 * q1_cols], I8))
        hsr = ent(nc.sbuf_tensor(tag + "_hsr", [1, q1_cols], F32))
        q1g = ent(nc.sbuf_tensor(tag + "_q1g", [128, 8 * (GM + 2)], BF16))
        hb = ent(nc.sbuf_tensor(tag + "_h", [128, 32 * (GM + 2)], BF16))
        hscb = ent(nc.sbuf_tensor(tag + "_hscb", [128, GM + 2], BF16))
        ca = ent(nc.sbuf_tensor(tag + "_ca", [128, 32 * GM], BF16))
        cb = ent(nc.sbuf_tensor(tag + "_cb", [128, 32 * GM], BF16))
        q2f = ent(nc.sbuf_tensor(tag + "_q2f", [128, 32 * GM], F32))
        mx = ent(nc.sbuf_tensor(tag + "_mx", [128, GM], F32))
        sr = ent(nc.sbuf_tensor(tag + "_sr", [1, GM], F32))
        scbuf = ent(nc.sbuf_tensor(tag + "_scb", [128, GM], BF16))
        dfm = ent(nc.sbuf_tensor(tag + "_dfm", [128, 8 * GM], BF16))
        xr = ent(nc.sbuf_tensor(tag + "_xr", [128, 2 * D], F32))
        xo = ent(nc.sbuf_tensor(tag + "_xo", [128, 2 * D], F32))
        psA = ent(nc.psum_tensor(tag + "_pA", [128, 1024], F32))
        psB = ent(nc.psum_tensor(tag + "_pB", [128, 1024], F32))
        psC = ent(nc.psum_tensor(tag + "_pC", [128, 1024], F32))
        psT = ent(nc.psum_tensor(tag + "_pT", [128, 1024], BF16))
        psM = ent(nc.psum_tensor(tag + "_pM", [128, 512], F32))
        xld0, dv0, sc0, mm0, ev0 = (s[k].n for k in ["xld", "dv", "sc", "mm", "ev"])

        xA0, xB0 = s["xldA"].n, s["xldB"].n

        @blk.sync
        def _(eng):
            for i in range(ntile):
                if i >= 2:
                    eng.wait_ge(s["dv"].h, dv0 + 2 * (i - 2) + 2)
                sem = s["xldA"] if i % 2 == 0 else s["xldB"]
                xin_dma(eng, i, xs[:, (i % 2) * D:(i % 2) * D + D], sem)
                sem.n += 16

        def evac_tile(eng, j):
            eng.tensor_copy(
                AP(q1f, j * 128, [[8 * q1_cols, 128], [q1_cols, 8], [1, 128]]),
                psT.ap().rearrange("p (c t) -> p c t", c=8))
            eng.tensor_copy(hsr[0:1, j * 128:(j + 1) * 128],
                            psM[0:1, (j % 2) * 128:(j % 2) * 128 + 128]).then_inc(s["ev"].h, 1)
            s["ev"].n += 1

        @blk.vector
        def _(eng):
            eng = AD(eng)
            for i in range(ntile):
                if i % 2 == 0:
                    eng.wait_ge(s["xldA"].h, xA0 + 16 * (i // 2 + 1))
                else:
                    eng.wait_ge(s["xldB"].h, xB0 + 16 * (i // 2 + 1))
                xt = xs[:, (i % 2) * D:(i % 2) * D + D]
                eng.scalar_tensor_tensor(dmp[:, :], xt, 1.0, xt, ALU.bypass,
                                         ALU.mult, accum_out=cl[:, 0:1])
                eng.tensor_scalar(dmp[:, :], xt, 1.0, 0.0, ALU.mult, ALU.add,
                                  accum_out=cl[:, 1:2])
                eng.tensor_scalar(cl[:, 2:3], cl[:, 1:2], 1.0 / D, None, ALU.mult)
                eng.tensor_tensor(cl[:, 3:4], cl[:, 2:3], cl[:, 2:3], ALU.mult)
                eng.tensor_scalar(cl[:, 4:5], cl[:, 0:1], 1.0 / D, None, ALU.mult)
                eng.tensor_tensor(cl[:, 4:5], cl[:, 4:5], cl[:, 3:4], ALU.subtract)
                eng.tensor_scalar(cl[:, 4:5], cl[:, 4:5], LN_EPS, None,
                                  ALU.add).then_inc(s["dv"].h, 1)
                s["dv"].n += 1
                eng.wait_ge(s["sc"].h, sc0 + i + 1)
                eng.wait_ge(s["mm"].h, mm0 + i)
                if i >= 1:
                    evac_tile(eng, i - 1)
                eng.reciprocal(cl[:, 6:7], cl[:, 5:6])
                eng.tensor_scalar(xn[:, :], xt, cl[:, 2:3], cl[:, 6:7],
                                  ALU.subtract, ALU.mult)
                eng.reduce_max(cl[:, 8:9], xn[:, :], axis=AXX,
                               apply_absolute_value=True)
                eng.tensor_scalar(cl[:, 9:10], cl[:, 8:9], 1e-5, None, ALU.max)
                eng.reciprocal(cl[:, 10:11], cl[:, 9:10])
                eng.tensor_scalar(cl[:, 10:11], cl[:, 10:11], 127.0, None, ALU.mult)
                eng.tensor_scalar(dmp[:, :], xn[:, :], cl[:, 10:11], RMAGIC,
                                  ALU.mult, ALU.add)
                eng.tensor_scalar(dmp[:, :], dmp[:, :], RMAGIC, -128.0,
                                  ALU.subtract, ALU.max)
                eng.tensor_scalar(q1b[:, :], dmp[:, :], 127.0, None, ALU.min)
                eng.scalar_tensor_tensor(cl[:, 11:12], cl[:, 9:10], 1.0 / 127.0,
                                         sb["wsc"][:, 0:1], ALU.mult,
                                         ALU.mult).then_inc(s["dv"].h, 1)
                s["dv"].n += 1
            eng.wait_ge(s["mm"].h, mm0 + ntile)
            evac_tile(eng, ntile - 1)

        @blk.scalar
        def _(eng):
            eng = AD(eng)
            for i in range(ntile):
                eng.wait_ge(s["dv"].h, dv0 + 2 * i + 1)
                eng.activation(cl[:, 5:6], cl[:, 4:5], AF.Sqrt).then_inc(s["sc"].h, 1)
                s["sc"].n += 1

        @blk.tensor
        def _(eng):
            for i in range(ntile):
                eng.wait_ge(s["dv"].h, dv0 + 2 * i + 2)
                for ch in range(8):
                    eng.transpose(psT[:, ch * 128:(ch + 1) * 128],
                                  q1b[:, ch * 128:(ch + 1) * 128], sb["c_ident"][:, :])
                eng.transpose(psM[0:1, (i % 2) * 128:(i % 2) * 128 + 128],
                              cl[:, 11:12], sb["c_identf"][:, :]).then_inc(s["mm"].h, 1)
                s["mm"].n += 1
        passA_ev = s["ev"].n

        prev_xo_dv = 0
        for (g0, glen) in groups:
            wlen = glen + 2
            nsl = (glen + 127) // 128
            sp_g0 = s["sp"].n

            dbg_wait = getattr(kb, "_dbg_wait_sp", 0)

            @blk.vector
            def _(eng, g0=g0, wlen=wlen, dbg_wait=dbg_wait):
                eng = AD(eng)
                eng.wait_ge(s["ev"].h, passA_ev)
                if pre_wait:
                    for semk, cnt in pre_wait:
                        eng.wait_ge(s[semk].h, cnt)
                if dbg_wait:
                    eng.wait_ge(s["gp"].h, dbg_wait)
                eng.tensor_copy(
                    AP(q1g, 0, [[8 * (GM + 2), 128], [GM + 2, 8], [1, wlen]]),
                    AP(q1f, g0 - 1, [[8 * q1_cols, 128], [q1_cols, 8], [1, wlen]]),
                ).then_inc(s["dv"].h, 1)
                s["dv"].n += 1
            dv_cast = s["dv"].n
            evb = s["ev"].n

            @blk.tensor
            def _(eng, g0=g0, wlen=wlen, dv_cast=dv_cast, evb=evb):
                eng.wait_ge(s["dv"].h, dv_cast)
                for oc in range(32):
                    slot = oc % 4
                    pst_ = (psA, psB)[slot // 2]
                    pcol = (slot % 2) * 512
                    if oc >= 4:
                        eng.wait_ge(s["ev"].h, evb + (oc - 4) + 1)
                    inst = None
                    for kc in range(8):
                        inst = eng.matmul(pst_[:, pcol:pcol + wlen],
                                   sb["w1q"][:, kc * DH + oc * 128:kc * DH + oc * 128 + 128],
                                   q1g[:, kc * (GM + 2):kc * (GM + 2) + wlen],
                                   start=(kc == 0), stop=(kc == 7))
                    inst.then_inc(s["mm"].h, 1)
                    s["mm"].n += 1
                eng.matmul(psM[:, 0:wlen], sb["c_row1f"][0:1, :],
                           hsr[0:1, g0 - 1:g0 - 1 + wlen], start=True,
                           stop=True).then_inc(s["mm"].h, 1)
                s["mm"].n += 1
            mmb = s["mm"].n - 33

            @blk.scalar
            def _(eng, wlen=wlen, mmb=mmb):
                eng = AD(eng)
                for oc in range(32):
                    slot = oc % 4
                    pst_ = (psA, psB)[slot // 2]
                    pcol = (slot % 2) * 512
                    eng.wait_ge(s["mm"].h, mmb + oc + 1)
                    eng.copy(hb[:, oc * wlen:(oc + 1) * wlen],
                             pst_[:, pcol:pcol + wlen]).then_inc(s["ev"].h, 1)
                    s["ev"].n += 1
                eng.wait_ge(s["mm"].h, mmb + 33)
                eng.copy(hscb[:, 0:wlen], psM[:, 0:wlen]).then_inc(s["sc"].h, 1)
                s["sc"].n += 1
            ev_h = s["ev"].n
            sc_hsc = s["sc"].n

            @blk.vector
            def _(eng, g0=g0, glen=glen, wlen=wlen, ev_h=ev_h, sc_hsc=sc_hsc):
                eng = AD(eng)
                eng.wait_ge(s["ev"].h, ev_h)
                eng.wait_ge(s["sc"].h, sc_hsc)
                hfull = AP(hb, 0, [[32 * (GM + 2), 128], [wlen, 32], [1, wlen]])
                hscv = AP(hscb, 0, [[GM + 2, 128], [0, 32], [1, wlen]])
                eng.tensor_tensor(hfull, hfull, hscv, ALU.mult)
                if edge_fix:
                    for eg0, ecol, eap in edge_fix:
                        if eg0 == g0:
                            hcol = AP(hb, ecol, [[32 * (GM + 2), 128], [wlen, 32], [0, 1]])
                            eng.tensor_scalar(hcol, hcol, eap, None, ALU.mult)
                hv = lambda off: AP(hb, off, [[32 * (GM + 2), 128], [wlen, 32], [1, glen]])
                av = AP(ca, 0, [[32 * GM, 128], [glen, 32], [1, glen]])
                bv = AP(cb, 0, [[32 * GM, 128], [glen, 32], [1, glen]])
                wt = lambda t: AP(sb["c_convw"], t, [[96, 128], [3, 32], [0, glen]])
                bs = AP(sb["c_convb"], 0, [[32, 128], [1, 32], [0, glen]])
                eng.tensor_tensor(av, hv(1), wt(1), ALU.mult)
                eng.tensor_tensor(bv, hv(0), wt(0), ALU.mult)
                eng.tensor_tensor(av, av, bv, ALU.add)
                eng.tensor_tensor(bv, hv(2), wt(2), ALU.mult)
                eng.tensor_tensor(av, av, bv, ALU.add)
                eng.tensor_tensor(av, av, bs, ALU.add).then_inc(s["dv"].h, 1)
                s["dv"].n += 1
            dv_conv = s["dv"].n

            @blk.scalar
            def _(eng, glen=glen, dv_conv=dv_conv):
                eng = AD(eng)
                eng.wait_ge(s["dv"].h, dv_conv)
                eng.activation(cb[:, 0:32 * glen], ca[:, 0:32 * glen],
                               AF.Silu).then_inc(s["sc"].h, 1)
                s["sc"].n += 1
            sc_silu = s["sc"].n

            @blk.vector
            def _(eng, glen=glen, sc_silu=sc_silu):
                eng = AD(eng)
                eng.wait_ge(s["sc"].h, sc_silu)
                bv3 = AP(cb, 0, [[32 * GM, 128], [1, glen], [glen, 32]])
                eng.tensor_reduce(mx[:, 0:glen], bv3, axis=AXX, op=ALU.max,
                                  apply_absolute_value=True).then_inc(s["dv"].h, 1)
                s["dv"].n += 1
            dv_max = s["dv"].n

            @blk.tensor
            def _(eng, glen=glen, nsl=nsl, dv_max=dv_max):
                eng.wait_ge(s["dv"].h, dv_max)
                inst = None
                for sl in range(nsl):
                    w = min(128, glen - sl * 128)
                    inst = eng.transpose(psM[0:w, sl * 128:sl * 128 + 128],
                                  mx[:, sl * 128:sl * 128 + w], sb["c_identf"][:, :])
                inst.then_inc(s["mm"].h, 1)
                s["mm"].n += 1
            mm_t1 = s["mm"].n

            @blk.vector
            def _(eng, glen=glen, nsl=nsl, mm_t1=mm_t1):
                eng = AD(eng)
                eng.wait_ge(s["mm"].h, mm_t1)
                for sl in range(nsl):
                    w = min(128, glen - sl * 128)
                    eng.reduce_max(cl[0:w, 12 + 2 * sl:13 + 2 * sl],
                                   psM[0:w, sl * 128:sl * 128 + 128], axis=AXX)
                    eng.tensor_scalar(cl[0:w, 12 + 2 * sl:13 + 2 * sl],
                                      cl[0:w, 12 + 2 * sl:13 + 2 * sl], 1e-5, None,
                                      ALU.max)
                    eng.scalar_tensor_tensor(cl[0:w, 13 + 2 * sl:14 + 2 * sl],
                                             cl[0:w, 12 + 2 * sl:13 + 2 * sl],
                                             0.5 / 127.0, sb["wsc"][0:w, 1:2],
                                             ALU.mult, ALU.mult)
                    eng.reciprocal(cl[0:w, 16 + sl:17 + sl],
                                   cl[0:w, 12 + 2 * sl:13 + 2 * sl])
                    eng.tensor_scalar(cl[0:w, 16 + sl:17 + sl],
                                      cl[0:w, 16 + sl:17 + sl], 127.0, None,
                                      ALU.mult)
                eng.drain().then_inc(s["dv"].h, 1)
                s["dv"].n += 1
            dv_cols = s["dv"].n

            @blk.tensor
            def _(eng, glen=glen, nsl=nsl, dv_cols=dv_cols):
                eng.wait_ge(s["dv"].h, dv_cols)
                inst = None
                for sl in range(nsl):
                    w = min(128, glen - sl * 128)
                    inst = eng.transpose(psM[0:1, 256 + sl * 128:256 + sl * 128 + w],
                                  cl[0:w, 16 + sl:17 + sl], sb["c_identf"][0:w, 0:w])
                inst.then_inc(s["mm"].h, 1)
                s["mm"].n += 1
            mm_t2 = s["mm"].n

            @blk.vector
            def _(eng, glen=glen, mm_t2=mm_t2):
                eng = AD(eng)
                eng.wait_ge(s["mm"].h, mm_t2)
                eng.tensor_copy(sr[0:1, 0:glen],
                                psM[0:1, 256:256 + glen]).then_inc(s["dv"].h, 1)
                s["dv"].n += 1
            dv_row = s["dv"].n

            @blk.tensor
            def _(eng, glen=glen, nsl=nsl, dv_row=dv_row):
                eng.wait_ge(s["dv"].h, dv_row)
                inst = None
                for sl in range(nsl):
                    w = min(128, glen - sl * 128)
                    inst = eng.matmul(psM[:, sl * 128:sl * 128 + w], sb["c_row1f"][0:1, :],
                               sr[0:1, sl * 128:sl * 128 + w], start=True, stop=True)
                inst.then_inc(s["mm"].h, 1)
                s["mm"].n += 1
            mm_bc = s["mm"].n

            @blk.scalar
            def _(eng, glen=glen, mm_bc=mm_bc):
                eng = AD(eng)
                eng.wait_ge(s["mm"].h, mm_bc)
                eng.copy(scbuf[:, 0:glen], psM[:, 0:glen]).then_inc(s["sc"].h, 1)
                s["sc"].n += 1
            sc_bc = s["sc"].n

            @blk.vector
            def _(eng, glen=glen, sc_bc=sc_bc):
                eng = AD(eng)
                eng.wait_ge(s["sc"].h, sc_bc)
                scv = AP(scbuf, 0, [[GM, 128], [0, 32], [1, glen]])
                bv = AP(cb, 0, [[32 * GM, 128], [glen, 32], [1, glen]])
                qv = AP(q2f, 0, [[32 * GM, 128], [glen, 32], [1, glen]])
                av = AP(ca, 0, [[32 * GM, 128], [glen, 32], [1, glen]])
                eng.tensor_tensor(qv, bv, scv, ALU.mult)
                eng.tensor_scalar(qv, qv, RMAGIC, RMAGIC, ALU.add, ALU.subtract)
                eng.tensor_scalar(av, qv, -128.0, 127.0, ALU.max,
                                  ALU.min).then_inc(s["dv"].h, 1)
                s["dv"].n += 1
            dv_q2 = s["dv"].n
            ev_b2 = s["ev"].n
            if "dbg_h" in kb.debug and g0 == groups[min(13, len(groups) - 1)][0]:
                @blk.gpsimd
                def _(eng, wlen=wlen, glen=glen, dv_q2=dv_q2):
                    eng.wait_ge(s["dv"].h, dv_q2)
                    eng.dma_start(out=kb.io["dbg_h"][:, 0:32 * wlen],
                                  in_=hb[:, 0:32 * wlen]).then_inc(s["gp"].h, 16)
                    eng.dma_start(out=kb.io["dbg_ys"][:, 0:32 * glen],
                                  in_=cb[:, 0:32 * glen]).then_inc(s["gp"].h, 16)
                    eng.dma_start(out=kb.io["dbg_q2"][:, 0:32 * glen],
                                  in_=ca[:, 0:32 * glen]).then_inc(s["gp"].h, 16)
                    eng.dma_start(out=kb.io["dbg_q1"][:, 0:8 * wlen],
                                  in_=q1g[:, 0:8 * wlen]).then_inc(s["gp"].h, 16)
                    s["gp"].n += 64
                kb._dbg_wait_sp = s["gp"].n

            @blk.tensor
            def _(eng, glen=glen, dv_q2=dv_q2, ev_b2=ev_b2):
                eng.wait_ge(s["dv"].h, dv_q2)
                for oc in range(8):
                    slot = oc % 2
                    if oc >= 2:
                        eng.wait_ge(s["ev"].h, ev_b2 + (oc - 2) + 1)
                    inst = None
                    for kc in range(32):
                        inst = eng.matmul(psC[:, slot * 512:slot * 512 + glen],
                                   sb["w2q"][:, kc * D + oc * 128:kc * D + oc * 128 + 128],
                                   ca[:, kc * glen:(kc + 1) * glen],
                                   start=(kc == 0), stop=(kc == 31))
                    inst.then_inc(s["mm"].h, 1)
                    s["mm"].n += 1
            mm_b2 = s["mm"].n - 8

            @blk.vector
            def _(eng, glen=glen, mm_b2=mm_b2):
                eng = AD(eng)
                for oc in range(8):
                    slot = oc % 2
                    eng.wait_ge(s["mm"].h, mm_b2 + oc + 1)
                    eng.tensor_copy(dfm[:, oc * glen:(oc + 1) * glen],
                                    psC[:, slot * 512:slot * 512 + glen]).then_inc(s["ev"].h, 1)
                    s["ev"].n += 1
            ev_dfm = s["ev"].n

            @blk.sync
            def _(eng, g0=g0, glen=glen, nsl=nsl, prev_xo_dv=prev_xo_dv):
                eng.wait_ge(s["dv"].h, prev_xo_dv)
                for sl in range(nsl):
                    w = min(128, glen - sl * 128)
                    resid_dma(eng, g0 + sl * 128, w, xr[:, sl * D:sl * D + D])
                    s["xld"].n += 16
            xld_r = s["xld"].n

            @blk.tensor
            def _(eng, glen=glen, nsl=nsl, ev_dfm=ev_dfm, dv_q2=dv_q2):
                eng.wait_ge(s["ev"].h, ev_dfm)
                for sl in range(nsl):
                    w = min(128, glen - sl * 128)
                    if sl >= 1:
                        eng.wait_ge(s["dv"].h, dv_q2 + sl)
                    inst = None
                    for ch in range(8):
                        inst = eng.transpose(psT[0:w, ch * 128:(ch + 1) * 128],
                                      dfm[:, ch * glen + sl * 128:ch * glen + sl * 128 + w],
                                      sb["c_ident"][:, :])
                    inst.then_inc(s["mm"].h, 1)
                    s["mm"].n += 1
            mm_dT = s["mm"].n - nsl

            @blk.vector
            def _(eng, glen=glen, nsl=nsl, mm_dT=mm_dT, xld_r=xld_r, sp_g0=sp_g0):
                eng = AD(eng)
                for sl in range(nsl):
                    w = min(128, glen - sl * 128)
                    eng.wait_ge(s["mm"].h, mm_dT + sl + 1)
                    eng.wait_ge(s["xld"].h, xld_r - 16 * (nsl - 1 - sl))
                    eng.wait_ge(s["sp"].h, sp_g0)
                    eng.scalar_tensor_tensor(xo[0:w, sl * D:sl * D + D],
                                             psT[0:w, 0:1024],
                                             cl[0:w, 13 + 2 * sl:14 + 2 * sl],
                                             xr[0:w, sl * D:sl * D + D],
                                             ALU.mult, ALU.add).then_inc(s["dv"].h, 1)
                    s["dv"].n += 1
            dv_out = s["dv"].n - nsl
            prev_xo_dv = s["dv"].n

            @blk.sync
            def _(eng, g0=g0, glen=glen, nsl=nsl, dv_out=dv_out):
                for sl in range(nsl):
                    w = min(128, glen - sl * 128)
                    eng.wait_ge(s["dv"].h, dv_out + sl + 1)
                    out_dma(eng, g0 + sl * 128, w, xo[:, sl * D:sl * D + D])
                    s["sp"].n += 16

        spf = s["sp"].n

        @blk.sync
        def _(eng):
            eng.wait_ge(s["sp"].h, spf)



def phase_s1(kb, blk):
    s, io = kb.s, kb.io
    groups = []
    t = 0
    while t < NKV:
        groups.append((t + XKV, min(GLEN, NKV - t)))
        t += GLEN

    def xin(eng, i, dst, sem):
        eng.dma_start(out=dst, in_=io["xw"][i * 128:(i + 1) * 128, :]).then_inc(sem.h, 16)

    def resid(eng, tok0, w, dst):
        eng.dma_start(out=dst[0:w, :], in_=io["xw"][tok0:tok0 + w, :]).then_inc(s["xld"].h, 16)

    def outd(eng, tok0, w, src):
        eng.dma_start(out=io["x1_scr"][tok0 - XKV:tok0 - XKV + w, :],
                      in_=src[0:w, :]).then_inc(s["sp"].h, 16)

    emit_bitffn(kb, blk, "s1", 29, groups, xin, resid, outd)

    @blk.sync
    def _(eng):
        eng.dma_start(out=io["w1q_scr"][:, :], in_=kb.sb["w1q"][:, :]).then_inc(s["wld"].h, 16)
        eng.dma_start(out=io["w2q_scr"][:, :], in_=kb.sb["w2q"][:, :]).then_inc(s["wld"].h, 16)
        s["wld"].n += 32




def phase_s2ab(kb, blk):
    nc, s, sb, io = kb.nc, kb.s, kb.sb, kb.io
    import contextlib
    with contextlib.ExitStack() as _es:
        ent = _es.enter_context
        wqkv = ent(nc.sbuf_tensor("ab_wqkv", [128, 8 * 3072], BF16))
        wg = ent(nc.sbuf_tensor("ab_wg", [128, 8 * 1024], BF16))
        xs = ent(nc.sbuf_tensor("ab_xs", [128, 2 * D], F32))
        cl = ent(nc.sbuf_tensor("ab_cl", [128, 16], F32))
        dmp = ent(nc.sbuf_tensor("ab_dmp", [128, D], F32))
        xnt = ent(nc.sbuf_tensor("ab_xnt", [128, D], BF16))
        z0t = ent(nc.sbuf_tensor("ab_z0t", [128, 2 * D], F32))
        xnfm = ent(nc.sbuf_tensor("ab_xnfm", [128, 8 * 512], BF16))
        kst = ent(nc.sbuf_tensor("ab_kst", [128, 2 * 512], BF16))
        qst = ent(nc.sbuf_tensor("ab_qst", [128, 2 * 512], BF16))
        gst = ent(nc.sbuf_tensor("ab_gst", [128, 2 * 512], BF16))
        vst = ent(nc.sbuf_tensor("ab_vst", [128, 1024], BF16))
        psT = ent(nc.psum_tensor("ab_pT", [128, 1024], BF16))
        psP = ent(nc.psum_tensor("ab_pP", [128, 1024], F32))
        psG = ent(nc.psum_tensor("ab_pG", [128, 512], F32))
        psV = ent(nc.psum_tensor("ab_pV", [128, 1024], F32))

        gp0 = s["gp"].n

        @blk.gpsimd
        def _(eng):
            eng.wait_ge(s["gp"].h, s["gp"].n)
            eng.dma_start(
                out=wqkv.ap().rearrange("p (c d) -> p c d", c=8),
                in_=io["w_qkv"].rearrange("(c p) d -> p c d", p=128)).then_inc(s["gp"].h, 16)
            s["gp"].n += 16
            eng.wait_ge(s["gp"].h, s["gp"].n)
            eng.dma_start(
                out=wg.ap().rearrange("p (c d) -> p c d", c=8),
                in_=io["w_gate"].rearrange("(c p) d -> p c d", p=128)).then_inc(s["gp"].h, 16)
            s["gp"].n += 16

        groups = []
        t = 0
        while t < NKV:
            groups.append((t, min(512, NKV - t)))
            t += 512

        for a0, alen in groups:
            ntl = (alen + 127) // 128
            do_q = a0 + alen > KVQ
            nKQ = 16 if do_q else 8
            xA0, xB0 = s["xldA"].n, s["xldB"].n
            dv0, sc0, mm0, ev0 = (s[k].n for k in ["dv", "sc", "mm", "ev"])

            zplan = [None] * ntl
            for i in range(ntl):
                r0 = a0 + i * 128
                if do_q and r0 + 128 > KVQ:
                    gidx = getattr(kb, "_spill_gidx", 0)
                    p = "spA" if gidx % 2 == 0 else "spB"
                    kb._spill_gidx = gidx + 1
                    zplan[i] = (p, s[p].n + 16)
                    s[p].n += 16

            @blk.sync
            def _(eng, a0=a0, ntl=ntl, dv0=dv0):
                for i in range(ntl):
                    if i >= 2:
                        eng.wait_ge(s["dv"].h, dv0 + 3 * (i - 2) + 3)
                    sem = s["xldA"] if i % 2 == 0 else s["xldB"]
                    r0 = a0 + i * 128
                    w = min(128, NKV - r0)
                    eng.dma_start(out=xs[0:w, (i % 2) * D:(i % 2) * D + D],
                                  in_=io["x1_scr"][r0:r0 + w, :]).then_inc(sem.h, 16)
                    sem.n += 16

            def evac_fm(eng, j):
                eng.tensor_copy(
                    AP(xnfm, j * 128, [[8 * 512, 128], [512, 8], [1, 128]]),
                    psT.ap().rearrange("p (c t) -> p c t", c=8)).then_inc(s["ev"].h, 1)
                s["ev"].n += 1

            @blk.vector
            def _(eng, a0=a0, ntl=ntl, xA0=xA0, xB0=xB0, sc0=sc0, mm0=mm0,
                  zplan=zplan):
                eng = AD(eng)
                for i in range(ntl):
                    if i % 2 == 0:
                        eng.wait_ge(s["xldA"].h, xA0 + 16 * (i // 2 + 1))
                    else:
                        eng.wait_ge(s["xldB"].h, xB0 + 16 * (i // 2 + 1))
                    xt = xs[:, (i % 2) * D:(i % 2) * D + D]
                    eng.scalar_tensor_tensor(dmp[:, :], xt, 1.0, xt, ALU.bypass,
                                             ALU.mult, accum_out=cl[:, 0:1])
                    eng.tensor_scalar(dmp[:, :], xt, 1.0, 0.0, ALU.mult, ALU.add,
                                      accum_out=cl[:, 5:6])
                    eng.tensor_scalar(cl[:, 6:7], cl[:, 5:6], 1.0 / D, None, ALU.mult)
                    eng.tensor_tensor(cl[:, 7:8], cl[:, 6:7], cl[:, 6:7], ALU.mult)
                    eng.tensor_scalar(cl[:, 1:2], cl[:, 0:1], 1.0 / D, None, ALU.mult)
                    eng.tensor_tensor(cl[:, 1:2], cl[:, 1:2], cl[:, 7:8], ALU.subtract)
                    eng.tensor_scalar(cl[:, 1:2], cl[:, 1:2], LN_EPS, None,
                                      ALU.add).then_inc(s["dv"].h, 1)
                    s["dv"].n += 1
                    eng.wait_ge(s["sc"].h, sc0 + 2 * i + 1)
                    eng.wait_ge(s["mm"].h, mm0 + i)
                    if i >= 1:
                        evac_fm(eng, i - 1)
                    eng.reciprocal(cl[:, 3:4], cl[:, 2:3])
                    zt = z0t[:, (i % 2) * D:(i % 2) * D + D]
                    if i >= 2 and zplan[i - 2] is not None:
                        p_, tgt_ = zplan[i - 2]
                        eng.wait_ge(s[p_].h, tgt_)
                    eng.tensor_scalar(zt, xt, cl[:, 6:7], cl[:, 3:4],
                                      ALU.subtract, ALU.mult)
                    eng.scalar_tensor_tensor(dmp[:, :], zt, 1.0, zt,
                                             ALU.bypass, ALU.mult,
                                             accum_out=cl[:, 8:9])
                    eng.tensor_scalar(cl[:, 9:10], cl[:, 8:9], 1.0 / D, RMS_EPS,
                                      ALU.mult, ALU.add).then_inc(s["dv"].h, 1)
                    s["dv"].n += 1
                    eng.wait_ge(s["sc"].h, sc0 + 2 * i + 2)
                    eng.reciprocal(cl[:, 11:12], cl[:, 10:11])
                    eng.tensor_scalar(dmp[:, :], zt, cl[:, 11:12], None,
                                      ALU.mult)
                    eng.tensor_tensor(xnt[:, :], dmp[:, :], sb["g1b"][:, :],
                                      ALU.mult).then_inc(s["dv"].h, 1)
                    s["dv"].n += 1
                eng.wait_ge(s["mm"].h, mm0 + ntl)
                evac_fm(eng, ntl - 1)

            @blk.scalar
            def _(eng, ntl=ntl, dv0=dv0):
                eng = AD(eng)
                for i in range(ntl):
                    eng.wait_ge(s["dv"].h, dv0 + 3 * i + 1)
                    eng.activation(cl[:, 2:3], cl[:, 1:2], AF.Sqrt).then_inc(s["sc"].h, 1)
                    s["sc"].n += 1
                    eng.wait_ge(s["dv"].h, dv0 + 3 * i + 2)
                    eng.activation(cl[:, 10:11], cl[:, 9:10], AF.Sqrt).then_inc(s["sc"].h, 1)
                    s["sc"].n += 1

            @blk.tensor
            def _(eng, ntl=ntl, dv0=dv0, ev0=ev0):
                for i in range(ntl):
                    eng.wait_ge(s["dv"].h, dv0 + 3 * i + 3)
                    if i >= 1:
                        eng.wait_ge(s["ev"].h, ev0 + i)
                    inst = None
                    for ch in range(8):
                        inst = eng.transpose(psT[:, ch * 128:(ch + 1) * 128],
                                             xnt[:, ch * 128:(ch + 1) * 128],
                                             sb["c_ident"][:, :])
                    inst.then_inc(s["mm"].h, 1)
                    s["mm"].n += 1

            @blk.sync
            def _(eng, a0=a0, ntl=ntl, dv0=dv0, zplan=zplan):
                for i in range(ntl):
                    if zplan[i] is None:
                        continue
                    r0 = a0 + i * 128
                    lo = max(r0, KVQ)
                    w = min(r0 + 128, NKV) - lo
                    p_, tgt_ = zplan[i]
                    eng.wait_ge(s["dv"].h, dv0 + 3 * i + 3)
                    eng.wait_ge(s[p_].h, tgt_ - 16)
                    eng.dma_start(
                        out=io["z0_scr"][lo - KVQ:lo - KVQ + w, :],
                        in_=z0t[lo - r0:lo - r0 + w, (i % 2) * D:(i % 2) * D + D]
                    ).then_inc(s[p_].h, 16)

            evF = s["ev"].n
            scG0 = s["sc"].n
            mmP = s["mm"].n
            nG = 8 if do_q else 0
            nsp_g = nKQ + nG
            nspills = nsp_g + ntl
            plan = []
            pc = {"spA": s["spA"].n, "spB": s["spB"].n}
            for j in range(nspills):
                p = "spA" if (s["spA"].n + s["spB"].n) // 16 % 2 == 0 else "spB"
                pass
            gidx0 = getattr(kb, "_spill_gidx", 0)
            for j in range(nspills):
                p = "spA" if (gidx0 + j) % 2 == 0 else "spB"
                pc[p] += 16
                plan.append((p, pc[p]))
            kb._spill_gidx = gidx0 + nspills

            @blk.tensor
            def _(eng, a0=a0, alen=alen, ntl=ntl, evF=evF, do_q=do_q,
                  scG0=scG0, nKQ=nKQ):
                eng.wait_ge(s["ev"].h, evF)
                eng.wait_ge(s["gp"].h, gp0 + 32)
                nmm = 0
                for pj in range(2 if do_q else 1):
                    col0 = 1024 if pj == 0 else 0
                    for oc in range(8):
                        slot = nmm % 2
                        if nmm >= 2:
                            eng.wait_ge(s["ev"].h, evF + nmm - 1)
                        inst = None
                        for kc in range(8):
                            inst = eng.matmul(
                                psP[:, slot * 512:slot * 512 + alen],
                                wqkv[:, kc * 3072 + col0 + oc * 128:kc * 3072 + col0 + oc * 128 + 128],
                                xnfm[:, kc * 512:kc * 512 + alen],
                                start=(kc == 0), stop=(kc == 7))
                        inst.then_inc(s["mm"].h, 1)
                        s["mm"].n += 1
                        nmm += 1
                if do_q:
                    for oc in range(8):
                        if oc >= 1:
                            eng.wait_ge(s["sc"].h, scG0 + oc)
                        inst = None
                        for kc in range(8):
                            inst = eng.matmul(
                                psG[:, 0:alen],
                                wg[:, kc * 1024 + oc * 128:kc * 1024 + oc * 128 + 128],
                                xnfm[:, kc * 512:kc * 512 + alen],
                                start=(kc == 0), stop=(kc == 7))
                        inst.then_inc(s["mm"].h, 1)
                        s["mm"].n += 1
                for ts_ in range(ntl):
                    tw = min(128, alen - ts_ * 128)
                    for half in range(2):
                        vi = 2 * ts_ + half
                        if ts_ >= 1:
                            eng.wait_ge(s["ev"].h, evF + nKQ + 2 * (ts_ - 1) + half + 1)
                        inst = None
                        for kc in range(8):
                            inst = eng.matmul(
                                psV[0:tw, half * 512:half * 512 + 512],
                                xnfm[:, kc * 512 + ts_ * 128:kc * 512 + ts_ * 128 + tw],
                                wqkv[:, kc * 3072 + 2048 + half * 512:kc * 3072 + 2048 + half * 512 + 512],
                                start=(kc == 0), stop=(kc == 7))
                        inst.then_inc(s["mm"].h, 1)
                        s["mm"].n += 1


            @blk.vector
            def _(eng, a0=a0, alen=alen, ntl=ntl, mmP=mmP, do_q=do_q,
                  nKQ=nKQ, nG=nG, nsp_g=nsp_g, plan=plan):
                eng = AD(eng)
                nmm = 0
                for pj in range(2 if do_q else 1):
                    dst = kst if pj == 0 else qst
                    for oc in range(8):
                        slot = nmm % 2
                        eng.wait_ge(s["mm"].h, mmP + nmm + 1)
                        if nmm >= 2:
                            p, tgt = plan[nmm - 2]
                            eng.wait_ge(s[p].h, tgt)
                        if pj == 0:
                            eng.tensor_copy(dst[:, slot * 512:slot * 512 + alen],
                                            psP[:, slot * 512:slot * 512 + alen]).then_inc(s["ev"].h, 1)
                        else:
                            eng.tensor_scalar(dst[:, slot * 512:slot * 512 + alen],
                                              psP[:, slot * 512:slot * 512 + alen],
                                              0.125, None, ALU.mult).then_inc(s["ev"].h, 1)
                        s["ev"].n += 1
                        nmm += 1
                for ts_ in range(ntl):
                    tw = min(128, alen - ts_ * 128)
                    for half in range(2):
                        vi = 2 * ts_ + half
                        eng.wait_ge(s["mm"].h, mmP + nKQ + nG + vi + 1)
                        if ts_ >= 1:
                            p, tgt = plan[nsp_g + ts_ - 1]
                            eng.wait_ge(s[p].h, tgt)
                        eng.tensor_copy(vst[0:tw, half * 512:half * 512 + 512],
                                        psV[0:tw, half * 512:half * 512 + 512]).then_inc(s["ev"].h, 1)
                        s["ev"].n += 1

            @blk.scalar
            def _(eng, alen=alen, mmP=mmP, do_q=do_q, nKQ=nKQ, plan=plan):
                eng = AD(eng)
                if do_q:
                    for oc in range(8):
                        eng.wait_ge(s["mm"].h, mmP + nKQ + oc + 1)
                        if oc >= 2:
                            p, tgt = plan[nKQ + oc - 2]
                            eng.wait_ge(s[p].h, tgt)
                        eng.activation(gst[:, (oc % 2) * 512:(oc % 2) * 512 + alen],
                                       psG[:, 0:alen], AF.Sigmoid,
                                       bias=sb["c_bg"][:, oc:oc + 1]).then_inc(s["sc"].h, 1)
                        s["sc"].n += 1

            @blk.sync
            def _(eng, a0=a0, alen=alen, ntl=ntl, do_q=do_q, evF=evF,
                  scG0=scG0, nKQ=nKQ, nG=nG, nsp_g=nsp_g, plan=plan):
                ev = evF
                j = 0
                for pj in range(2 if do_q else 1):
                    for oc in range(8):
                        slot = (pj * 8 + oc) % 2
                        ev += 1
                        eng.wait_ge(s["ev"].h, ev)
                        p, tgt = plan[j]
                        eng.wait_ge(s[p].h, tgt - 16)
                        if pj == 0:
                            eng.dma_start(
                                out=io["k_scr"][:, oc * NKV + a0:oc * NKV + a0 + alen],
                                in_=kst[:, slot * 512:slot * 512 + alen]).then_inc(s[p].h, 16)
                        else:
                            qa = a0 - KVQ
                            eng.dma_start(
                                out=io["q_scr"][:, oc * NQ + qa:oc * NQ + qa + alen],
                                in_=qst[:, slot * 512:slot * 512 + alen]).then_inc(s[p].h, 16)
                        s[p].n += 16
                        j += 1
                if do_q:
                    for oc in range(8):
                        eng.wait_ge(s["sc"].h, scG0 + oc + 1)
                        qa = a0 - KVQ
                        p, tgt = plan[j]
                        eng.wait_ge(s[p].h, tgt - 16)
                        eng.dma_start(
                            out=io["g_scr"][:, oc * NQ + qa:oc * NQ + qa + alen],
                            in_=gst[:, (oc % 2) * 512:(oc % 2) * 512 + alen]).then_inc(s[p].h, 16)
                        s[p].n += 16
                        j += 1
                for ts_ in range(ntl):
                    tw = min(128, alen - ts_ * 128)
                    ev += 2
                    eng.wait_ge(s["ev"].h, ev)
                    r0 = a0 + ts_ * 128
                    p, tgt = plan[j]
                    eng.wait_ge(s[p].h, tgt - 16)
                    eng.dma_start(out=io["v_scr"][r0:r0 + tw, :],
                                  in_=vst[0:tw, 0:1024]).then_inc(s[p].h, 16)
                    s[p].n += 16
                    j += 1

        spfA, spfB = s["spA"].n, s["spB"].n

        @blk.sync
        def _(eng):
            eng.wait_ge(s["spA"].h, spfA)
            eng.wait_ge(s["spB"].h, spfB)



def phase_s2c(kb, blk):
    nc, s, sb, io = kb.nc, kb.s, kb.sb, kb.io
    import contextlib

    with contextlib.ExitStack() as _es:
        ent = _es.enter_context
        kh = ent(nc.sbuf_tensor("at_kh", [128, 8 * NKH], BF16))
        qh = ent(nc.sbuf_tensor("at_qh", [128, 8 * NQH], BF16))
        vn = ent(nc.sbuf_tensor("at_vn", [128, 9 * 1024], BF16))
        gh = ent(nc.sbuf_tensor("at_gh", [128, 8 * NQH], BF16))
        von = ent(nc.sbuf_tensor("at_von", [128, 8 * NQH], BF16))
        vof = ent(nc.sbuf_tensor("at_vof", [128, 8 * NQH], BF16))
        vr1 = ent(nc.sbuf_tensor("at_vr1", [128, 2 * 1024], BF16))
        vr2 = ent(nc.sbuf_tensor("at_vr2", [128, 2 * 1024], BF16))
        mskn = ent(nc.sbuf_tensor("at_mskn", [128, 16 * W_NEAR], BF16))
        mskf = ent(nc.sbuf_tensor("at_mskf", [128, 16 * W_FARM], BF16))
        mskn0 = ent(nc.sbuf_tensor("at_mskn0", [128, 16 * W_NEAR], BF16))
        mskf0 = ent(nc.sbuf_tensor("at_mskf0", [128, 16 * W_FARM], BF16))
        pbuf = ent(nc.sbuf_tensor("at_p", [128, 2 * 256], BF16))
        pts = ent(nc.sbuf_tensor("at_pts", [128, 2 * 256], BF16))
        sums_n = ent(nc.sbuf_tensor("at_sn", [128, 9 * 16], F32))
        sums_f = ent(nc.sbuf_tensor("at_sf", [128, 16 * 16], F32))
        sqst = ent(nc.sbuf_tensor("at_sq", [128, NKH], F32))
        qkm = ent(nc.sbuf_tensor("at_qkm", [128, 8], F32))
        negmb = ent(nc.sbuf_tensor("at_nm", [128, 16], F32))
        rows = ent(nc.sbuf_tensor("at_rows", [1, 32], F32))
        denr = ent(nc.sbuf_tensor("at_den", [16, NQH], F32))
        denf = ent(nc.sbuf_tensor("at_denf", [16, NQH], F32))
        pbsb = ent(nc.sbuf_tensor("at_pb", [45, 16], F32))
        pbhi = ent(nc.sbuf_tensor("at_pbhi", [45, 16], BF16))
        pblo = ent(nc.sbuf_tensor("at_pblo", [45, 16], BF16))
        pblof = ent(nc.sbuf_tensor("at_pblof", [45, 16], F32))
        indst = ent(nc.sbuf_tensor("at_ind", [45, 5120], BF16))
        mrst = ent(nc.sbuf_tensor("at_mr", [16, 512], F32))
        cnrow = ent(nc.sbuf_tensor("at_cnrow", [1, W_NEAR], F32))
        cfrow = ent(nc.sbuf_tensor("at_cfrow", [1, W_FARM], F32))
        cmb = ent(nc.sbuf_tensor("at_cmb", [128, 512], BF16))
        cmb2 = ent(nc.sbuf_tensor("at_cmb2", [128, 512], BF16))
        aost = ent(nc.sbuf_tensor("at_aost", [128, 2 * 512], BF16))
        psS = ent(nc.psum_tensor("at_psS", [128, 1024], F32))
        psT1 = ent(nc.psum_tensor("at_psT1", [128, 256], BF16))
        psT2 = ent(nc.psum_tensor("at_psT2", [128, 256], BF16))
        psO = ent(nc.psum_tensor("at_psO", [128, 1024], F32))
        psD = ent(nc.psum_tensor("at_psD", [128, 1024], F32))

        xld0 = s["xld"].n

        @blk.sync
        def _(eng):
            eng.dma_start(out=pbsb[:, :], in_=io["pb"][:, :]).then_inc(s["xld"].h, 16)
            eng.dma_start(out=cnrow[0:1, :], in_=io["causal_n"][0:1, :]).then_inc(s["xld"].h, 16)
            eng.dma_start(out=cfrow[0:1, :], in_=io["causal_f"][0:1, :]).then_inc(s["xld"].h, 16)
            s["xld"].n += 48

        @blk.vector
        def _(eng):
            eng = AD(eng)
            eng.wait_ge(s["xld"].h, xld0 + 48)
            eng.tensor_copy(pbhi[:, :], pbsb[:, :])
            eng.tensor_tensor(pblof[:, :], pbsb[:, :], pbhi[:, :], ALU.subtract)
            eng.tensor_copy(pblo[:, :], pblof[:, :]).then_inc(s["dv"].h, 1)
            s["dv"].n += 1
        dv_pb = s["dv"].n

        for srcname, wdt, rowoff in [("ind_near", W_NEAR, 0),
                                     ("ind_far", W_FARM, 128 * W_NEAR)]:
            ncols = 128 * wdt
            nchunks = (ncols + 5119) // 5120
            for ch in range(nchunks):
                c0 = ch * 5120
                cw = min(5120, ncols - c0)
                nsub = (cw + 511) // 512
                dv_prev = s["dv"].n

                @blk.sync
                def _(eng, c0=c0, cw=cw, srcname=srcname, dv_prev=dv_prev):
                    eng.wait_ge(s["dv"].h, dv_prev)
                    eng.dma_start(out=indst[:, 0:cw],
                                  in_=io[srcname][:, c0:c0 + cw]).then_inc(s["xld"].h, 16)
                    s["xld"].n += 16
                xn2 = s["xld"].n
                mmb = s["mm"].n
                dvb = s["dv"].n
                spb = s["sp"].n

                @blk.tensor
                def _(eng, cw=cw, xn2=xn2, nsub=nsub, dvb=dvb):
                    eng.wait_ge(s["xld"].h, xn2)
                    eng.wait_ge(s["dv"].h, dv_pb)
                    for sub in range(nsub):
                        w = min(512, cw - sub * 512)
                        if sub >= 2:
                            eng.wait_ge(s["dv"].h, dvb + sub - 1)
                        bank = (sub % 2) * 512
                        eng.matmul(psD[0:16, bank:bank + w], pbhi[:, :],
                                   indst[:, sub * 512:sub * 512 + w],
                                   start=True, stop=False)
                        eng.matmul(psD[0:16, bank:bank + w], pblo[:, :],
                                   indst[:, sub * 512:sub * 512 + w],
                                   start=False, stop=True).then_inc(s["mm"].h, 1)
                        s["mm"].n += 1

                @blk.vector
                def _(eng, cw=cw, mmb=mmb, nsub=nsub, spb=spb):
                    eng = AD(eng)
                    for sub in range(nsub):
                        w = min(512, cw - sub * 512)
                        eng.wait_ge(s["mm"].h, mmb + sub + 1)
                        eng.wait_ge(s["sp"].h, spb + 16 * sub)
                        bank = (sub % 2) * 512
                        eng.tensor_copy(mrst[0:16, 0:w],
                                        psD[0:16, bank:bank + w]).then_inc(s["dv"].h, 1)
                        s["dv"].n += 1

                @blk.sync
                def _(eng, c0=c0, cw=cw, rowoff=rowoff, dvb2=s["dv"].n - nsub,
                      nsub=nsub, spb=spb):
                    for sub in range(nsub):
                        w = min(512, cw - sub * 512)
                        eng.wait_ge(s["dv"].h, dvb2 + sub + 1)
                        eng.wait_ge(s["sp"].h, spb + 16 * sub)
                        eng.dma_start(
                            out=io["mrow_scr"][0:16, rowoff + c0 + sub * 512:
                                               rowoff + c0 + sub * 512 + w],
                            in_=mrst[0:16, 0:w]).then_inc(s["sp"].h, 16)
                        s["sp"].n += 16

        sp_mrow = s["sp"].n

        @blk.gpsimd
        def _(eng):
            eng.wait_ge(s["sp"].h, sp_mrow)
            for h in range(16):
                eng.wait_ge(s["gp"].h, s["gp"].n)
                eng.dma_start(
                    out=mskn[:, h * W_NEAR:(h + 1) * W_NEAR],
                    in_=io["mrow_scr"][h:h + 1, 0:128 * W_NEAR]
                    .rearrange("o (i c) -> (o i) c", i=128)).then_inc(s["gp"].h, 16)
                s["gp"].n += 16
                eng.wait_ge(s["gp"].h, s["gp"].n)
                eng.dma_start(
                    out=mskf[:, h * W_FARM:(h + 1) * W_FARM],
                    in_=io["mrow_scr"][h:h + 1, 128 * W_NEAR:128 * (W_NEAR + W_FARM)]
                    .rearrange("o (i c) -> (o i) c", i=128)).then_inc(s["gp"].h, 16)
                s["gp"].n += 16
        gp_msk = s["gp"].n
        mm_cv = s["mm"].n

        @blk.tensor
        def _(eng, dv_need=s["dv"].n):
            eng.wait_ge(s["xld"].h, xld0 + 48)
            eng.wait_ge(s["dv"].h, dv_need)
            eng.matmul(psD[:, 0:W_NEAR], sb["c_row1f"][0:1, :],
                       cnrow[0:1, 0:W_NEAR], start=True, stop=True)
            eng.matmul(psD[:, 512:512 + W_FARM], sb["c_row1f"][0:1, :],
                       cfrow[0:1, 0:W_FARM], start=True,
                       stop=True).then_inc(s["mm"].h, 1)
            s["mm"].n += 1

        @blk.vector
        def _(eng, mm_cv=mm_cv, gp_msk=gp_msk):
            eng = AD(eng)
            eng.wait_ge(s["mm"].h, mm_cv + 1)
            eng.wait_ge(s["gp"].h, gp_msk)
            cnv = AP(psD, 0, [[1024, 128], [0, 16], [1, W_NEAR]])
            cfv = AP(psD, 512, [[1024, 128], [0, 16], [1, W_FARM]])
            mnv = AP(mskn, 0, [[16 * W_NEAR, 128], [W_NEAR, 16], [1, W_NEAR]])
            mfv = AP(mskf, 0, [[16 * W_FARM, 128], [W_FARM, 16], [1, W_FARM]])
            mn0 = AP(mskn0, 0, [[16 * W_NEAR, 128], [W_NEAR, 16], [1, W_NEAR]])
            mf0 = AP(mskf0, 0, [[16 * W_FARM, 128], [W_FARM, 16], [1, W_FARM]])
            eng.tensor_tensor(mn0, mnv, cnv, ALU.add)
            eng.tensor_tensor(mf0, mfv, cfv, ALU.add).then_inc(s["dv"].h, 1)
            s["dv"].n += 1
        dv_msk = s["dv"].n

        aog_spill_plan = []

        for hh in range(2):
            QH0 = hh * NQH
            use_v0 = (hh == 0)
            xA0 = s["xldA"].n
            dv_need = s["dv"].n

            @blk.sync
            def _(eng, QH0=QH0, dv_need=dv_need):
                eng.wait_ge(s["dv"].h, dv_need)
                eng.dma_start(
                    out=kh.ap().rearrange("p (c t) -> p c t", c=8),
                    in_=io["k_scr"].rearrange("p (c t) -> p c t", c=8)[:, :, QH0:QH0 + NKH],
                ).then_inc(s["xldA"].h, 16)
                eng.dma_start(
                    out=qh.ap().rearrange("p (c t) -> p c t", c=8),
                    in_=io["q_scr"].rearrange("p (c t) -> p c t", c=8)[:, :, QH0:QH0 + NQH],
                ).then_inc(s["xldA"].h, 16)
                s["xldA"].n += 32
                for vt in range(9):
                    r0 = QH0 + 1504 + vt * 128
                    w = min(128, 1072 - vt * 128)
                    eng.dma_start(out=vn[0:w, vt * 1024:vt * 1024 + 1024],
                                  in_=io["v_scr"][r0:r0 + w, :]).then_inc(s["xldA"].h, 16)
                    s["xldA"].n += 16
                eng.dma_start(
                    out=gh.ap().rearrange("p (c t) -> p c t", c=8),
                    in_=io["g_scr"].rearrange("p (c t) -> p c t", c=8)[:, :, QH0:QH0 + NQH],
                ).then_inc(s["xldA"].h, 16)
                s["xldA"].n += 16
            xA_all = s["xldA"].n

            for which, buf, wid in [(0, qh, NQH), (1, kh, NKH)]:
                nsub = (wid + 511) // 512
                for kc in range(8):
                    @blk.vector
                    def _(eng, buf=buf, wid=wid, kc=kc, xA_all=xA_all,
                          mm_need=s["mm"].n):
                        eng = AD(eng)
                        eng.wait_ge(s["xldA"].h, xA_all)
                        eng.wait_ge(s["mm"].h, mm_need)
                        eng.tensor_tensor(sqst[:, 0:wid], buf[:, kc * wid:(kc + 1) * wid],
                                          buf[:, kc * wid:(kc + 1) * wid], ALU.mult)
                        eng.drain().then_inc(s["dv"].h, 1)
                        s["dv"].n += 1
                    dv_sq = s["dv"].n

                    for sub in range(nsub):
                        w = min(512, wid - sub * 512)
                        mmx = s["mm"].n

                        @blk.tensor
                        def _(eng, sub=sub, w=w, dv_sq=dv_sq, dv_rd=s["dv"].n):
                            eng.wait_ge(s["dv"].h, dv_sq)
                            if sub >= 1:
                                eng.wait_ge(s["dv"].h, dv_rd)
                            eng.matmul(psD[0:2, 0:w], sb["c_selqf"][:, :],
                                       sqst[:, sub * 512:sub * 512 + w],
                                       start=True, stop=True).then_inc(s["mm"].h, 1)
                            s["mm"].n += 1

                        @blk.vector
                        def _(eng, which=which, kc=kc, sub=sub, w=w, mmx=mmx):
                            eng = AD(eng)
                            eng.wait_ge(s["mm"].h, mmx + 1)
                            eng.reduce_max(qkm[0:2, 4:5], psD[0:2, 0:w], axis=AXX)
                            if kc == 0 and sub == 0:
                                eng.tensor_copy(qkm[0:2, which:which + 1],
                                                qkm[0:2, 4:5])
                            else:
                                eng.tensor_tensor(qkm[0:2, which:which + 1],
                                                  qkm[0:2, which:which + 1],
                                                  qkm[0:2, 4:5], ALU.max)
                            eng.drain().then_inc(s["dv"].h, 1)
                            s["dv"].n += 1
            dv_qk = s["dv"].n
            mm_t2x = s["mm"].n

            @blk.tensor
            def _(eng, dv_qk=dv_qk):
                eng.wait_ge(s["dv"].h, dv_qk)
                eng.transpose(psD[0:2, 0:2], qkm[0:2, 0:2],
                              sb["c_identf"][0:2, 0:2]).then_inc(s["mm"].h, 1)
                s["mm"].n += 1

            @blk.vector
            def _(eng, mm_t2x=mm_t2x):
                eng = AD(eng)
                eng.wait_ge(s["mm"].h, mm_t2x + 1)
                eng.reduce_max(qkm[0:2, 5:6], psD[0:2, 0:2], axis=AXX).then_inc(s["dv"].h, 1)
                s["dv"].n += 1
            dv_gm = s["dv"].n
            mm_gm = s["mm"].n

            @blk.tensor
            def _(eng, dv_gm=dv_gm):
                eng.wait_ge(s["dv"].h, dv_gm)
                eng.nop().then_inc(s["mm"].h, 1)
                s["mm"].n += 1

            @blk.vector
            def _(eng, mm_gm=mm_gm, sc_need=s["sc"].n):
                eng = AD(eng)
                eng.wait_ge(s["mm"].h, mm_gm + 1)
                eng.nop()
                eng.drain().then_inc(s["dv"].h, 1)
                s["dv"].n += 1
            dv_fix = s["dv"].n
            sc_sq0 = s["sc"].n

            @blk.scalar
            def _(eng, dv_fix=dv_fix):
                eng = AD(eng)
                eng.wait_ge(s["dv"].h, dv_fix)
                eng.activation(qkm[0:2, 6:7], qkm[0:2, 5:6],
                               AF.Sqrt).then_inc(s["sc"].h, 1)
                s["sc"].n += 1
            sc_sq = s["sc"].n
            mm_pr = s["mm"].n

            @blk.tensor
            def _(eng, sc_sq=sc_sq):
                eng.wait_ge(s["sc"].h, sc_sq)
                eng.transpose(psD[0:1, 0:2], qkm[0:2, 6:7],
                              sb["c_identf"][0:2, 0:2]).then_inc(s["mm"].h, 1)
                s["mm"].n += 1

            @blk.vector
            def _(eng, mm_pr=mm_pr):
                eng = AD(eng)
                eng.wait_ge(s["mm"].h, mm_pr + 1)
                eng.tensor_copy(rows[0:1, 0:2], psD[0:1, 0:2])
                eng.tensor_tensor(rows[0:1, 2:3], rows[0:1, 0:1], rows[0:1, 1:2],
                                  ALU.mult)
                eng.tensor_scalar(rows[0:1, 2:3], rows[0:1, 2:3], -1.02, -0.5,
                                  ALU.mult, ALU.add).then_inc(s["dv"].h, 1)
                s["dv"].n += 1
            dv_row = s["dv"].n
            mm_bc = s["mm"].n

            @blk.tensor
            def _(eng, dv_row=dv_row):
                eng.wait_ge(s["dv"].h, dv_row)
                eng.matmul(psD[:, 512:513], sb["c_row1f"][0:1, :],
                           rows[0:1, 2:3], start=True, stop=True).then_inc(s["mm"].h, 1)
                s["mm"].n += 1

            @blk.vector
            def _(eng, mm_bc=mm_bc):
                eng = AD(eng)
                eng.wait_ge(s["mm"].h, mm_bc + 1)
                eng.tensor_copy(negmb[:, 0:1], psD[:, 512:513]).then_inc(s["dv"].h, 1)
                s["dv"].n += 1
            dv_nmb = s["dv"].n

            def attn_iter(idx, score_fn, Mq, W, exp_dst, acc_ap, av_lhs, out_ap):
                slot = idx % 2
                mm0 = s["mm"].n

                @blk.tensor
                def _(eng, mm0=mm0, sc_need=s["sc"].n, idx=idx):
                    if idx >= 2:
                        eng.wait_ge(s["sc"].h, sc_need)
                    score_fn(eng, slot)
                    s["mm"].n += 1
                mm_s = s["mm"].n

                @blk.scalar
                def _(eng, mm_s=mm_s, Mq=Mq, W=W, slot=slot, exp_dst=exp_dst,
                      acc_ap=acc_ap, dv_need=s["dv"].n, idx=idx):
                    eng = AD(eng)
                    eng.wait_ge(s["mm"].h, mm_s)
                    if idx >= 2:
                        eng.wait_ge(s["dv"].h, dv_need)
                    eng.activation(pbuf[0:Mq, slot * 256:slot * 256 + W],
                                   psS[0:Mq, slot * 512:slot * 512 + W],
                                   AF.Exp, bias=exp_dst,
                                   accum_out=acc_ap).then_inc(s["sc"].h, 1)
                    s["sc"].n += 1
                sc_e = s["sc"].n

                @blk.tensor
                def _(eng, sc_e=sc_e, Mq=Mq, W=W, slot=slot, dv_need=s["dv"].n,
                      idx=idx):
                    eng.wait_ge(s["sc"].h, sc_e)
                    if idx >= 1:
                        eng.wait_ge(s["dv"].h, dv_need)
                    w1_ = min(W, 128)
                    w2 = W - w1_
                    inst = eng.transpose(psT1[0:w1_, 0:Mq],
                                         pbuf[0:Mq, slot * 256:slot * 256 + w1_],
                                         sb["c_ident"][0:Mq, 0:Mq])
                    if w2 > 0:
                        inst = eng.transpose(psT2[0:w2, 0:Mq],
                                             pbuf[0:Mq, slot * 256 + 128:slot * 256 + W],
                                             sb["c_ident"][0:Mq, 0:Mq])
                    inst.then_inc(s["mm"].h, 1)
                    s["mm"].n += 1
                mm_t = s["mm"].n

                @blk.vector
                def _(eng, mm_t=mm_t, Mq=Mq, W=W):
                    eng = AD(eng)
                    eng.wait_ge(s["mm"].h, mm_t)
                    w1_ = min(W, 128)
                    w2 = W - w1_
                    eng.tensor_copy(pts[0:w1_, 0:Mq], psT1[0:w1_, 0:Mq])
                    if w2 > 0:
                        eng.tensor_copy(pts[0:w2, 256:256 + Mq], psT2[0:w2, 0:Mq])
                    eng.drain().then_inc(s["dv"].h, 1)
                    s["dv"].n += 1
                dv_pt = s["dv"].n

                @blk.tensor
                def _(eng, dv_pt=dv_pt, Mq=Mq, W=W, av_lhs=av_lhs, idx=idx,
                      dv_o=s["dv"].n):
                    eng.wait_ge(s["dv"].h, dv_pt)
                    oslot = idx % 2
                    w1_ = min(W, 128)
                    w2 = W - w1_
                    inst = eng.matmul(psO[0:64, oslot * 512:oslot * 512 + Mq],
                                      av_lhs[0], pts[0:w1_, 0:Mq],
                                      start=True, stop=(w2 <= 0))
                    if w2 > 0:
                        inst = eng.matmul(psO[0:64, oslot * 512:oslot * 512 + Mq],
                                          av_lhs[1], pts[0:w2, 256:256 + Mq],
                                          start=False, stop=True)
                    inst.then_inc(s["mm"].h, 1)
                    s["mm"].n += 1
                mm_av = s["mm"].n

                @blk.vector
                def _(eng, mm_av=mm_av, Mq=Mq, out_ap=out_ap, idx=idx):
                    eng = AD(eng)
                    eng.wait_ge(s["mm"].h, mm_av)
                    oslot = idx % 2
                    eng.tensor_copy(out_ap,
                                    psO[0:64, oslot * 512:oslot * 512 + Mq]).then_inc(s["dv"].h, 1)
                    s["dv"].n += 1

            idx = 0
            for t in range(9):
                T0 = 128 * t
                Mq = 128 if t < 8 else 16
                W = Mq + 32
                for h in range(16):
                    rbase = (h % 2) * 64
                    chq = h // 2
                    msk = mskn0 if (use_v0 and t == 0) else mskn

                    def score_n(eng, slot, T0=T0, Mq=Mq, W=W, h=h, rbase=rbase,
                                chq=chq, msk=msk, first=(idx == 0),
                                dv_nmb=dv_nmb):
                        if first:
                            eng.wait_ge(s["dv"].h, dv_nmb)
                        eng.matmul(psS[0:Mq, slot * 512:slot * 512 + W],
                                   qh[rbase:rbase + 64, chq * NQH + T0:chq * NQH + T0 + Mq],
                                   kh[rbase:rbase + 64, chq * NKH + 1504 + T0:chq * NKH + 1504 + T0 + W],
                                   start=True, stop=False, tile_position=(rbase, 0))
                        eng.matmul(psS[0:Mq, slot * 512:slot * 512 + W],
                                   sb["c_ident"][:, 0:Mq],
                                   msk[:, h * W_NEAR:h * W_NEAR + W],
                                   start=False, stop=True,
                                   tile_position=(0, 0)).then_inc(s["mm"].h, 1)

                    av = [vn[0:min(W, 128), t * 1024 + h * 64:t * 1024 + h * 64 + 64]]
                    if W > 128:
                        av.append(vn[0:W - 128, (t + 1) * 1024 + h * 64:(t + 1) * 1024 + h * 64 + 64])
                    attn_iter(idx, score_n, Mq, W,
                              negmb[0:Mq, 0:1],
                              sums_n[0:Mq, t * 16 + h:t * 16 + h + 1],
                              av,
                              von[rbase:rbase + 64, chq * NQH + T0:chq * NQH + T0 + Mq])
                    idx += 1

            for r in range(16):
                gpr0 = s["gp"].n

                @blk.gpsimd
                def _(eng, r=r, QH0=QH0, dv_need=s["dv"].n):
                    eng.wait_ge(s["dv"].h, dv_need)
                    sl = r % 2
                    eng.wait_ge(s["gp"].h, s["gp"].n)
                    eng.dma_start(
                        out=vr1[:, sl * 1024:sl * 1024 + 1024],
                        in_=io["v_scr"][QH0 + r:QH0 + r + 16 * 127 + 1:16, :]).then_inc(s["gp"].h, 16)
                    s["gp"].n += 16
                    eng.wait_ge(s["gp"].h, s["gp"].n)
                    eng.dma_start(
                        out=vr2[0:33, sl * 1024:sl * 1024 + 1024],
                        in_=io["v_scr"][QH0 + r + 16 * 128:QH0 + r + 16 * 160 + 1:16, :]).then_inc(s["gp"].h, 16)
                    s["gp"].n += 16
                gpr = s["gp"].n

                msk = mskf0 if use_v0 else mskf
                for h in range(16):
                    rbase = (h % 2) * 64
                    chq = h // 2
                    sl = r % 2

                    def score_f(eng, slot, r=r, h=h, rbase=rbase, chq=chq,
                                msk=msk, gpr=gpr, first=(h == 0)):
                        if first:
                            eng.wait_ge(s["gp"].h, gpr)
                        eng.matmul(psS[0:SUBQ, slot * 512:slot * 512 + W_FARH],
                                   qh[rbase:rbase + 64, chq * NQH + r:chq * NQH + r + 16 * (SUBQ - 1) + 1:16],
                                   kh[rbase:rbase + 64, chq * NKH + r:chq * NKH + r + 16 * (W_FARH - 1) + 1:16],
                                   start=True, stop=False, tile_position=(rbase, 0))
                        eng.matmul(psS[0:SUBQ, slot * 512:slot * 512 + W_FARH],
                                   sb["c_ident"][:, 0:SUBQ],
                                   msk[:, h * W_FARM:h * W_FARM + W_FARH],
                                   start=False, stop=True,
                                   tile_position=(0, 0)).then_inc(s["mm"].h, 1)

                    av = [vr1[:, sl * 1024 + h * 64:sl * 1024 + h * 64 + 64],
                          vr2[0:33, sl * 1024 + h * 64:sl * 1024 + h * 64 + 64]]
                    attn_iter(idx, score_f, SUBQ, W_FARH,
                              negmb[0:SUBQ, 0:1],
                              sums_f[0:SUBQ, r * 16 + h:r * 16 + h + 1],
                              av,
                              vof[rbase:rbase + 64, chq * NQH + r:chq * NQH + r + 16 * (SUBQ - 1) + 1:16])
                    idx += 1

            dv_att = s["dv"].n

            for t in range(9):
                T0 = 128 * t
                Mq = 128 if t < 8 else 16
                mmx = s["mm"].n

                @blk.tensor
                def _(eng, t=t, Mq=Mq, dv_att=dv_att, dv_need=s["dv"].n):
                    eng.wait_ge(s["dv"].h, dv_att)
                    if t >= 1:
                        eng.wait_ge(s["dv"].h, dv_need)
                    eng.transpose(psD[0:16, 0:Mq], sums_n[0:Mq, t * 16:(t + 1) * 16],
                                  sb["c_identf"][0:Mq, 0:Mq]).then_inc(s["mm"].h, 1)
                    s["mm"].n += 1

                @blk.vector
                def _(eng, t=t, T0=T0, Mq=Mq, mmx=mmx):
                    eng = AD(eng)
                    eng.wait_ge(s["mm"].h, mmx + 1)
                    eng.tensor_copy(denr[0:16, T0:T0 + Mq],
                                    psD[0:16, 0:Mq]).then_inc(s["dv"].h, 1)
                    s["dv"].n += 1

            for r in range(16):
                mmx = s["mm"].n

                @blk.tensor
                def _(eng, r=r, dv_need=s["dv"].n):
                    eng.wait_ge(s["dv"].h, dv_need)
                    eng.transpose(psD[0:16, 0:SUBQ], sums_f[0:SUBQ, r * 16:(r + 1) * 16],
                                  sb["c_identf"][0:SUBQ, 0:SUBQ]).then_inc(s["mm"].h, 1)
                    s["mm"].n += 1

                @blk.vector
                def _(eng, r=r, mmx=mmx):
                    eng = AD(eng)
                    eng.wait_ge(s["mm"].h, mmx + 1)
                    eng.tensor_copy(denf[0:16, r:r + 16 * (SUBQ - 1) + 1:16],
                                    psD[0:16, 0:SUBQ]).then_inc(s["dv"].h, 1)
                    s["dv"].n += 1

            @blk.vector
            def _(eng):
                eng = AD(eng)
                eng.tensor_tensor(denr[:, :], denr[:, :], denf[:, :], ALU.add)
                eng.tensor_scalar(denr[:, :], denr[:, :], 1e-30, None, ALU.add)
                eng.reciprocal(denr[:, :], denr[:, :]).then_inc(s["dv"].h, 1)
                s["dv"].n += 1
            dv_den = s["dv"].n

            subs = [(0, 512), (512, 512), (1024, 16)]
            for c in range(8):
                for si, (sb0, sw) in enumerate(subs):
                    gidx = getattr(kb, "_spill_gidx", 0)
                    p = "spA" if gidx % 2 == 0 else "spB"
                    kb._spill_gidx = gidx + 1
                    tgt = s[p].n + 16
                    mmx = s["mm"].n

                    @blk.tensor
                    def _(eng, c=c, sb0=sb0, sw=sw, dv_den=dv_den, si=si,
                          dv_need=s["dv"].n):
                        eng.wait_ge(s["dv"].h, dv_den)
                        if c + si > 0:
                            eng.wait_ge(s["dv"].h, dv_need)
                        eng.matmul(psD[:, 0:sw], sb["c_selh"][:, c * 128:(c + 1) * 128],
                                   denr[0:16, sb0:sb0 + sw],
                                   start=True, stop=True).then_inc(s["mm"].h, 1)
                        s["mm"].n += 1

                    @blk.vector
                    def _(eng, c=c, sb0=sb0, sw=sw, mmx=mmx, p=p, tgt=tgt,
                          QH0=QH0):
                        eng = AD(eng)
                        eng.wait_ge(s["mm"].h, mmx + 1)
                        eng.wait_ge(s[p].h, tgt - 16)
                        slot = (getattr(kb, "_aog_slot", 0)) % 2
                        kb._aog_slot = getattr(kb, "_aog_slot", 0) + 1
                        eng.tensor_tensor(cmb[:, 0:sw],
                                          von[:, c * NQH + sb0:c * NQH + sb0 + sw],
                                          vof[:, c * NQH + sb0:c * NQH + sb0 + sw],
                                          ALU.add)
                        eng.tensor_tensor(cmb2[:, 0:sw], cmb[:, 0:sw],
                                          psD[:, 0:sw], ALU.mult)
                        eng.tensor_tensor(aost[:, slot * 512:slot * 512 + sw],
                                          cmb2[:, 0:sw],
                                          gh[:, c * NQH + sb0:c * NQH + sb0 + sw],
                                          ALU.mult)
                        eng.drain().then_inc(s["dv"].h, 1)
                        s["dv"].n += 1
                    dv_cb = s["dv"].n

                    @blk.sync
                    def _(eng, c=c, sb0=sb0, sw=sw, dv_cb=dv_cb, p=p, tgt=tgt,
                          QH0=QH0):
                        eng.wait_ge(s["dv"].h, dv_cb)
                        eng.wait_ge(s[p].h, tgt - 16)
                        slot = (getattr(kb, "_aog_slot_sp", 0)) % 2
                        kb._aog_slot_sp = getattr(kb, "_aog_slot_sp", 0) + 1
                        eng.dma_start(
                            out=io["ao_scr"][:, c * NQ + QH0 + sb0:c * NQ + QH0 + sb0 + sw],
                            in_=aost[:, slot * 512:slot * 512 + sw]).then_inc(s[p].h, 16)
                        s[p].n += 16

        spfA, spfB = s["spA"].n, s["spB"].n

        @blk.sync
        def _(eng):
            eng.wait_ge(s["spA"].h, spfA)
            eng.wait_ge(s["spB"].h, spfB)



def matmul_fm_groups(kb, blk, tag, wname, wkc, woc, src_scr, src_cols,
                     dst_scr, resid_scr, resid_off, scale_half=False,
                     resid2=None, resid2_off=0):
    nc, s, sb, io = kb.nc, kb.s, kb.sb, kb.io
    import contextlib
    with contextlib.ExitStack() as _es:
        ent = _es.enter_context
        wsb = ent(nc.sbuf_tensor(tag + "_w", [128, wkc * woc * 128], BF16))
        ast_ = ent(nc.sbuf_tensor(tag + "_a", [128, 2 * wkc * 512], BF16))
        xs = ent(nc.sbuf_tensor(tag + "_xs", [128, 4 * D], F32))
        xs2 = ent(nc.sbuf_tensor(tag + "_xs2", [128, 4 * D], F32)) if resid2 is not None else None
        dfm = ent(nc.sbuf_tensor(tag + "_d", [128, woc * 512], BF16))
        xo = ent(nc.sbuf_tensor(tag + "_xo", [128, D], F32))
        psP = ent(nc.psum_tensor(tag + "_pP", [128, 1024], F32))
        psT = ent(nc.psum_tensor(tag + "_pT", [128, 1024], BF16))

        gp0 = s["gp"].n

        fence = {k: s[k].n for k in ["mm", "ev", "dv", "sc"]}

        @blk.gpsimd
        def _(eng):
            for k, v in fence.items():
                eng.wait_ge(s[k].h, v)
            eng.wait_ge(s["gp"].h, s["gp"].n)
            eng.dma_start(
                out=wsb.ap().rearrange("p (c d) -> p c d", c=wkc),
                in_=io[wname].rearrange("(c p) d -> p c d", p=128)).then_inc(s["gp"].h, 16)
            s["gp"].n += 16
        gpw = s["gp"].n

        groups = []
        t = 0
        while t < src_cols:
            groups.append((t, min(512, src_cols - t)))
            t += 512

        for gi, (a0, alen) in enumerate(groups):
            ntl = (alen + 127) // 128
            sl = gi % 2
            sem = s["xldA"] if sl == 0 else s["xldB"]
            x0 = sem.n

            @blk.sync
            def _(eng, a0=a0, alen=alen, ntl=ntl, sl=sl, sem=sem,
                  dv_need=getattr(kb, "_" + tag + "_dvprev", 0)):
                eng.wait_ge(s["dv"].h, dv_need)
                eng.dma_start(
                    out=AP(ast_, sl * wkc * 512, [[2 * wkc * 512, 128], [512, wkc], [1, alen]]),
                    in_=src_scr.rearrange("p (c t) -> p c t", c=wkc)[:, :, a0:a0 + alen],
                ).then_inc(sem.h, 16)
                sem.n += 16
                for i in range(ntl):
                    r0 = resid_off + a0 + i * 128
                    w = min(128, src_cols - (a0 + i * 128))
                    eng.dma_start(out=xs[0:w, i * D:i * D + D],
                                  in_=resid_scr[r0:r0 + w, :]).then_inc(sem.h, 16)
                    sem.n += 16
                    if resid2 is not None:
                        r2 = resid2_off + a0 + i * 128
                        eng.dma_start(out=xs2[0:w, i * D:i * D + D],
                                      in_=resid2[r2:r2 + w, :]).then_inc(sem.h, 16)
                        sem.n += 16
            x1_ = sem.n
            mm0 = s["mm"].n
            dv0 = s["dv"].n

            @blk.tensor
            def _(eng, a0=a0, alen=alen, sl=sl, sem=sem, x1_=x1_, gpw=gpw,
                  dv0=dv0):
                eng.wait_ge(sem.h, x1_)
                eng.wait_ge(s["gp"].h, gpw)
                for oc in range(woc):
                    slot = oc % 2
                    if oc >= 2:
                        eng.wait_ge(s["dv"].h, dv0 + (oc - 2) + 1)
                    inst = None
                    for kc in range(wkc):
                        inst = eng.matmul(
                            psP[:, slot * 512:slot * 512 + alen],
                            wsb[:, kc * woc * 128 + oc * 128:kc * woc * 128 + oc * 128 + 128],
                            ast_[:, sl * wkc * 512 + kc * 512:sl * wkc * 512 + kc * 512 + alen],
                            start=(kc == 0), stop=(kc == wkc - 1))
                    inst.then_inc(s["mm"].h, 1)
                    s["mm"].n += 1

            @blk.vector
            def _(eng, alen=alen, mm0=mm0):
                eng = AD(eng)
                for oc in range(woc):
                    slot = oc % 2
                    eng.wait_ge(s["mm"].h, mm0 + oc + 1)
                    eng.tensor_copy(dfm[:, oc * 512:oc * 512 + alen],
                                    psP[:, slot * 512:slot * 512 + alen]).then_inc(s["dv"].h, 1)
                    s["dv"].n += 1
            dv_dfm = s["dv"].n
            mmT = s["mm"].n

            @blk.tensor
            def _(eng, alen=alen, ntl=ntl, dv_dfm=dv_dfm):
                eng.wait_ge(s["dv"].h, dv_dfm)
                for i in range(ntl):
                    w = min(128, alen - i * 128)
                    if i >= 1:
                        eng.wait_ge(s["dv"].h, dv_dfm + i)
                    inst = None
                    for ch in range(8):
                        inst = eng.transpose(psT[0:w, ch * 128:(ch + 1) * 128],
                                             dfm[:, ch * 512 + i * 128:ch * 512 + i * 128 + w],
                                             sb["c_ident"][:, :])
                    inst.then_inc(s["mm"].h, 1)
                    s["mm"].n += 1

            @blk.vector
            def _(eng, a0=a0, alen=alen, ntl=ntl, mmT=mmT,
                  spn=(s["spA"].n, s["spB"].n)):
                eng = AD(eng)
                for i in range(ntl):
                    w = min(128, alen - i * 128)
                    eng.wait_ge(s["mm"].h, mmT + i + 1)
                    gidx = kb._spill_gidx
                    p = "spA" if gidx % 2 == 0 else "spB"
                    eng.wait_ge(s["spA"].h, s["spA"].n)
                    eng.wait_ge(s["spB"].h, s["spB"].n)
                    if scale_half:
                        eng.scalar_tensor_tensor(xo[0:w, :], psT[0:w, 0:1024], 0.5,
                                                 xs[0:w, i * D:i * D + D],
                                                 ALU.mult, ALU.add)
                    else:
                        eng.tensor_tensor(xo[0:w, :], psT[0:w, 0:1024],
                                          xs[0:w, i * D:i * D + D], ALU.add)
                    if resid2 is not None:
                        eng.tensor_tensor(xo[0:w, :], xo[0:w, :],
                                          xs2[0:w, i * D:i * D + D], ALU.add)
                    eng.drain().then_inc(s["dv"].h, 1)
                    s["dv"].n += 1
                    dv_x = s["dv"].n

                    kb._mf_spills.append((a0 + i * 128, w, p, s[p].n + 16, dv_x))
                    kb._spill_gidx += 1
                    s[p].n += 16

            @blk.sync
            def _(eng, dst_scr=dst_scr):
                while kb._mf_emit < len(kb._mf_spills):
                    r0, w, p, tgt, dv_x = kb._mf_spills[kb._mf_emit]
                    kb._mf_emit += 1
                    eng.wait_ge(s["dv"].h, dv_x)
                    eng.wait_ge(s[p].h, tgt - 16)
                    eng.dma_start(out=dst_scr[r0:r0 + w, :],
                                  in_=xo[0:w, :]).then_inc(s[p].h, 16)
            setattr(kb, "_" + tag + "_dvprev", s["dv"].n)

        spfA, spfB = s["spA"].n, s["spB"].n

        @blk.sync
        def _(eng):
            eng.wait_ge(s["spA"].h, spfA)
            eng.wait_ge(s["spB"].h, spfB)


def phase_s2d(kb, blk):
    kb._mf_spills = []
    kb._mf_emit = 0
    kb._spill_gidx = getattr(kb, "_spill_gidx", 0)
    matmul_fm_groups(kb, blk, "s2d", "w_out", 8, 8, kb.io["ao_scr"], NQ,
                     kb.io["x2_scr"], kb.io["z0_scr"], 0)



def phase_s2ef(kb, blk):
    nc, s, sb, io = kb.nc, kb.s, kb.sb, kb.io
    import contextlib
    with contextlib.ExitStack() as _es:
        ent = _es.enter_context
        wfg = ent(nc.sbuf_tensor("ef_wfg", [128, 8 * DS], BF16))
        wfu = ent(nc.sbuf_tensor("ef_wfu", [128, 8 * DS], BF16))
        xs = ent(nc.sbuf_tensor("ef_xs", [128, 2 * D], F32))
        cl = ent(nc.sbuf_tensor("ef_cl", [128, 8], F32))
        dmp = ent(nc.sbuf_tensor("ef_dmp", [128, D], F32))
        xnt = ent(nc.sbuf_tensor("ef_xnt", [128, D], BF16))
        xnfm = ent(nc.sbuf_tensor("ef_xnfm", [128, 8 * 512], BF16))
        sg = ent(nc.sbuf_tensor("ef_sg", [128, 512], BF16))
        ug = ent(nc.sbuf_tensor("ef_ug", [128, 512], BF16))
        gst = ent(nc.sbuf_tensor("ef_gst", [128, 2 * 512], BF16))
        psT = ent(nc.psum_tensor("ef_pT", [128, 1024], BF16))
        psG = ent(nc.psum_tensor("ef_pG", [128, 512], F32))
        psU = ent(nc.psum_tensor("ef_pU", [128, 512], F32))

        gp0 = s["gp"].n

        @blk.gpsimd
        def _(eng):
            eng.wait_ge(s["gp"].h, s["gp"].n)
            eng.dma_start(
                out=wfg.ap().rearrange("p (c d) -> p c d", c=8),
                in_=io["w_fg"].rearrange("(c p) d -> p c d", p=128)).then_inc(s["gp"].h, 16)
            s["gp"].n += 16
            eng.wait_ge(s["gp"].h, s["gp"].n)
            eng.dma_start(
                out=wfu.ap().rearrange("p (c d) -> p c d", c=8),
                in_=io["w_fu"].rearrange("(c p) d -> p c d", p=128)).then_inc(s["gp"].h, 16)
            s["gp"].n += 16
        gpw = s["gp"].n

        groups = []
        t = 0
        while t < NQ:
            groups.append((t, min(512, NQ - t)))
            t += 512

        for a0, alen in groups:
            ntl = (alen + 127) // 128
            xA0, xB0 = s["xldA"].n, s["xldB"].n
            dv0, sc0, mm0, ev0 = (s[k].n for k in ["dv", "sc", "mm", "ev"])

            @blk.sync
            def _(eng, a0=a0, ntl=ntl, dv0=dv0):
                for i in range(ntl):
                    if i >= 2:
                        eng.wait_ge(s["dv"].h, dv0 + 2 * (i - 2) + 2)
                    sem = s["xldA"] if i % 2 == 0 else s["xldB"]
                    r0 = a0 + i * 128
                    w = min(128, NQ - r0)
                    eng.dma_start(out=xs[0:w, (i % 2) * D:(i % 2) * D + D],
                                  in_=io["x2_scr"][r0:r0 + w, :]).then_inc(sem.h, 16)
                    sem.n += 16

            def evac_fm(eng, j):
                eng.tensor_copy(
                    AP(xnfm, j * 128, [[8 * 512, 128], [512, 8], [1, 128]]),
                    psT.ap().rearrange("p (c t) -> p c t", c=8)).then_inc(s["ev"].h, 1)
                s["ev"].n += 1

            @blk.vector
            def _(eng, ntl=ntl, xA0=xA0, xB0=xB0, sc0=sc0, mm0=mm0):
                eng = AD(eng)
                for i in range(ntl):
                    if i % 2 == 0:
                        eng.wait_ge(s["xldA"].h, xA0 + 16 * (i // 2 + 1))
                    else:
                        eng.wait_ge(s["xldB"].h, xB0 + 16 * (i // 2 + 1))
                    xt = xs[:, (i % 2) * D:(i % 2) * D + D]
                    eng.scalar_tensor_tensor(dmp[:, :], xt, 1.0, xt, ALU.bypass,
                                             ALU.mult, accum_out=cl[:, 0:1])
                    eng.tensor_scalar(cl[:, 1:2], cl[:, 0:1], 1.0 / D, RMS_EPS,
                                      ALU.mult, ALU.add).then_inc(s["dv"].h, 1)
                    s["dv"].n += 1
                    eng.wait_ge(s["sc"].h, sc0 + i + 1)
                    eng.wait_ge(s["mm"].h, mm0 + i)
                    if i >= 1:
                        evac_fm(eng, i - 1)
                    eng.reciprocal(cl[:, 3:4], cl[:, 2:3])
                    eng.tensor_scalar(dmp[:, :], xt, cl[:, 3:4], None, ALU.mult)
                    eng.tensor_tensor(xnt[:, :], dmp[:, :], sb["g2b"][:, :],
                                      ALU.mult).then_inc(s["dv"].h, 1)
                    s["dv"].n += 1
                eng.wait_ge(s["mm"].h, mm0 + ntl)
                evac_fm(eng, ntl - 1)

            @blk.scalar
            def _(eng, ntl=ntl, dv0=dv0):
                eng = AD(eng)
                for i in range(ntl):
                    eng.wait_ge(s["dv"].h, dv0 + 2 * i + 1)
                    eng.activation(cl[:, 2:3], cl[:, 1:2], AF.Sqrt).then_inc(s["sc"].h, 1)
                    s["sc"].n += 1

            @blk.tensor
            def _(eng, ntl=ntl, dv0=dv0, ev0=ev0):
                for i in range(ntl):
                    eng.wait_ge(s["dv"].h, dv0 + 2 * i + 2)
                    if i >= 1:
                        eng.wait_ge(s["ev"].h, ev0 + i)
                    inst = None
                    for ch in range(8):
                        inst = eng.transpose(psT[:, ch * 128:(ch + 1) * 128],
                                             xnt[:, ch * 128:(ch + 1) * 128],
                                             sb["c_ident"][:, :])
                    inst.then_inc(s["mm"].h, 1)
                    s["mm"].n += 1
            evF = s["ev"].n
            mmP = s["mm"].n
            scP = s["sc"].n
            dvP = s["dv"].n

            @blk.tensor
            def _(eng, alen=alen, evF=evF, gpw=gpw, scP=scP, dvP=dvP):
                eng.wait_ge(s["ev"].h, evF)
                eng.wait_ge(s["gp"].h, gpw)
                for oc in range(22):
                    if oc >= 1:
                        eng.wait_ge(s["sc"].h, scP + oc)
                        eng.wait_ge(s["dv"].h, dvP + 2 * oc - 1)
                    inst = None
                    for kc in range(8):
                        inst = eng.matmul(psG[:, 0:alen],
                                          wfg[:, kc * DS + oc * 128:kc * DS + oc * 128 + 128],
                                          xnfm[:, kc * 512:kc * 512 + alen],
                                          start=(kc == 0), stop=(kc == 7))
                    inst.then_inc(s["mm"].h, 1)
                    s["mm"].n += 1
                    inst = None
                    for kc in range(8):
                        inst = eng.matmul(psU[:, 0:alen],
                                          wfu[:, kc * DS + oc * 128:kc * DS + oc * 128 + 128],
                                          xnfm[:, kc * 512:kc * 512 + alen],
                                          start=(kc == 0), stop=(kc == 7))
                    inst.then_inc(s["mm"].h, 1)
                    s["mm"].n += 1

            @blk.scalar
            def _(eng, alen=alen, mmP=mmP, dvP=dvP):
                eng = AD(eng)
                for oc in range(22):
                    eng.wait_ge(s["mm"].h, mmP + 2 * oc + 1)
                    if oc >= 1:
                        eng.wait_ge(s["dv"].h, dvP + 2 * oc)
                    eng.activation(sg[:, 0:alen], psG[:, 0:alen],
                                   AF.Silu).then_inc(s["sc"].h, 1)
                    s["sc"].n += 1

            @blk.vector
            def _(eng, a0=a0, alen=alen, mmP=mmP, scP=scP):
                eng = AD(eng)
                for oc in range(22):
                    eng.wait_ge(s["mm"].h, mmP + 2 * oc + 2)
                    eng.wait_ge(s["sc"].h, scP + oc + 1)
                    eng.tensor_copy(ug[:, 0:alen], psU[:, 0:alen])
                    eng.drain().then_inc(s["dv"].h, 1)
                    s["dv"].n += 1
                    gidx = kb._spill_gidx
                    p = "spA" if gidx % 2 == 0 else "spB"
                    eng.wait_ge(s[p].h, s[p].n)
                    eng.tensor_tensor(gst[:, (oc % 2) * 512:(oc % 2) * 512 + alen],
                                      sg[:, 0:alen], ug[:, 0:alen],
                                      ALU.mult).then_inc(s["dv"].h, 1)
                    s["dv"].n += 1
                    kb._mf_spills.append((a0, alen, oc, p, s[p].n + 16, s["dv"].n))
                    kb._spill_gidx += 1
                    s[p].n += 16

            @blk.sync
            def _(eng):
                while kb._mf_emit < len(kb._mf_spills):
                    a0_, alen_, oc, p, tgt, dv_x = kb._mf_spills[kb._mf_emit]
                    kb._mf_emit += 1
                    eng.wait_ge(s["dv"].h, dv_x)
                    eng.wait_ge(s[p].h, tgt - 16)
                    eng.dma_start(
                        out=io["gt_scr"][:, oc * NQ + a0_:oc * NQ + a0_ + alen_],
                        in_=gst[:, (oc % 2) * 512:(oc % 2) * 512 + alen_]).then_inc(s[p].h, 16)

        spfA, spfB = s["spA"].n, s["spB"].n

        @blk.sync
        def _(eng):
            eng.wait_ge(s["spA"].h, spfA)
            eng.wait_ge(s["spB"].h, spfB)

    kb._mf_spills = []
    kb._mf_emit = 0
    matmul_fm_groups(kb, blk, "ef2", "w_fd", 22, 8, kb.io["gt_scr"], NQ,
                     kb.io["x2b_scr"], kb.io["x2_scr"], 0,
                     resid2=kb.io["x1_scr"], resid2_off=KVQ)



def phase_s3(kb, blk):
    nc, s, io, sb = kb.nc, kb.s, kb.io, kb.sb
    kb.sb["w1q"] = kb.sbuf("w1q_b", [128, 8 * DH], FP8)
    kb.sb["w2q"] = kb.sbuf("w2q_b", [128, 32 * D], FP8)
    wld0 = s["wld"].n

    @blk.sync
    def _(eng):
        eng.dma_start(out=kb.sb["w1q"][:, :], in_=io["w1q_scr"][:, :]).then_inc(s["wld"].h, 16)
        eng.dma_start(out=kb.sb["w2q"][:, :], in_=io["w2q_scr"][:, :]).then_inc(s["wld"].h, 16)
        s["wld"].n += 32
    wldw = s["wld"].n

    R0 = 14

    def xin(eng, i, dst, sem):
        r0 = R0 + i * 128
        w = min(128, NQ - r0)
        eng.dma_start(out=dst[0:w, :],
                      in_=io["x2b_scr"][r0:r0 + w, :]).then_inc(sem.h, 16)

    def resid(eng, tok0, w, dst):
        eng.dma_start(out=dst[0:w, :],
                      in_=io["x2b_scr"][R0 + tok0:R0 + tok0 + w, :]).then_inc(s["xld"].h, 16)

    def outd(eng, tok0, w, src):
        eng.dma_start(out=io["xout"][tok0 - 2:tok0 - 2 + w, :],
                      in_=src[0:w, :]).then_inc(s["sp"].h, 16)

    groups = [(2 + GLEN * k, GLEN) for k in range(2048 // GLEN)]
    edge_fix = [(groups[0][0], 0, sb["c_edgel"][:, 0:1]),
                (groups[-1][0], GLEN + 1, sb["c_edger"][:, 0:1])]
    emit_bitffn(kb, blk, "s3", 17, groups, xin, resid, outd,
                edge_fix=edge_fix, pre_wait=[("wld", wldw)])


PHASES = ["p0", "s1", "s2ab", "s2c", "s2d", "s2ef", "s3"]


def build_program(debug=(), upto=None):
    nc = bass.Bass("TRN2", target_bir_lowering=False)
    kb = KB(nc, debug, upto)
    declare_io(kb)
    fns = dict(p0=phase_p0, s1=phase_s1, s2ab=phase_s2ab, s2c=phase_s2c,
               s2d=phase_s2d, s2ef=phase_s2ef, s3=phase_s3)
    with nc.Block() as blk:
        for ph in PHASES:
            fns[ph](kb, blk)
            kb.barrier(blk)
            if ph == "s1":
                kb.free("w2q")
                kb.free("w1q")
            if upto == ph:
                break
    return nc, kb


def kernel(**inputs):
    nc, kb = build_program()
    in_maps = [host_core_inputs(c, inputs) for c in range(NCORE)]
    res = run_bass_kernel_spmd(nc, in_maps, core_ids=list(range(NCORE)))
    out = np.zeros((B, N, D), dtype=np.float32)
    for c in range(NCORE):
        b, j = c // 4, c % 4
        out[b, j * CHUNK:(j + 1) * CHUNK] = res.results[c]["xout"]
    return out


# version 02
# speedup vs baseline: 1.0346x; 1.0346x over previous
import numpy as np
import ml_dtypes
import concourse.bass as bass
import concourse.mybir as mybir
from concourse.bass_utils import run_bass_kernel_spmd

F32 = mybir.dt.float32
BF16 = mybir.dt.bfloat16
FP8 = mybir.dt.float8e4
I8 = mybir.dt.int8
AF = mybir.ActivationFunctionType
ALU = mybir.AluOpType
AXX = mybir.AxisListType.X
AP = bass.AP

D = 1024
DH = 4096
DS = 2816
H = 16
B = 2
N = 8192
CHUNK = 2048
NCORE = 8

OFFSETS = sorted(set(range(0, 33)) | {48, 64, 96, 128, 192, 256, 384, 512, 768, 1024, 1536})
FAR_SUB = [o // 16 for o in OFFSETS if o > 32]

NQ = 2080
NKV = 3616
NXW = 3712
XKV = 2
KVQ = 1536

NQH = 1040
NKH = 2576
W_NEAR = 160
W_FARM = 224
SUBQ = 65
W_FARH = 161

NEG = -30000.0
LN_EPS = 1e-5
RMAGIC = 12582912.0
RMS_EPS = 1e-6


class Sem:
    def __init__(self, nc, name):
        self.h = nc.alloc_semaphore(name=name)
        self.n = 0


class AD:
    _OPS = {"tensor_tensor", "tensor_scalar", "tensor_copy", "tensor_add",
            "tensor_sub", "tensor_mul", "tensor_max", "scalar_tensor_tensor",
            "tensor_reduce", "reduce_max", "reduce_sum", "reciprocal",
            "activation", "copy", "memset", "add", "mul", "sqrt", "square"}
    _STATE = {}

    @classmethod
    def reset(cls):
        for pw, pr in cls._STATE.values():
            pw.clear()
            pr.clear()

    def __init__(self, eng):
        self._e = eng
        self._pw, self._pr = AD._STATE.setdefault(str(eng.engine), ([], []))

    @staticmethod
    def _box(ap):
        t = ap.tensor
        rowlen = 1
        for s in tuple(t.shape)[1:]:
            rowlen *= int(s)
        bp = ap.base_partition()
        ps = ap.partition_size()
        c0, span = 0, rowlen
        if ap.ndim == 2 and ap.stride() >= 1:
            c = ap.offset - bp * rowlen
            sp = (ap.free_size() - 1) * ap.stride() + 1
            if 0 <= c and c + sp <= rowlen:
                c0, span = c, sp
        return (t.name, bp, bp + ps, c0, c0 + span)

    @staticmethod
    def _hit(b, lst):
        for o in lst:
            if (b[0] == o[0] and b[1] < o[2] and o[1] < b[2]
                    and b[3] < o[4] and o[3] < b[4]):
                return True
        return False

    def __getattr__(self, k):
        a = getattr(self._e, k)
        if k == "drain":
            def fd(*args, **kw):
                self._pw.clear()
                self._pr.clear()
                return a(*args, **kw)
            return fd
        if k in AD._OPS:
            def f(*args, **kw):
                outs, ins = [], []
                first = True
                for x in args:
                    if isinstance(x, AP):
                        if first:
                            outs.append(x)
                            first = False
                        else:
                            ins.append(x)
                for kk, x in kw.items():
                    if isinstance(x, AP):
                        (outs if kk == "accum_out" else ins).append(x)
                ob = [AD._box(x) for x in outs]
                ib = [AD._box(x) for x in ins]
                if (any(AD._hit(b, self._pw) for b in ib)
                        or any(AD._hit(b, self._pw) for b in ob)
                        or any(AD._hit(b, self._pr) for b in ob)):
                    self._e.drain()
                    self._pw.clear()
                    self._pr.clear()
                self._pw.extend(ob)
                self._pr.extend(ib)
                return a(*args, **kw)
            return f
        return a



def host_consts():
    ident = np.eye(128, dtype=np.float32)
    selq = np.zeros((128, 2), dtype=np.float32)
    selq[0:64, 0] = 1.0
    selq[64:128, 1] = 1.0
    sel2 = np.zeros((2, 128), dtype=np.float32)
    sel2[0, 0:64] = 1.0
    sel2[1, 64:128] = 1.0
    ind_near = np.zeros((45, 128, W_NEAR), dtype=np.float32)
    for i in range(128):
        for o in range(0, 33):
            c = i + 32 - o
            if 0 <= c < W_NEAR:
                ind_near[OFFSETS.index(o), i, c] = 1.0
    ind_near[44] = 1.0 - np.minimum(ind_near[0:44].sum(axis=0), 1.0)
    ind_far = np.zeros((45, 128, W_FARM), dtype=np.float32)
    for i in range(128):
        for os_ in FAR_SUB:
            c = i + 96 - os_
            if 0 <= c < W_FARM:
                ind_far[OFFSETS.index(os_ * 16), i, c] = 1.0
    ind_far[44] = 1.0 - np.minimum(ind_far[0:44].sum(axis=0), 1.0)
    selh = np.zeros((16, 8 * 128), dtype=np.float32)
    for c in range(8):
        selh[2 * c, c * 128:c * 128 + 64] = 1.0
        selh[2 * c + 1, c * 128 + 64:c * 128 + 128] = 1.0
    return dict(
        selh=selh,
        ident=ident.astype(ml_dtypes.bfloat16),
        identf=ident,
        selq=selq.astype(ml_dtypes.bfloat16),
        selqf=selq,
        sel2f=sel2,
        row1=np.ones((1, 128), dtype=ml_dtypes.bfloat16),
        row1f=np.ones((1, 128), dtype=np.float32),
        onescol=np.ones((128, 1), dtype=np.float32),
        ind_near=ind_near.reshape(45, -1).astype(ml_dtypes.bfloat16),
        ind_far=ind_far.reshape(45, -1).astype(ml_dtypes.bfloat16),
    )


_CONSTS = None


def host_core_inputs(core, inputs):
    global _CONSTS
    if _CONSTS is None:
        _CONSTS = host_consts()
    b, j = core // 4, core % 4
    s0 = j * CHUNK
    x = np.asarray(inputs["L0_attn_x"])
    lo, hi = s0 - 1554, s0 - 1554 + NXW
    pl, ph = max(0, -lo), max(0, hi - N)
    xw = np.zeros((NXW, D), dtype=np.float32)
    xw[pl:NXW - ph] = x[b, max(0, lo):min(N, hi)]
    cn = np.zeros((1, W_NEAR), dtype=np.float32)
    cf = np.zeros((1, W_FARM), dtype=np.float32)
    if j == 0:
        cn[0, 0:48] = NEG
        cf[0, 0:97] = NEG
    edge_l = np.full((128, 1), 0.0 if j == 0 else 1.0, dtype=np.float32)
    edge_r = np.full((128, 1), 0.0 if j == 3 else 1.0, dtype=np.float32)
    pb = np.concatenate([np.asarray(inputs["L0_attn_pos_bias"], dtype=np.float32),
                         np.full((1, H), NEG, dtype=np.float32)], axis=0)
    return dict(
        xw=xw, causal_n=cn, causal_f=cf, pb=pb, edge_l=edge_l, edge_r=edge_r,
        w_qkv=np.asarray(inputs["L0_attn_w_qkv"], dtype=np.float32),
        w_out=np.asarray(inputs["L0_attn_w_out"], dtype=np.float32),
        w_gate=np.asarray(inputs["L0_attn_w_gate"], dtype=np.float32),
        b_gate=np.ascontiguousarray(np.asarray(inputs["L0_attn_b_gate"], dtype=np.float32).reshape(8, 128).T),
        n1s=np.asarray(inputs["L0_attn_norm1_scale"], dtype=np.float32).reshape(1, D),
        n2s=np.asarray(inputs["L0_attn_norm2_scale"], dtype=np.float32).reshape(1, D),
        w_fg=np.asarray(inputs["L0_attn_w_ffn_gate"], dtype=np.float32),
        w_fu=np.asarray(inputs["L0_attn_w_ffn_up"], dtype=np.float32),
        w_fd=np.asarray(inputs["L0_attn_w_ffn_down"], dtype=np.float32),
        w1=np.asarray(inputs["L0_ffn_w1"], dtype=np.float32),
        w2=np.asarray(inputs["L0_ffn_w2"], dtype=np.float32),
        convw=np.ascontiguousarray(
            np.asarray(inputs["L0_ffn_conv_w"], dtype=np.float32)
            .reshape(32, 128, 3).transpose(1, 0, 2).reshape(128, 96)),
        convb=np.ascontiguousarray(
            np.asarray(inputs["L0_ffn_conv_b"], dtype=np.float32)
            .reshape(32, 128).T),
        **_CONSTS,
    )



class KB:
    def __init__(self, nc, debug=(), upto=None):
        self.nc = nc
        self.debug = set(debug)
        self.upto = upto
        self.io = {}
        self.sb = {}
        self._guards = []
        self.s = {k: Sem(nc, k) for k in
                  ["xld", "xldA", "xldB", "wld", "mm", "ev", "dv", "sc",
                   "sp", "spA", "spB", "gp", "ph"]}
        self.nbar = 0

    def sbuf(self, name, shape, dtype):
        g = self.nc.sbuf_tensor(name, shape, dtype)
        t = g.__enter__()
        self._guards.append(g)
        self._gmap = getattr(self, "_gmap", {})
        self._gmap[name] = g
        self.sb[name] = t
        return t

    def free(self, name):
        self._gmap[name].__exit__(None, None, None)
        del self.sb[name]

    def barrier(self, blk):
        AD.reset()
        self.nbar += 1
        tgt = self.nbar * 5
        ph = self.s["ph"]
        dma_waits = [(k, self.s[k].n) for k in
                     ["xld", "xldA", "xldB", "wld", "sp", "spA", "spB", "gp"]]
        for attr in ["sync", "tensor", "vector", "scalar", "gpsimd"]:
            @getattr(blk, attr)
            def _(eng, attr=attr):
                if attr == "sync":
                    for k, v in dma_waits:
                        eng.wait_ge(self.s[k].h, v)
                if attr == "gpsimd":
                    eng.sem_inc(ph.h, 1)
                else:
                    eng.drain().then_inc(ph.h, 1)
                eng.wait_ge(ph.h, tgt)
        ph.n = tgt


def declare_io(kb):
    nc = kb.nc

    def di(name, shape, dtype):
        kb.io[name] = nc.dram_tensor(name, shape, dtype, kind="ExternalInput").ap()

    def do(name, shape, dtype):
        kb.io[name] = nc.dram_tensor(name, shape, dtype, kind="ExternalOutput").ap()

    def ds(name, shape, dtype):
        kind = "ExternalOutput" if name in kb.debug else "Internal"
        kb.io[name] = nc.dram_tensor(name, shape, dtype, kind=kind).ap()

    di("xw", [NXW, D], F32)
    di("causal_n", [1, W_NEAR], F32)
    di("causal_f", [1, W_FARM], F32)
    di("edge_l", [128, 1], F32)
    di("edge_r", [128, 1], F32)
    di("pb", [45, H], F32)
    di("w_qkv", [D, 3 * D], F32)
    di("w_out", [D, D], F32)
    di("w_gate", [D, D], F32)
    di("b_gate", [128, 8], F32)
    di("n1s", [1, D], F32)
    di("n2s", [1, D], F32)
    di("w_fg", [D, DS], F32)
    di("w_fu", [D, DS], F32)
    di("w_fd", [DS, D], F32)
    di("w1", [DH, D], F32)
    di("w2", [D, DH], F32)
    di("convw", [128, 96], F32)
    di("convb", [128, 32], F32)
    di("ident", [128, 128], BF16)
    di("identf", [128, 128], F32)
    di("selq", [128, 2], BF16)
    di("selqf", [128, 2], F32)
    di("sel2f", [2, 128], F32)
    di("selh", [16, 8 * 128], F32)
    di("row1", [1, 128], BF16)
    di("row1f", [1, 128], F32)
    di("onescol", [128, 1], F32)
    di("ind_near", [45, 128 * W_NEAR], BF16)
    di("ind_far", [45, 128 * W_FARM], BF16)
    do("xout", [CHUNK, D], F32)
    ds("x1_scr", [NKV, D], F32)
    ds("k_scr", [128, 8 * NKV], BF16)
    ds("v_scr", [NKV, D], BF16)
    ds("q_scr", [128, 8 * NQ], BF16)
    ds("g_scr", [128, 8 * NQ], BF16)
    ds("ao_scr", [128, 8 * NQ], BF16)
    ds("x2_scr", [NQ, D], F32)
    ds("z0_scr", [NQ, D], F32)
    ds("gt_scr", [128, 22 * NQ], BF16)
    ds("x2b_scr", [NQ, D], F32)
    ds("mrow_scr", [H, 128 * (W_NEAR + W_FARM)], F32)
    ds("w1q_scr", [128, 8 * DH], FP8)
    ds("w2q_scr", [128, 32 * D], FP8)
    ds("dbg_h", [128, 32 * 130], F32)
    ds("dbg_ys", [128, 32 * 128], F32)
    ds("dbg_q2", [128, 32 * 128], F32)
    ds("dbg_q1", [128, 8 * 130], F32)
    ds("dbg_w1q", [128, 8 * DH], F32)
    ds("dbg_wsc", [128, 2], F32)

    sb = kb.sbuf
    sb("c_ident", [128, 128], BF16)
    sb("c_identf", [128, 128], F32)
    sb("c_row1", [1, 128], BF16)
    sb("c_row1f", [1, 128], F32)
    sb("c_ones", [128, 1], F32)
    sb("c_selq", [128, 2], BF16)
    sb("c_selqf", [128, 2], F32)
    sb("c_sel2f", [2, 128], F32)
    sb("c_selh", [16, 8 * 128], F32)
    sb("c_convw", [128, 32 * 3], F32)
    sb("c_convb", [128, 32], F32)
    sb("c_bg", [128, 8], F32)
    sb("c_edgel", [128, 1], F32)
    sb("c_edger", [128, 1], F32)
    sb("wsc", [128, 2], F32)
    sb("g1b", [128, D], BF16)
    sb("g2b", [128, D], BF16)



def quant_weight(kb, blk, src, nrow, ncol, dst, wsc_col, tag="qw"):
    nc, s, sb = kb.nc, kb.s, kb.sb
    rch, cch = nrow // 128, ncol // 128
    rps = max(1, 8192 // ncol)
    nst = (rch + rps - 1) // rps
    src3 = src.rearrange("(c p) d -> p c d", p=128)

    import contextlib
    with contextlib.ExitStack() as _es:
        ent = _es.enter_context
        stage = ent(nc.sbuf_tensor(tag + "_stage", [128, 8192], F32))
        dump = ent(nc.sbuf_tensor(tag + "_dump", [128, 8192], F32))
        wbf = ent(nc.sbuf_tensor(tag + "_bf", [128, 8192], BF16))
        acc = ent(nc.sbuf_tensor(tag + "_acc", [128, 8], F32))
        srow = ent(nc.sbuf_tensor(tag + "_srow", [1, 1], F32))
        pst = ent(nc.psum_tensor(tag + "_ps", [128, 1024], BF16))
        p1 = ent(nc.psum_tensor(tag + "_p1", [1, 1], F32))
        pbx = ent(nc.psum_tensor(tag + "_pb", [128, 1], F32))
        sc0, wld0 = s["sc"].n, s["wld"].n
        fence = {k: s[k].n for k in ["mm", "ev", "dv", "sc"]}

        @blk.sync
        def _(eng):
            for k, v in fence.items():
                eng.wait_ge(s[k].h, v)
            for st in range(nst):
                if st >= 1:
                    eng.wait_ge(s["sc"].h, sc0 + st)
                r0 = st * rps
                nr = min(rps, rch - r0)
                eng.dma_start(
                    out=stage.ap().rearrange("p (c d) -> p c d", c=rps)[:, 0:nr, 0:ncol]
                    if rps > 1 else stage[:, 0:ncol],
                    in_=src3[:, r0:r0 + nr, :]).then_inc(s["wld"].h, 16)
                s["wld"].n += 16

        @blk.scalar
        def _(eng):
            eng = AD(eng)
            for st in range(nst):
                eng.wait_ge(s["wld"].h, wld0 + 16 * (st + 1))
                nr = min(rps, rch - st * rps)
                eng.activation(dump[:, 0:nr * ncol], stage[:, 0:nr * ncol], AF.Abs,
                               accum_out=acc[:, st:st + 1]).then_inc(s["sc"].h, 1)
                s["sc"].n += 1

        @blk.vector
        def _(eng):
            eng = AD(eng)
            eng.wait_ge(s["sc"].h, sc0 + nst)
            for st in range(1, nst):
                eng.tensor_tensor(acc[:, 0:1], acc[:, 0:1], acc[:, st:st + 1], ALU.add)
            eng.drain().then_inc(s["dv"].h, 1)
            s["dv"].n += 1
        dv1 = s["dv"].n

        @blk.tensor
        def _(eng):
            eng.wait_ge(s["dv"].h, dv1)
            eng.matmul(p1[0:1, 0:1], sb["c_ones"][:, 0:1], acc[:, 0:1],
                       start=True, stop=True).then_inc(s["mm"].h, 1)
            s["mm"].n += 1
        mm1 = s["mm"].n

        @blk.vector
        def _(eng):
            eng = AD(eng)
            eng.wait_ge(s["mm"].h, mm1)
            eng.tensor_scalar(srow[0:1, 0:1], p1[0:1, 0:1], 1.0 / (nrow * ncol), 1e-5,
                              ALU.mult, ALU.max).then_inc(s["dv"].h, 1)
            s["dv"].n += 1
        dv2 = s["dv"].n

        @blk.tensor
        def _(eng):
            eng.wait_ge(s["dv"].h, dv2)
            eng.matmul(pbx[:, 0:1], sb["c_row1f"][0:1, :], srow[0:1, 0:1],
                       start=True, stop=True).then_inc(s["mm"].h, 1)
            s["mm"].n += 1
        mm2 = s["mm"].n

        @blk.vector
        def _(eng):
            eng = AD(eng)
            eng.wait_ge(s["mm"].h, mm2)
            eng.tensor_copy(sb["wsc"][:, wsc_col:wsc_col + 1], pbx[:, 0:1])
            eng.reciprocal(acc[:, 1:2], pbx[:, 0:1]).then_inc(s["dv"].h, 1)
            s["dv"].n += 1
        dv3 = s["dv"].n

        for st in range(nst):
            r0 = st * rps
            nr = min(rps, rch - r0)
            ev_pre = s["ev"].n

            @blk.sync
            def _(eng, st=st, r0=r0, nr=nr, ev_pre=ev_pre):
                eng.wait_ge(s["ev"].h, ev_pre)
                if st == 0:
                    eng.wait_ge(s["sc"].h, sc0 + nst)
                eng.dma_start(
                    out=stage.ap().rearrange("p (c d) -> p c d", c=rps)[:, 0:nr, 0:ncol]
                    if rps > 1 else stage[:, 0:ncol],
                    in_=src3[:, r0:r0 + nr, :]).then_inc(s["wld"].h, 16)
                s["wld"].n += 16
            wnow = s["wld"].n

            @blk.vector
            def _(eng, nr=nr, wnow=wnow, st=st):
                eng = AD(eng)
                eng.wait_ge(s["wld"].h, wnow)
                if st == 0:
                    eng.wait_ge(s["dv"].h, dv3)
                nel = nr * ncol
                eng.tensor_scalar(dump[:, 0:nel], stage[:, 0:nel], acc[:, 1:2],
                                  RMAGIC, ALU.mult, ALU.add)
                eng.tensor_scalar(dump[:, 0:nel], dump[:, 0:nel], RMAGIC, -1.0,
                                  ALU.subtract, ALU.max)
                eng.tensor_scalar(wbf[:, 0:nel], dump[:, 0:nel], 1.0, None,
                                  ALU.min).then_inc(s["dv"].h, 1)
                s["dv"].n += 1
            dnow = s["dv"].n
            evb = s["ev"].n

            @blk.tensor
            def _(eng, r0=r0, nr=nr, dnow=dnow, evb=evb):
                eng.wait_ge(s["dv"].h, dnow)
                for cc in range(cch):
                    if cc >= 1:
                        eng.wait_ge(s["ev"].h, evb + cc)
                    inst = None
                    for rl in range(nr):
                        inst = eng.transpose(pst[0:128, rl * 128:(rl + 1) * 128],
                                      wbf[:, rl * ncol + cc * 128:rl * ncol + cc * 128 + 128],
                                      sb["c_ident"][:, :])
                    inst.then_inc(s["mm"].h, 1)
                    s["mm"].n += 1
            mmb = s["mm"].n - cch

            @blk.vector
            def _(eng, r0=r0, nr=nr, mmb=mmb):
                eng = AD(eng)
                for cc in range(cch):
                    eng.wait_ge(s["mm"].h, mmb + cc + 1)
                    eng.tensor_copy(
                        AP(dst, cc * nrow + r0 * 128, [[dst.shape[1], 128], [1, nr * 128]]),
                        pst[:, 0:nr * 128]).then_inc(s["ev"].h, 1)
                    s["ev"].n += 1


def phase_p0(kb, blk):
    nc, s, io, sb = kb.nc, kb.s, kb.io, kb.sb
    kb.sbuf("w1q", [128, 8 * DH], FP8)
    kb.sbuf("w2q", [128, 32 * D], FP8)

    @blk.sync
    def _(eng):
        pairs = [
            (sb["c_ident"][:], io["ident"][:, :]),
            (sb["c_identf"][:], io["identf"][:, :]),
            (sb["c_row1"][:], io["row1"][:, :]),
            (sb["c_row1f"][:], io["row1f"][:, :]),
            (sb["c_ones"][:], io["onescol"][:, :]),
            (sb["c_selq"][:], io["selq"][:, :]),
            (sb["c_selqf"][:], io["selqf"][:, :]),
            (sb["c_sel2f"][:], io["sel2f"][:, :]),
            (sb["c_selh"][:], io["selh"][:, :]),
            (sb["c_convw"][:], io["convw"][:, :]),
            (sb["c_convb"][:], io["convb"][:, :]),
            (sb["c_bg"][:], io["b_gate"][:, :]),
            (sb["c_edgel"][:], io["edge_l"][:, :]),
            (sb["c_edger"][:], io["edge_r"][:, :]),
        ]
        for o_, i_ in pairs:
            eng.dma_start(out=o_, in_=i_).then_inc(s["xld"].h, 16)
            s["xld"].n += 16
    xc = s["xld"].n

    with (
        nc.sbuf_tensor("p0_grow", [1, D], BF16) as grow,
        nc.sbuf_tensor("p0_grow2", [1, D], BF16) as grow2,
        nc.psum_tensor("p0_gps", [128, 512], F32) as gps,
    ):
        @blk.gpsimd
        def _(eng):
            eng.wait_ge(s["gp"].h, s["gp"].n)
            eng.dma_start(out=grow[0:1, :], in_=io["n1s"][0:1, :]).then_inc(s["gp"].h, 16)
            s["gp"].n += 16
            eng.wait_ge(s["gp"].h, s["gp"].n)
            eng.dma_start(out=grow2[0:1, :], in_=io["n2s"][0:1, :]).then_inc(s["gp"].h, 16)
            s["gp"].n += 16
        gnow = s["gp"].n

        for gi, (rowt, dstn) in enumerate([(grow, "g1b"), (grow2, "g2b")]):
            evb = s["ev"].n

            @blk.tensor
            def _(eng, rowt=rowt, gnow=gnow, evb=evb):
                eng.wait_ge(s["gp"].h, gnow)
                eng.wait_ge(s["xld"].h, xc)
                eng.wait_ge(s["ev"].h, evb)
                for hf in range(2):
                    if hf == 1:
                        eng.wait_ge(s["ev"].h, evb + 1)
                    eng.matmul(gps[:, 0:512], sb["c_row1"][0:1, :],
                               rowt[0:1, hf * 512:hf * 512 + 512],
                               start=True, stop=True).then_inc(s["mm"].h, 1)
                    s["mm"].n += 1
            mmb = s["mm"].n - 2

            @blk.vector
            def _(eng, dstn=dstn, mmb=mmb):
                eng = AD(eng)
                for hf in range(2):
                    eng.wait_ge(s["mm"].h, mmb + hf + 1)
                    eng.tensor_copy(sb[dstn][:, hf * 512:hf * 512 + 512],
                                    gps[:, 0:512]).then_inc(s["ev"].h, 1)
                    s["ev"].n += 1

    quant_weight(kb, blk, io["w1"], DH, D, sb["w1q"], 0, tag="qw1")
    quant_weight(kb, blk, io["w2"], D, DH, sb["w2q"], 1, tag="qw2")
    if "dbg_w1q" in kb.debug:
        @blk.gpsimd
        def _(eng):
            eng.wait_ge(s["ev"].h, s["ev"].n)
            eng.dma_start(out=io["dbg_w1q"][:, :], in_=sb["w1q"][:, :]).then_inc(s["gp"].h, 16)
            eng.dma_start(out=io["dbg_wsc"][:, :], in_=sb["wsc"][:, :]).then_inc(s["gp"].h, 16)
            s["gp"].n += 32



GLEN = 128


def emit_bitffn(kb, blk, tag, ntile, groups, xin_dma, resid_dma, out_dma,
                glen_max=GLEN, edge_fix=None, pre_wait=None):
    nc, s, sb = kb.nc, kb.s, kb.sb
    GM = glen_max
    q1_cols = 128 * ntile
    import contextlib
    with contextlib.ExitStack() as _es:
        ent = _es.enter_context
        xs = ent(nc.sbuf_tensor(tag + "_xs", [128, 2 * D], F32))
        dmp = ent(nc.sbuf_tensor(tag + "_dmp", [128, D], F32))
        cl = ent(nc.sbuf_tensor(tag + "_cl", [128, 24], F32))
        xn = ent(nc.sbuf_tensor(tag + "_xn", [128, D], F32))
        q1b = ent(nc.sbuf_tensor(tag + "_q1b", [128, D], BF16))
        q1f = ent(nc.sbuf_tensor(tag + "_q1f", [128, 8 * q1_cols], I8))
        hsr = ent(nc.sbuf_tensor(tag + "_hsr", [1, q1_cols], F32))
        q1g = ent(nc.sbuf_tensor(tag + "_q1g", [128, 8 * (GM + 2)], BF16))
        hb = ent(nc.sbuf_tensor(tag + "_h", [128, 32 * (GM + 2)], BF16))
        hscb = ent(nc.sbuf_tensor(tag + "_hscb", [128, GM + 2], BF16))
        ca = ent(nc.sbuf_tensor(tag + "_ca", [128, 32 * GM], BF16))
        cb = ent(nc.sbuf_tensor(tag + "_cb", [128, 32 * GM], BF16))
        q2f = ent(nc.sbuf_tensor(tag + "_q2f", [128, 32 * GM], F32))
        mx = ent(nc.sbuf_tensor(tag + "_mx", [128, GM], F32))
        sr = ent(nc.sbuf_tensor(tag + "_sr", [1, GM], F32))
        scbuf = ent(nc.sbuf_tensor(tag + "_scb", [128, GM], BF16))
        dfm = ent(nc.sbuf_tensor(tag + "_dfm", [128, 8 * GM], BF16))
        xr = ent(nc.sbuf_tensor(tag + "_xr", [128, 2 * D], F32))
        xo = ent(nc.sbuf_tensor(tag + "_xo", [128, 2 * D], F32))
        psA = ent(nc.psum_tensor(tag + "_pA", [128, 1024], F32))
        psB = ent(nc.psum_tensor(tag + "_pB", [128, 1024], F32))
        psC = ent(nc.psum_tensor(tag + "_pC", [128, 1024], F32))
        psT = ent(nc.psum_tensor(tag + "_pT", [128, 1024], BF16))
        psM = ent(nc.psum_tensor(tag + "_pM", [128, 512], F32))
        xld0, dv0, sc0, mm0, ev0 = (s[k].n for k in ["xld", "dv", "sc", "mm", "ev"])

        xA0, xB0 = s["xldA"].n, s["xldB"].n

        @blk.sync
        def _(eng):
            for i in range(ntile):
                if i >= 2:
                    eng.wait_ge(s["dv"].h, dv0 + 2 * (i - 2) + 2)
                sem = s["xldA"] if i % 2 == 0 else s["xldB"]
                xin_dma(eng, i, xs[:, (i % 2) * D:(i % 2) * D + D], sem)
                sem.n += 16

        def evac_tile(eng, j):
            eng.tensor_copy(
                AP(q1f, j * 128, [[8 * q1_cols, 128], [q1_cols, 8], [1, 128]]),
                psT.ap().rearrange("p (c t) -> p c t", c=8))
            eng.tensor_copy(hsr[0:1, j * 128:(j + 1) * 128],
                            psM[0:1, (j % 2) * 128:(j % 2) * 128 + 128]).then_inc(s["ev"].h, 1)
            s["ev"].n += 1

        @blk.vector
        def _(eng):
            eng = AD(eng)
            for i in range(ntile):
                if i % 2 == 0:
                    eng.wait_ge(s["xldA"].h, xA0 + 16 * (i // 2 + 1))
                else:
                    eng.wait_ge(s["xldB"].h, xB0 + 16 * (i // 2 + 1))
                xt = xs[:, (i % 2) * D:(i % 2) * D + D]
                eng.scalar_tensor_tensor(dmp[:, :], xt, 1.0, xt, ALU.bypass,
                                         ALU.mult, accum_out=cl[:, 0:1])
                eng.tensor_scalar(dmp[:, :], xt, 1.0, 0.0, ALU.mult, ALU.add,
                                  accum_out=cl[:, 1:2])
                eng.tensor_scalar(cl[:, 2:3], cl[:, 1:2], 1.0 / D, None, ALU.mult)
                eng.tensor_tensor(cl[:, 3:4], cl[:, 2:3], cl[:, 2:3], ALU.mult)
                eng.tensor_scalar(cl[:, 4:5], cl[:, 0:1], 1.0 / D, None, ALU.mult)
                eng.tensor_tensor(cl[:, 4:5], cl[:, 4:5], cl[:, 3:4], ALU.subtract)
                eng.tensor_scalar(cl[:, 4:5], cl[:, 4:5], LN_EPS, None,
                                  ALU.add).then_inc(s["dv"].h, 1)
                s["dv"].n += 1
                eng.wait_ge(s["sc"].h, sc0 + i + 1)
                eng.wait_ge(s["mm"].h, mm0 + i)
                if i >= 1:
                    evac_tile(eng, i - 1)
                eng.reciprocal(cl[:, 6:7], cl[:, 5:6])
                eng.tensor_scalar(xn[:, :], xt, cl[:, 2:3], cl[:, 6:7],
                                  ALU.subtract, ALU.mult)
                eng.reduce_max(cl[:, 8:9], xn[:, :], axis=AXX,
                               apply_absolute_value=True)
                eng.tensor_scalar(cl[:, 9:10], cl[:, 8:9], 1e-5, None, ALU.max)
                eng.reciprocal(cl[:, 10:11], cl[:, 9:10])
                eng.tensor_scalar(cl[:, 10:11], cl[:, 10:11], 127.0, None, ALU.mult)
                eng.tensor_scalar(dmp[:, :], xn[:, :], cl[:, 10:11], RMAGIC,
                                  ALU.mult, ALU.add)
                eng.tensor_scalar(dmp[:, :], dmp[:, :], RMAGIC, -128.0,
                                  ALU.subtract, ALU.max)
                eng.tensor_scalar(q1b[:, :], dmp[:, :], 127.0, None, ALU.min)
                eng.scalar_tensor_tensor(cl[:, 11:12], cl[:, 9:10], 1.0 / 127.0,
                                         sb["wsc"][:, 0:1], ALU.mult,
                                         ALU.mult).then_inc(s["dv"].h, 1)
                s["dv"].n += 1
            eng.wait_ge(s["mm"].h, mm0 + ntile)
            evac_tile(eng, ntile - 1)

        @blk.scalar
        def _(eng):
            eng = AD(eng)
            for i in range(ntile):
                eng.wait_ge(s["dv"].h, dv0 + 2 * i + 1)
                eng.activation(cl[:, 5:6], cl[:, 4:5], AF.Sqrt).then_inc(s["sc"].h, 1)
                s["sc"].n += 1

        @blk.tensor
        def _(eng):
            for i in range(ntile):
                eng.wait_ge(s["dv"].h, dv0 + 2 * i + 2)
                for ch in range(8):
                    eng.transpose(psT[:, ch * 128:(ch + 1) * 128],
                                  q1b[:, ch * 128:(ch + 1) * 128], sb["c_ident"][:, :])
                eng.transpose(psM[0:1, (i % 2) * 128:(i % 2) * 128 + 128],
                              cl[:, 11:12], sb["c_identf"][:, :]).then_inc(s["mm"].h, 1)
                s["mm"].n += 1
        passA_ev = s["ev"].n

        prev_xo_dv = 0
        for (g0, glen) in groups:
            wlen = glen + 2
            nsl = (glen + 127) // 128
            sp_g0 = s["sp"].n

            dbg_wait = getattr(kb, "_dbg_wait_sp", 0)

            @blk.vector
            def _(eng, g0=g0, wlen=wlen, dbg_wait=dbg_wait):
                eng = AD(eng)
                eng.wait_ge(s["ev"].h, passA_ev)
                if pre_wait:
                    for semk, cnt in pre_wait:
                        eng.wait_ge(s[semk].h, cnt)
                if dbg_wait:
                    eng.wait_ge(s["gp"].h, dbg_wait)
                eng.tensor_copy(
                    AP(q1g, 0, [[8 * (GM + 2), 128], [GM + 2, 8], [1, wlen]]),
                    AP(q1f, g0 - 1, [[8 * q1_cols, 128], [q1_cols, 8], [1, wlen]]),
                ).then_inc(s["dv"].h, 1)
                s["dv"].n += 1
            dv_cast = s["dv"].n
            evb = s["ev"].n

            @blk.tensor
            def _(eng, g0=g0, wlen=wlen, dv_cast=dv_cast, evb=evb):
                eng.wait_ge(s["dv"].h, dv_cast)
                for oc in range(32):
                    slot = oc % 4
                    pst_ = (psA, psB)[slot // 2]
                    pcol = (slot % 2) * 512
                    if oc >= 4:
                        eng.wait_ge(s["ev"].h, evb + (oc - 4) + 1)
                    inst = None
                    for kc in range(8):
                        inst = eng.matmul(pst_[:, pcol:pcol + wlen],
                                   sb["w1q"][:, kc * DH + oc * 128:kc * DH + oc * 128 + 128],
                                   q1g[:, kc * (GM + 2):kc * (GM + 2) + wlen],
                                   start=(kc == 0), stop=(kc == 7))
                    inst.then_inc(s["mm"].h, 1)
                    s["mm"].n += 1
                eng.matmul(psM[:, 0:wlen], sb["c_row1f"][0:1, :],
                           hsr[0:1, g0 - 1:g0 - 1 + wlen], start=True,
                           stop=True).then_inc(s["mm"].h, 1)
                s["mm"].n += 1
            mmb = s["mm"].n - 33

            @blk.scalar
            def _(eng, wlen=wlen, mmb=mmb):
                eng = AD(eng)
                for oc in range(32):
                    slot = oc % 4
                    pst_ = (psA, psB)[slot // 2]
                    pcol = (slot % 2) * 512
                    eng.wait_ge(s["mm"].h, mmb + oc + 1)
                    eng.copy(hb[:, oc * wlen:(oc + 1) * wlen],
                             pst_[:, pcol:pcol + wlen]).then_inc(s["ev"].h, 1)
                    s["ev"].n += 1
                eng.wait_ge(s["mm"].h, mmb + 33)
                eng.copy(hscb[:, 0:wlen], psM[:, 0:wlen]).then_inc(s["sc"].h, 1)
                s["sc"].n += 1
            ev_h = s["ev"].n
            sc_hsc = s["sc"].n

            @blk.vector
            def _(eng, g0=g0, glen=glen, wlen=wlen, ev_h=ev_h, sc_hsc=sc_hsc):
                eng = AD(eng)
                eng.wait_ge(s["ev"].h, ev_h)
                eng.wait_ge(s["sc"].h, sc_hsc)
                hfull = AP(hb, 0, [[32 * (GM + 2), 128], [wlen, 32], [1, wlen]])
                hscv = AP(hscb, 0, [[GM + 2, 128], [0, 32], [1, wlen]])
                eng.tensor_tensor(hfull, hfull, hscv, ALU.mult)
                if edge_fix:
                    for eg0, ecol, eap in edge_fix:
                        if eg0 == g0:
                            hcol = AP(hb, ecol, [[32 * (GM + 2), 128], [wlen, 32], [0, 1]])
                            eng.tensor_scalar(hcol, hcol, eap, None, ALU.mult)
                hv = lambda off: AP(hb, off, [[32 * (GM + 2), 128], [wlen, 32], [1, glen]])
                av = AP(ca, 0, [[32 * GM, 128], [glen, 32], [1, glen]])
                bv = AP(cb, 0, [[32 * GM, 128], [glen, 32], [1, glen]])
                wt = lambda t: AP(sb["c_convw"], t, [[96, 128], [3, 32], [0, glen]])
                bs = AP(sb["c_convb"], 0, [[32, 128], [1, 32], [0, glen]])
                eng.tensor_tensor(av, hv(1), wt(1), ALU.mult)
                eng.tensor_tensor(bv, hv(0), wt(0), ALU.mult)
                eng.tensor_tensor(av, av, bv, ALU.add)
                eng.tensor_tensor(bv, hv(2), wt(2), ALU.mult)
                eng.tensor_tensor(av, av, bv, ALU.add)
                eng.tensor_tensor(av, av, bs, ALU.add).then_inc(s["dv"].h, 1)
                s["dv"].n += 1
            dv_conv = s["dv"].n

            @blk.scalar
            def _(eng, glen=glen, dv_conv=dv_conv):
                eng = AD(eng)
                eng.wait_ge(s["dv"].h, dv_conv)
                eng.activation(cb[:, 0:32 * glen], ca[:, 0:32 * glen],
                               AF.Silu).then_inc(s["sc"].h, 1)
                s["sc"].n += 1
            sc_silu = s["sc"].n

            @blk.vector
            def _(eng, glen=glen, sc_silu=sc_silu):
                eng = AD(eng)
                eng.wait_ge(s["sc"].h, sc_silu)
                bv3 = AP(cb, 0, [[32 * GM, 128], [1, glen], [glen, 32]])
                eng.tensor_reduce(mx[:, 0:glen], bv3, axis=AXX, op=ALU.max,
                                  apply_absolute_value=True).then_inc(s["dv"].h, 1)
                s["dv"].n += 1
            dv_max = s["dv"].n

            @blk.tensor
            def _(eng, glen=glen, nsl=nsl, dv_max=dv_max):
                eng.wait_ge(s["dv"].h, dv_max)
                inst = None
                for sl in range(nsl):
                    w = min(128, glen - sl * 128)
                    inst = eng.transpose(psM[0:w, sl * 128:sl * 128 + 128],
                                  mx[:, sl * 128:sl * 128 + w], sb["c_identf"][:, :])
                inst.then_inc(s["mm"].h, 1)
                s["mm"].n += 1
            mm_t1 = s["mm"].n

            @blk.vector
            def _(eng, glen=glen, nsl=nsl, mm_t1=mm_t1):
                eng = AD(eng)
                eng.wait_ge(s["mm"].h, mm_t1)
                for sl in range(nsl):
                    w = min(128, glen - sl * 128)
                    eng.reduce_max(cl[0:w, 12 + 2 * sl:13 + 2 * sl],
                                   psM[0:w, sl * 128:sl * 128 + 128], axis=AXX)
                    eng.tensor_scalar(cl[0:w, 12 + 2 * sl:13 + 2 * sl],
                                      cl[0:w, 12 + 2 * sl:13 + 2 * sl], 1e-5, None,
                                      ALU.max)
                    eng.scalar_tensor_tensor(cl[0:w, 13 + 2 * sl:14 + 2 * sl],
                                             cl[0:w, 12 + 2 * sl:13 + 2 * sl],
                                             0.5 / 127.0, sb["wsc"][0:w, 1:2],
                                             ALU.mult, ALU.mult)
                    eng.reciprocal(cl[0:w, 16 + sl:17 + sl],
                                   cl[0:w, 12 + 2 * sl:13 + 2 * sl])
                    eng.tensor_scalar(cl[0:w, 16 + sl:17 + sl],
                                      cl[0:w, 16 + sl:17 + sl], 127.0, None,
                                      ALU.mult)
                eng.drain().then_inc(s["dv"].h, 1)
                s["dv"].n += 1
            dv_cols = s["dv"].n

            @blk.tensor
            def _(eng, glen=glen, nsl=nsl, dv_cols=dv_cols):
                eng.wait_ge(s["dv"].h, dv_cols)
                inst = None
                for sl in range(nsl):
                    w = min(128, glen - sl * 128)
                    inst = eng.transpose(psM[0:1, 256 + sl * 128:256 + sl * 128 + w],
                                  cl[0:w, 16 + sl:17 + sl], sb["c_identf"][0:w, 0:w])
                inst.then_inc(s["mm"].h, 1)
                s["mm"].n += 1
            mm_t2 = s["mm"].n

            @blk.vector
            def _(eng, glen=glen, mm_t2=mm_t2):
                eng = AD(eng)
                eng.wait_ge(s["mm"].h, mm_t2)
                eng.tensor_copy(sr[0:1, 0:glen],
                                psM[0:1, 256:256 + glen]).then_inc(s["dv"].h, 1)
                s["dv"].n += 1
            dv_row = s["dv"].n

            @blk.tensor
            def _(eng, glen=glen, nsl=nsl, dv_row=dv_row):
                eng.wait_ge(s["dv"].h, dv_row)
                inst = None
                for sl in range(nsl):
                    w = min(128, glen - sl * 128)
                    inst = eng.matmul(psM[:, sl * 128:sl * 128 + w], sb["c_row1f"][0:1, :],
                               sr[0:1, sl * 128:sl * 128 + w], start=True, stop=True)
                inst.then_inc(s["mm"].h, 1)
                s["mm"].n += 1
            mm_bc = s["mm"].n

            @blk.scalar
            def _(eng, glen=glen, mm_bc=mm_bc):
                eng = AD(eng)
                eng.wait_ge(s["mm"].h, mm_bc)
                eng.copy(scbuf[:, 0:glen], psM[:, 0:glen]).then_inc(s["sc"].h, 1)
                s["sc"].n += 1
            sc_bc = s["sc"].n

            @blk.vector
            def _(eng, glen=glen, sc_bc=sc_bc):
                eng = AD(eng)
                eng.wait_ge(s["sc"].h, sc_bc)
                scv = AP(scbuf, 0, [[GM, 128], [0, 32], [1, glen]])
                bv = AP(cb, 0, [[32 * GM, 128], [glen, 32], [1, glen]])
                qv = AP(q2f, 0, [[32 * GM, 128], [glen, 32], [1, glen]])
                av = AP(ca, 0, [[32 * GM, 128], [glen, 32], [1, glen]])
                eng.tensor_tensor(qv, bv, scv, ALU.mult)
                eng.tensor_scalar(qv, qv, RMAGIC, RMAGIC, ALU.add, ALU.subtract)
                eng.tensor_scalar(av, qv, -128.0, 127.0, ALU.max,
                                  ALU.min).then_inc(s["dv"].h, 1)
                s["dv"].n += 1
            dv_q2 = s["dv"].n
            ev_b2 = s["ev"].n
            if "dbg_h" in kb.debug and g0 == groups[min(13, len(groups) - 1)][0]:
                @blk.gpsimd
                def _(eng, wlen=wlen, glen=glen, dv_q2=dv_q2):
                    eng.wait_ge(s["dv"].h, dv_q2)
                    eng.dma_start(out=kb.io["dbg_h"][:, 0:32 * wlen],
                                  in_=hb[:, 0:32 * wlen]).then_inc(s["gp"].h, 16)
                    eng.dma_start(out=kb.io["dbg_ys"][:, 0:32 * glen],
                                  in_=cb[:, 0:32 * glen]).then_inc(s["gp"].h, 16)
                    eng.dma_start(out=kb.io["dbg_q2"][:, 0:32 * glen],
                                  in_=ca[:, 0:32 * glen]).then_inc(s["gp"].h, 16)
                    eng.dma_start(out=kb.io["dbg_q1"][:, 0:8 * wlen],
                                  in_=q1g[:, 0:8 * wlen]).then_inc(s["gp"].h, 16)
                    s["gp"].n += 64
                kb._dbg_wait_sp = s["gp"].n

            @blk.tensor
            def _(eng, glen=glen, dv_q2=dv_q2, ev_b2=ev_b2):
                eng.wait_ge(s["dv"].h, dv_q2)
                for oc in range(8):
                    slot = oc % 2
                    if oc >= 2:
                        eng.wait_ge(s["ev"].h, ev_b2 + (oc - 2) + 1)
                    inst = None
                    for kc in range(32):
                        inst = eng.matmul(psC[:, slot * 512:slot * 512 + glen],
                                   sb["w2q"][:, kc * D + oc * 128:kc * D + oc * 128 + 128],
                                   ca[:, kc * glen:(kc + 1) * glen],
                                   start=(kc == 0), stop=(kc == 31))
                    inst.then_inc(s["mm"].h, 1)
                    s["mm"].n += 1
            mm_b2 = s["mm"].n - 8

            @blk.vector
            def _(eng, glen=glen, mm_b2=mm_b2):
                eng = AD(eng)
                for oc in range(8):
                    slot = oc % 2
                    eng.wait_ge(s["mm"].h, mm_b2 + oc + 1)
                    eng.tensor_copy(dfm[:, oc * glen:(oc + 1) * glen],
                                    psC[:, slot * 512:slot * 512 + glen]).then_inc(s["ev"].h, 1)
                    s["ev"].n += 1
            ev_dfm = s["ev"].n

            @blk.sync
            def _(eng, g0=g0, glen=glen, nsl=nsl, prev_xo_dv=prev_xo_dv):
                eng.wait_ge(s["dv"].h, prev_xo_dv)
                for sl in range(nsl):
                    w = min(128, glen - sl * 128)
                    resid_dma(eng, g0 + sl * 128, w, xr[:, sl * D:sl * D + D])
                    s["xld"].n += 16
            xld_r = s["xld"].n

            @blk.tensor
            def _(eng, glen=glen, nsl=nsl, ev_dfm=ev_dfm, dv_q2=dv_q2):
                eng.wait_ge(s["ev"].h, ev_dfm)
                for sl in range(nsl):
                    w = min(128, glen - sl * 128)
                    if sl >= 1:
                        eng.wait_ge(s["dv"].h, dv_q2 + sl)
                    inst = None
                    for ch in range(8):
                        inst = eng.transpose(psT[0:w, ch * 128:(ch + 1) * 128],
                                      dfm[:, ch * glen + sl * 128:ch * glen + sl * 128 + w],
                                      sb["c_ident"][:, :])
                    inst.then_inc(s["mm"].h, 1)
                    s["mm"].n += 1
            mm_dT = s["mm"].n - nsl

            @blk.vector
            def _(eng, glen=glen, nsl=nsl, mm_dT=mm_dT, xld_r=xld_r, sp_g0=sp_g0):
                eng = AD(eng)
                for sl in range(nsl):
                    w = min(128, glen - sl * 128)
                    eng.wait_ge(s["mm"].h, mm_dT + sl + 1)
                    eng.wait_ge(s["xld"].h, xld_r - 16 * (nsl - 1 - sl))
                    eng.wait_ge(s["sp"].h, sp_g0)
                    eng.scalar_tensor_tensor(xo[0:w, sl * D:sl * D + D],
                                             psT[0:w, 0:1024],
                                             cl[0:w, 13 + 2 * sl:14 + 2 * sl],
                                             xr[0:w, sl * D:sl * D + D],
                                             ALU.mult, ALU.add).then_inc(s["dv"].h, 1)
                    s["dv"].n += 1
            dv_out = s["dv"].n - nsl
            prev_xo_dv = s["dv"].n

            @blk.sync
            def _(eng, g0=g0, glen=glen, nsl=nsl, dv_out=dv_out):
                for sl in range(nsl):
                    w = min(128, glen - sl * 128)
                    eng.wait_ge(s["dv"].h, dv_out + sl + 1)
                    out_dma(eng, g0 + sl * 128, w, xo[:, sl * D:sl * D + D])
                    s["sp"].n += 16

        spf = s["sp"].n

        @blk.sync
        def _(eng):
            eng.wait_ge(s["sp"].h, spf)



def phase_s1(kb, blk):
    s, io = kb.s, kb.io
    groups = []
    t = 0
    while t < NKV:
        groups.append((t + XKV, min(GLEN, NKV - t)))
        t += GLEN

    def xin(eng, i, dst, sem):
        eng.dma_start(out=dst, in_=io["xw"][i * 128:(i + 1) * 128, :]).then_inc(sem.h, 16)

    def resid(eng, tok0, w, dst):
        eng.dma_start(out=dst[0:w, :], in_=io["xw"][tok0:tok0 + w, :]).then_inc(s["xld"].h, 16)

    def outd(eng, tok0, w, src):
        eng.dma_start(out=io["x1_scr"][tok0 - XKV:tok0 - XKV + w, :],
                      in_=src[0:w, :]).then_inc(s["sp"].h, 16)

    emit_bitffn(kb, blk, "s1", 29, groups, xin, resid, outd)

    @blk.sync
    def _(eng):
        eng.dma_start(out=io["w1q_scr"][:, :], in_=kb.sb["w1q"][:, :]).then_inc(s["wld"].h, 16)
        eng.dma_start(out=io["w2q_scr"][:, :], in_=kb.sb["w2q"][:, :]).then_inc(s["wld"].h, 16)
        s["wld"].n += 32




def phase_s2ab(kb, blk):
    nc, s, sb, io = kb.nc, kb.s, kb.sb, kb.io
    import contextlib
    with contextlib.ExitStack() as _es:
        ent = _es.enter_context
        wqkv = ent(nc.sbuf_tensor("ab_wqkv", [128, 8 * 3072], BF16))
        wg = ent(nc.sbuf_tensor("ab_wg", [128, 8 * 1024], BF16))
        xs = ent(nc.sbuf_tensor("ab_xs", [128, 2 * D], F32))
        cl = ent(nc.sbuf_tensor("ab_cl", [128, 16], F32))
        dmp = ent(nc.sbuf_tensor("ab_dmp", [128, D], F32))
        xnt = ent(nc.sbuf_tensor("ab_xnt", [128, D], BF16))
        z0t = ent(nc.sbuf_tensor("ab_z0t", [128, 2 * D], F32))
        xnfm = ent(nc.sbuf_tensor("ab_xnfm", [128, 8 * 512], BF16))
        kst = ent(nc.sbuf_tensor("ab_kst", [128, 2 * 512], BF16))
        qst = ent(nc.sbuf_tensor("ab_qst", [128, 2 * 512], BF16))
        gst = ent(nc.sbuf_tensor("ab_gst", [128, 2 * 512], BF16))
        vst = ent(nc.sbuf_tensor("ab_vst", [128, 1024], BF16))
        psT = ent(nc.psum_tensor("ab_pT", [128, 1024], BF16))
        psP = ent(nc.psum_tensor("ab_pP", [128, 1024], F32))
        psG = ent(nc.psum_tensor("ab_pG", [128, 512], F32))
        psV = ent(nc.psum_tensor("ab_pV", [128, 1024], F32))

        gp0 = s["gp"].n

        @blk.gpsimd
        def _(eng):
            eng.wait_ge(s["gp"].h, s["gp"].n)
            eng.dma_start(
                out=wqkv.ap().rearrange("p (c d) -> p c d", c=8),
                in_=io["w_qkv"].rearrange("(c p) d -> p c d", p=128)).then_inc(s["gp"].h, 16)
            s["gp"].n += 16
            eng.wait_ge(s["gp"].h, s["gp"].n)
            eng.dma_start(
                out=wg.ap().rearrange("p (c d) -> p c d", c=8),
                in_=io["w_gate"].rearrange("(c p) d -> p c d", p=128)).then_inc(s["gp"].h, 16)
            s["gp"].n += 16

        groups = []
        t = 0
        while t < NKV:
            groups.append((t, min(512, NKV - t)))
            t += 512

        for a0, alen in groups:
            ntl = (alen + 127) // 128
            do_q = a0 + alen > KVQ
            nKQ = 16 if do_q else 8
            xA0, xB0 = s["xldA"].n, s["xldB"].n
            dv0, sc0, mm0, ev0 = (s[k].n for k in ["dv", "sc", "mm", "ev"])

            zplan = [None] * ntl
            for i in range(ntl):
                r0 = a0 + i * 128
                if do_q and r0 + 128 > KVQ:
                    gidx = getattr(kb, "_spill_gidx", 0)
                    p = "spA" if gidx % 2 == 0 else "spB"
                    kb._spill_gidx = gidx + 1
                    zplan[i] = (p, s[p].n + 16)
                    s[p].n += 16

            @blk.sync
            def _(eng, a0=a0, ntl=ntl, dv0=dv0):
                for i in range(ntl):
                    if i >= 2:
                        eng.wait_ge(s["dv"].h, dv0 + 3 * (i - 2) + 3)
                    sem = s["xldA"] if i % 2 == 0 else s["xldB"]
                    r0 = a0 + i * 128
                    w = min(128, NKV - r0)
                    eng.dma_start(out=xs[0:w, (i % 2) * D:(i % 2) * D + D],
                                  in_=io["x1_scr"][r0:r0 + w, :]).then_inc(sem.h, 16)
                    sem.n += 16

            def evac_fm(eng, j):
                eng.tensor_copy(
                    AP(xnfm, j * 128, [[8 * 512, 128], [512, 8], [1, 128]]),
                    psT.ap().rearrange("p (c t) -> p c t", c=8)).then_inc(s["ev"].h, 1)
                s["ev"].n += 1

            @blk.vector
            def _(eng, a0=a0, ntl=ntl, xA0=xA0, xB0=xB0, sc0=sc0, mm0=mm0,
                  zplan=zplan):
                eng = AD(eng)
                for i in range(ntl):
                    if i % 2 == 0:
                        eng.wait_ge(s["xldA"].h, xA0 + 16 * (i // 2 + 1))
                    else:
                        eng.wait_ge(s["xldB"].h, xB0 + 16 * (i // 2 + 1))
                    xt = xs[:, (i % 2) * D:(i % 2) * D + D]
                    eng.scalar_tensor_tensor(dmp[:, :], xt, 1.0, xt, ALU.bypass,
                                             ALU.mult, accum_out=cl[:, 0:1])
                    eng.tensor_scalar(dmp[:, :], xt, 1.0, 0.0, ALU.mult, ALU.add,
                                      accum_out=cl[:, 5:6])
                    eng.tensor_scalar(cl[:, 6:7], cl[:, 5:6], 1.0 / D, None, ALU.mult)
                    eng.tensor_tensor(cl[:, 7:8], cl[:, 6:7], cl[:, 6:7], ALU.mult)
                    eng.tensor_scalar(cl[:, 1:2], cl[:, 0:1], 1.0 / D, None, ALU.mult)
                    eng.tensor_tensor(cl[:, 1:2], cl[:, 1:2], cl[:, 7:8], ALU.subtract)
                    eng.tensor_scalar(cl[:, 1:2], cl[:, 1:2], LN_EPS, None,
                                      ALU.add).then_inc(s["dv"].h, 1)
                    s["dv"].n += 1
                    eng.wait_ge(s["sc"].h, sc0 + 2 * i + 1)
                    eng.wait_ge(s["mm"].h, mm0 + i)
                    if i >= 1:
                        evac_fm(eng, i - 1)
                    eng.reciprocal(cl[:, 3:4], cl[:, 2:3])
                    zt = z0t[:, (i % 2) * D:(i % 2) * D + D]
                    if i >= 2 and zplan[i - 2] is not None:
                        p_, tgt_ = zplan[i - 2]
                        eng.wait_ge(s[p_].h, tgt_)
                    eng.tensor_scalar(zt, xt, cl[:, 6:7], cl[:, 3:4],
                                      ALU.subtract, ALU.mult)
                    eng.scalar_tensor_tensor(dmp[:, :], zt, 1.0, zt,
                                             ALU.bypass, ALU.mult,
                                             accum_out=cl[:, 8:9])
                    eng.tensor_scalar(cl[:, 9:10], cl[:, 8:9], 1.0 / D, RMS_EPS,
                                      ALU.mult, ALU.add).then_inc(s["dv"].h, 1)
                    s["dv"].n += 1
                    eng.wait_ge(s["sc"].h, sc0 + 2 * i + 2)
                    eng.reciprocal(cl[:, 11:12], cl[:, 10:11])
                    eng.tensor_scalar(dmp[:, :], zt, cl[:, 11:12], None,
                                      ALU.mult)
                    eng.tensor_tensor(xnt[:, :], dmp[:, :], sb["g1b"][:, :],
                                      ALU.mult).then_inc(s["dv"].h, 1)
                    s["dv"].n += 1
                eng.wait_ge(s["mm"].h, mm0 + ntl)
                evac_fm(eng, ntl - 1)

            @blk.scalar
            def _(eng, ntl=ntl, dv0=dv0):
                eng = AD(eng)
                for i in range(ntl):
                    eng.wait_ge(s["dv"].h, dv0 + 3 * i + 1)
                    eng.activation(cl[:, 2:3], cl[:, 1:2], AF.Sqrt).then_inc(s["sc"].h, 1)
                    s["sc"].n += 1
                    eng.wait_ge(s["dv"].h, dv0 + 3 * i + 2)
                    eng.activation(cl[:, 10:11], cl[:, 9:10], AF.Sqrt).then_inc(s["sc"].h, 1)
                    s["sc"].n += 1

            @blk.tensor
            def _(eng, ntl=ntl, dv0=dv0, ev0=ev0):
                for i in range(ntl):
                    eng.wait_ge(s["dv"].h, dv0 + 3 * i + 3)
                    if i >= 1:
                        eng.wait_ge(s["ev"].h, ev0 + i)
                    inst = None
                    for ch in range(8):
                        inst = eng.transpose(psT[:, ch * 128:(ch + 1) * 128],
                                             xnt[:, ch * 128:(ch + 1) * 128],
                                             sb["c_ident"][:, :])
                    inst.then_inc(s["mm"].h, 1)
                    s["mm"].n += 1

            @blk.sync
            def _(eng, a0=a0, ntl=ntl, dv0=dv0, zplan=zplan):
                for i in range(ntl):
                    if zplan[i] is None:
                        continue
                    r0 = a0 + i * 128
                    lo = max(r0, KVQ)
                    w = min(r0 + 128, NKV) - lo
                    p_, tgt_ = zplan[i]
                    eng.wait_ge(s["dv"].h, dv0 + 3 * i + 3)
                    eng.wait_ge(s[p_].h, tgt_ - 16)
                    eng.dma_start(
                        out=io["z0_scr"][lo - KVQ:lo - KVQ + w, :],
                        in_=z0t[lo - r0:lo - r0 + w, (i % 2) * D:(i % 2) * D + D]
                    ).then_inc(s[p_].h, 16)

            evF = s["ev"].n
            scG0 = s["sc"].n
            mmP = s["mm"].n
            nG = 8 if do_q else 0
            nsp_g = nKQ + nG
            nspills = nsp_g + ntl
            plan = []
            pc = {"spA": s["spA"].n, "spB": s["spB"].n}
            for j in range(nspills):
                p = "spA" if (s["spA"].n + s["spB"].n) // 16 % 2 == 0 else "spB"
                pass
            gidx0 = getattr(kb, "_spill_gidx", 0)
            for j in range(nspills):
                p = "spA" if (gidx0 + j) % 2 == 0 else "spB"
                pc[p] += 16
                plan.append((p, pc[p]))
            kb._spill_gidx = gidx0 + nspills

            @blk.tensor
            def _(eng, a0=a0, alen=alen, ntl=ntl, evF=evF, do_q=do_q,
                  scG0=scG0, nKQ=nKQ):
                eng.wait_ge(s["ev"].h, evF)
                eng.wait_ge(s["gp"].h, gp0 + 32)
                nmm = 0
                for pj in range(2 if do_q else 1):
                    col0 = 1024 if pj == 0 else 0
                    for oc in range(8):
                        slot = nmm % 2
                        if nmm >= 2:
                            eng.wait_ge(s["ev"].h, evF + nmm - 1)
                        inst = None
                        for kc in range(8):
                            inst = eng.matmul(
                                psP[:, slot * 512:slot * 512 + alen],
                                wqkv[:, kc * 3072 + col0 + oc * 128:kc * 3072 + col0 + oc * 128 + 128],
                                xnfm[:, kc * 512:kc * 512 + alen],
                                start=(kc == 0), stop=(kc == 7))
                        inst.then_inc(s["mm"].h, 1)
                        s["mm"].n += 1
                        nmm += 1
                if do_q:
                    for oc in range(8):
                        if oc >= 1:
                            eng.wait_ge(s["sc"].h, scG0 + oc)
                        inst = None
                        for kc in range(8):
                            inst = eng.matmul(
                                psG[:, 0:alen],
                                wg[:, kc * 1024 + oc * 128:kc * 1024 + oc * 128 + 128],
                                xnfm[:, kc * 512:kc * 512 + alen],
                                start=(kc == 0), stop=(kc == 7))
                        inst.then_inc(s["mm"].h, 1)
                        s["mm"].n += 1
                for ts_ in range(ntl):
                    tw = min(128, alen - ts_ * 128)
                    for half in range(2):
                        vi = 2 * ts_ + half
                        if ts_ >= 1:
                            eng.wait_ge(s["ev"].h, evF + nKQ + 2 * (ts_ - 1) + half + 1)
                        inst = None
                        for kc in range(8):
                            inst = eng.matmul(
                                psV[0:tw, half * 512:half * 512 + 512],
                                xnfm[:, kc * 512 + ts_ * 128:kc * 512 + ts_ * 128 + tw],
                                wqkv[:, kc * 3072 + 2048 + half * 512:kc * 3072 + 2048 + half * 512 + 512],
                                start=(kc == 0), stop=(kc == 7))
                        inst.then_inc(s["mm"].h, 1)
                        s["mm"].n += 1


            @blk.vector
            def _(eng, a0=a0, alen=alen, ntl=ntl, mmP=mmP, do_q=do_q,
                  nKQ=nKQ, nG=nG, nsp_g=nsp_g, plan=plan):
                eng = AD(eng)
                nmm = 0
                for pj in range(2 if do_q else 1):
                    dst = kst if pj == 0 else qst
                    for oc in range(8):
                        slot = nmm % 2
                        eng.wait_ge(s["mm"].h, mmP + nmm + 1)
                        if nmm >= 2:
                            p, tgt = plan[nmm - 2]
                            eng.wait_ge(s[p].h, tgt)
                        if pj == 0:
                            eng.tensor_copy(dst[:, slot * 512:slot * 512 + alen],
                                            psP[:, slot * 512:slot * 512 + alen]).then_inc(s["ev"].h, 1)
                        else:
                            eng.tensor_scalar(dst[:, slot * 512:slot * 512 + alen],
                                              psP[:, slot * 512:slot * 512 + alen],
                                              0.125, None, ALU.mult).then_inc(s["ev"].h, 1)
                        s["ev"].n += 1
                        nmm += 1
                for ts_ in range(ntl):
                    tw = min(128, alen - ts_ * 128)
                    for half in range(2):
                        vi = 2 * ts_ + half
                        eng.wait_ge(s["mm"].h, mmP + nKQ + nG + vi + 1)
                        if ts_ >= 1:
                            p, tgt = plan[nsp_g + ts_ - 1]
                            eng.wait_ge(s[p].h, tgt)
                        eng.tensor_copy(vst[0:tw, half * 512:half * 512 + 512],
                                        psV[0:tw, half * 512:half * 512 + 512]).then_inc(s["ev"].h, 1)
                        s["ev"].n += 1

            @blk.scalar
            def _(eng, alen=alen, mmP=mmP, do_q=do_q, nKQ=nKQ, plan=plan):
                eng = AD(eng)
                if do_q:
                    for oc in range(8):
                        eng.wait_ge(s["mm"].h, mmP + nKQ + oc + 1)
                        if oc >= 2:
                            p, tgt = plan[nKQ + oc - 2]
                            eng.wait_ge(s[p].h, tgt)
                        eng.activation(gst[:, (oc % 2) * 512:(oc % 2) * 512 + alen],
                                       psG[:, 0:alen], AF.Sigmoid,
                                       bias=sb["c_bg"][:, oc:oc + 1]).then_inc(s["sc"].h, 1)
                        s["sc"].n += 1

            @blk.sync
            def _(eng, a0=a0, alen=alen, ntl=ntl, do_q=do_q, evF=evF,
                  scG0=scG0, nKQ=nKQ, nG=nG, nsp_g=nsp_g, plan=plan):
                ev = evF
                j = 0
                for pj in range(2 if do_q else 1):
                    for oc in range(8):
                        slot = (pj * 8 + oc) % 2
                        ev += 1
                        eng.wait_ge(s["ev"].h, ev)
                        p, tgt = plan[j]
                        eng.wait_ge(s[p].h, tgt - 16)
                        if pj == 0:
                            eng.dma_start(
                                out=io["k_scr"][:, oc * NKV + a0:oc * NKV + a0 + alen],
                                in_=kst[:, slot * 512:slot * 512 + alen]).then_inc(s[p].h, 16)
                        else:
                            qa = a0 - KVQ
                            eng.dma_start(
                                out=io["q_scr"][:, oc * NQ + qa:oc * NQ + qa + alen],
                                in_=qst[:, slot * 512:slot * 512 + alen]).then_inc(s[p].h, 16)
                        s[p].n += 16
                        j += 1
                if do_q:
                    for oc in range(8):
                        eng.wait_ge(s["sc"].h, scG0 + oc + 1)
                        qa = a0 - KVQ
                        p, tgt = plan[j]
                        eng.wait_ge(s[p].h, tgt - 16)
                        eng.dma_start(
                            out=io["g_scr"][:, oc * NQ + qa:oc * NQ + qa + alen],
                            in_=gst[:, (oc % 2) * 512:(oc % 2) * 512 + alen]).then_inc(s[p].h, 16)
                        s[p].n += 16
                        j += 1
                for ts_ in range(ntl):
                    tw = min(128, alen - ts_ * 128)
                    ev += 2
                    eng.wait_ge(s["ev"].h, ev)
                    r0 = a0 + ts_ * 128
                    p, tgt = plan[j]
                    eng.wait_ge(s[p].h, tgt - 16)
                    eng.dma_start(out=io["v_scr"][r0:r0 + tw, :],
                                  in_=vst[0:tw, 0:1024]).then_inc(s[p].h, 16)
                    s[p].n += 16
                    j += 1

        spfA, spfB = s["spA"].n, s["spB"].n

        @blk.sync
        def _(eng):
            eng.wait_ge(s["spA"].h, spfA)
            eng.wait_ge(s["spB"].h, spfB)



def phase_s2c(kb, blk):
    nc, s, sb, io = kb.nc, kb.s, kb.sb, kb.io
    import contextlib

    with contextlib.ExitStack() as _es:
        ent = _es.enter_context
        kh = ent(nc.sbuf_tensor("at_kh", [128, 8 * NKH], BF16))
        qh = ent(nc.sbuf_tensor("at_qh", [128, 8 * NQH], BF16))
        vn = ent(nc.sbuf_tensor("at_vn", [128, 9 * 1024], BF16))
        gh = ent(nc.sbuf_tensor("at_gh", [128, 8 * NQH], BF16))
        von = ent(nc.sbuf_tensor("at_von", [128, 8 * NQH], BF16))
        vof = ent(nc.sbuf_tensor("at_vof", [128, 8 * NQH], BF16))
        vr1 = ent(nc.sbuf_tensor("at_vr1", [128, 2 * 1024], BF16))
        vr2 = ent(nc.sbuf_tensor("at_vr2", [128, 2 * 1024], BF16))
        mskn = ent(nc.sbuf_tensor("at_mskn", [128, 16 * W_NEAR], BF16))
        mskf = ent(nc.sbuf_tensor("at_mskf", [128, 16 * W_FARM], BF16))
        mskn0 = ent(nc.sbuf_tensor("at_mskn0", [128, 16 * W_NEAR], BF16))
        mskf0 = ent(nc.sbuf_tensor("at_mskf0", [128, 16 * W_FARM], BF16))
        pbuf = ent(nc.sbuf_tensor("at_p", [128, 2 * 256], BF16))
        pts = ent(nc.sbuf_tensor("at_pts", [128, 2 * 256], BF16))
        sums_n = ent(nc.sbuf_tensor("at_sn", [128, 9 * 16], F32))
        sums_f = ent(nc.sbuf_tensor("at_sf", [128, 16 * 16], F32))
        sqst = ent(nc.sbuf_tensor("at_sq", [128, NKH], F32))
        qkm = ent(nc.sbuf_tensor("at_qkm", [128, 8], F32))
        negmb = ent(nc.sbuf_tensor("at_nm", [128, 16], F32))
        rows = ent(nc.sbuf_tensor("at_rows", [1, 32], F32))
        denr = ent(nc.sbuf_tensor("at_den", [16, NQH], F32))
        denf = ent(nc.sbuf_tensor("at_denf", [16, NQH], F32))
        pbsb = ent(nc.sbuf_tensor("at_pb", [45, 16], F32))
        pbhi = ent(nc.sbuf_tensor("at_pbhi", [45, 16], BF16))
        pblo = ent(nc.sbuf_tensor("at_pblo", [45, 16], BF16))
        pblof = ent(nc.sbuf_tensor("at_pblof", [45, 16], F32))
        indst = ent(nc.sbuf_tensor("at_ind", [45, 5120], BF16))
        mrst = ent(nc.sbuf_tensor("at_mr", [16, 512], F32))
        cnrow = ent(nc.sbuf_tensor("at_cnrow", [1, W_NEAR], F32))
        cfrow = ent(nc.sbuf_tensor("at_cfrow", [1, W_FARM], F32))
        cmb = ent(nc.sbuf_tensor("at_cmb", [128, 512], BF16))
        cmb2 = ent(nc.sbuf_tensor("at_cmb2", [128, 512], BF16))
        aost = ent(nc.sbuf_tensor("at_aost", [128, 2 * 512], BF16))
        psS = ent(nc.psum_tensor("at_psS", [128, 1024], F32))
        psT1 = ent(nc.psum_tensor("at_psT1", [128, 256], BF16))
        psT2 = ent(nc.psum_tensor("at_psT2", [128, 256], BF16))
        psO = ent(nc.psum_tensor("at_psO", [128, 1024], F32))
        psD = ent(nc.psum_tensor("at_psD", [128, 1024], F32))

        xld0 = s["xld"].n

        @blk.sync
        def _(eng):
            eng.dma_start(out=pbsb[:, :], in_=io["pb"][:, :]).then_inc(s["xld"].h, 16)
            eng.dma_start(out=cnrow[0:1, :], in_=io["causal_n"][0:1, :]).then_inc(s["xld"].h, 16)
            eng.dma_start(out=cfrow[0:1, :], in_=io["causal_f"][0:1, :]).then_inc(s["xld"].h, 16)
            s["xld"].n += 48

        @blk.vector
        def _(eng):
            eng = AD(eng)
            eng.wait_ge(s["xld"].h, xld0 + 48)
            eng.tensor_copy(pbhi[:, :], pbsb[:, :])
            eng.tensor_tensor(pblof[:, :], pbsb[:, :], pbhi[:, :], ALU.subtract)
            eng.tensor_copy(pblo[:, :], pblof[:, :]).then_inc(s["dv"].h, 1)
            s["dv"].n += 1
        dv_pb = s["dv"].n

        for srcname, wdt, rowoff in [("ind_near", W_NEAR, 0),
                                     ("ind_far", W_FARM, 128 * W_NEAR)]:
            ncols = 128 * wdt
            nchunks = (ncols + 5119) // 5120
            for ch in range(nchunks):
                c0 = ch * 5120
                cw = min(5120, ncols - c0)
                nsub = (cw + 511) // 512
                dv_prev = s["dv"].n

                @blk.sync
                def _(eng, c0=c0, cw=cw, srcname=srcname, dv_prev=dv_prev):
                    eng.wait_ge(s["dv"].h, dv_prev)
                    eng.dma_start(out=indst[:, 0:cw],
                                  in_=io[srcname][:, c0:c0 + cw]).then_inc(s["xld"].h, 16)
                    s["xld"].n += 16
                xn2 = s["xld"].n
                mmb = s["mm"].n
                dvb = s["dv"].n
                spb = s["sp"].n

                @blk.tensor
                def _(eng, cw=cw, xn2=xn2, nsub=nsub, dvb=dvb):
                    eng.wait_ge(s["xld"].h, xn2)
                    eng.wait_ge(s["dv"].h, dv_pb)
                    for sub in range(nsub):
                        w = min(512, cw - sub * 512)
                        if sub >= 2:
                            eng.wait_ge(s["dv"].h, dvb + sub - 1)
                        bank = (sub % 2) * 512
                        eng.matmul(psD[0:16, bank:bank + w], pbhi[:, :],
                                   indst[:, sub * 512:sub * 512 + w],
                                   start=True, stop=False)
                        eng.matmul(psD[0:16, bank:bank + w], pblo[:, :],
                                   indst[:, sub * 512:sub * 512 + w],
                                   start=False, stop=True).then_inc(s["mm"].h, 1)
                        s["mm"].n += 1

                @blk.vector
                def _(eng, cw=cw, mmb=mmb, nsub=nsub, spb=spb):
                    eng = AD(eng)
                    for sub in range(nsub):
                        w = min(512, cw - sub * 512)
                        eng.wait_ge(s["mm"].h, mmb + sub + 1)
                        eng.wait_ge(s["sp"].h, spb + 16 * sub)
                        bank = (sub % 2) * 512
                        eng.tensor_copy(mrst[0:16, 0:w],
                                        psD[0:16, bank:bank + w]).then_inc(s["dv"].h, 1)
                        s["dv"].n += 1

                @blk.sync
                def _(eng, c0=c0, cw=cw, rowoff=rowoff, dvb2=s["dv"].n - nsub,
                      nsub=nsub, spb=spb):
                    for sub in range(nsub):
                        w = min(512, cw - sub * 512)
                        eng.wait_ge(s["dv"].h, dvb2 + sub + 1)
                        eng.wait_ge(s["sp"].h, spb + 16 * sub)
                        eng.dma_start(
                            out=io["mrow_scr"][0:16, rowoff + c0 + sub * 512:
                                               rowoff + c0 + sub * 512 + w],
                            in_=mrst[0:16, 0:w]).then_inc(s["sp"].h, 16)
                        s["sp"].n += 16

        sp_mrow = s["sp"].n

        @blk.gpsimd
        def _(eng):
            eng.wait_ge(s["sp"].h, sp_mrow)
            for h in range(16):
                eng.wait_ge(s["gp"].h, s["gp"].n)
                eng.dma_start(
                    out=mskn[:, h * W_NEAR:(h + 1) * W_NEAR],
                    in_=io["mrow_scr"][h:h + 1, 0:128 * W_NEAR]
                    .rearrange("o (i c) -> (o i) c", i=128)).then_inc(s["gp"].h, 16)
                s["gp"].n += 16
                eng.wait_ge(s["gp"].h, s["gp"].n)
                eng.dma_start(
                    out=mskf[:, h * W_FARM:(h + 1) * W_FARM],
                    in_=io["mrow_scr"][h:h + 1, 128 * W_NEAR:128 * (W_NEAR + W_FARM)]
                    .rearrange("o (i c) -> (o i) c", i=128)).then_inc(s["gp"].h, 16)
                s["gp"].n += 16
        gp_msk = s["gp"].n
        mm_cv = s["mm"].n

        @blk.tensor
        def _(eng, dv_need=s["dv"].n):
            eng.wait_ge(s["xld"].h, xld0 + 48)
            eng.wait_ge(s["dv"].h, dv_need)
            eng.matmul(psD[:, 0:W_NEAR], sb["c_row1f"][0:1, :],
                       cnrow[0:1, 0:W_NEAR], start=True, stop=True)
            eng.matmul(psD[:, 512:512 + W_FARM], sb["c_row1f"][0:1, :],
                       cfrow[0:1, 0:W_FARM], start=True,
                       stop=True).then_inc(s["mm"].h, 1)
            s["mm"].n += 1

        @blk.vector
        def _(eng, mm_cv=mm_cv, gp_msk=gp_msk):
            eng = AD(eng)
            eng.wait_ge(s["mm"].h, mm_cv + 1)
            eng.wait_ge(s["gp"].h, gp_msk)
            cnv = AP(psD, 0, [[1024, 128], [0, 16], [1, W_NEAR]])
            cfv = AP(psD, 512, [[1024, 128], [0, 16], [1, W_FARM]])
            mnv = AP(mskn, 0, [[16 * W_NEAR, 128], [W_NEAR, 16], [1, W_NEAR]])
            mfv = AP(mskf, 0, [[16 * W_FARM, 128], [W_FARM, 16], [1, W_FARM]])
            mn0 = AP(mskn0, 0, [[16 * W_NEAR, 128], [W_NEAR, 16], [1, W_NEAR]])
            mf0 = AP(mskf0, 0, [[16 * W_FARM, 128], [W_FARM, 16], [1, W_FARM]])
            eng.tensor_tensor(mn0, mnv, cnv, ALU.add)
            eng.tensor_tensor(mf0, mfv, cfv, ALU.add).then_inc(s["dv"].h, 1)
            s["dv"].n += 1
        dv_msk = s["dv"].n

        aog_spill_plan = []

        for hh in range(2):
            QH0 = hh * NQH
            use_v0 = (hh == 0)
            xA0 = s["xldA"].n
            dv_need = s["dv"].n

            @blk.sync
            def _(eng, QH0=QH0, dv_need=dv_need):
                eng.wait_ge(s["dv"].h, dv_need)
                eng.dma_start(
                    out=kh.ap().rearrange("p (c t) -> p c t", c=8),
                    in_=io["k_scr"].rearrange("p (c t) -> p c t", c=8)[:, :, QH0:QH0 + NKH],
                ).then_inc(s["xldA"].h, 16)
                eng.dma_start(
                    out=qh.ap().rearrange("p (c t) -> p c t", c=8),
                    in_=io["q_scr"].rearrange("p (c t) -> p c t", c=8)[:, :, QH0:QH0 + NQH],
                ).then_inc(s["xldA"].h, 16)
                s["xldA"].n += 32
                for vt in range(9):
                    r0 = QH0 + 1504 + vt * 128
                    w = min(128, 1072 - vt * 128)
                    eng.dma_start(out=vn[0:w, vt * 1024:vt * 1024 + 1024],
                                  in_=io["v_scr"][r0:r0 + w, :]).then_inc(s["xldA"].h, 16)
                    s["xldA"].n += 16
                eng.dma_start(
                    out=gh.ap().rearrange("p (c t) -> p c t", c=8),
                    in_=io["g_scr"].rearrange("p (c t) -> p c t", c=8)[:, :, QH0:QH0 + NQH],
                ).then_inc(s["xldA"].h, 16)
                s["xldA"].n += 16
            xA_all = s["xldA"].n

            for which, buf, wid in [(0, qh, NQH), (1, kh, NKH)]:
                nsub = (wid + 511) // 512
                for kc in range(8):
                    @blk.vector
                    def _(eng, buf=buf, wid=wid, kc=kc, xA_all=xA_all,
                          mm_need=s["mm"].n):
                        eng = AD(eng)
                        eng.wait_ge(s["xldA"].h, xA_all)
                        eng.wait_ge(s["mm"].h, mm_need)
                        eng.tensor_tensor(sqst[:, 0:wid], buf[:, kc * wid:(kc + 1) * wid],
                                          buf[:, kc * wid:(kc + 1) * wid], ALU.mult)
                        eng.drain().then_inc(s["dv"].h, 1)
                        s["dv"].n += 1
                    dv_sq = s["dv"].n

                    for sub in range(nsub):
                        w = min(512, wid - sub * 512)
                        mmx = s["mm"].n

                        @blk.tensor
                        def _(eng, sub=sub, w=w, dv_sq=dv_sq, dv_rd=s["dv"].n):
                            eng.wait_ge(s["dv"].h, dv_sq)
                            if sub >= 1:
                                eng.wait_ge(s["dv"].h, dv_rd)
                            eng.matmul(psD[0:2, 0:w], sb["c_selqf"][:, :],
                                       sqst[:, sub * 512:sub * 512 + w],
                                       start=True, stop=True).then_inc(s["mm"].h, 1)
                            s["mm"].n += 1

                        @blk.vector
                        def _(eng, which=which, kc=kc, sub=sub, w=w, mmx=mmx):
                            eng = AD(eng)
                            eng.wait_ge(s["mm"].h, mmx + 1)
                            eng.reduce_max(qkm[0:2, 4:5], psD[0:2, 0:w], axis=AXX)
                            if kc == 0 and sub == 0:
                                eng.tensor_copy(qkm[0:2, which:which + 1],
                                                qkm[0:2, 4:5])
                            else:
                                eng.tensor_tensor(qkm[0:2, which:which + 1],
                                                  qkm[0:2, which:which + 1],
                                                  qkm[0:2, 4:5], ALU.max)
                            eng.drain().then_inc(s["dv"].h, 1)
                            s["dv"].n += 1
            dv_qk = s["dv"].n
            mm_t2x = s["mm"].n

            @blk.tensor
            def _(eng, dv_qk=dv_qk):
                eng.wait_ge(s["dv"].h, dv_qk)
                eng.transpose(psD[0:2, 0:2], qkm[0:2, 0:2],
                              sb["c_identf"][0:2, 0:2]).then_inc(s["mm"].h, 1)
                s["mm"].n += 1

            @blk.vector
            def _(eng, mm_t2x=mm_t2x):
                eng = AD(eng)
                eng.wait_ge(s["mm"].h, mm_t2x + 1)
                eng.reduce_max(qkm[0:2, 5:6], psD[0:2, 0:2], axis=AXX).then_inc(s["dv"].h, 1)
                s["dv"].n += 1
            dv_gm = s["dv"].n
            mm_gm = s["mm"].n

            @blk.tensor
            def _(eng, dv_gm=dv_gm):
                eng.wait_ge(s["dv"].h, dv_gm)
                eng.nop().then_inc(s["mm"].h, 1)
                s["mm"].n += 1

            @blk.vector
            def _(eng, mm_gm=mm_gm, sc_need=s["sc"].n):
                eng = AD(eng)
                eng.wait_ge(s["mm"].h, mm_gm + 1)
                eng.nop()
                eng.drain().then_inc(s["dv"].h, 1)
                s["dv"].n += 1
            dv_fix = s["dv"].n
            sc_sq0 = s["sc"].n

            @blk.scalar
            def _(eng, dv_fix=dv_fix):
                eng = AD(eng)
                eng.wait_ge(s["dv"].h, dv_fix)
                eng.activation(qkm[0:2, 6:7], qkm[0:2, 5:6],
                               AF.Sqrt).then_inc(s["sc"].h, 1)
                s["sc"].n += 1
            sc_sq = s["sc"].n
            mm_pr = s["mm"].n

            @blk.tensor
            def _(eng, sc_sq=sc_sq):
                eng.wait_ge(s["sc"].h, sc_sq)
                eng.transpose(psD[0:1, 0:2], qkm[0:2, 6:7],
                              sb["c_identf"][0:2, 0:2]).then_inc(s["mm"].h, 1)
                s["mm"].n += 1

            @blk.vector
            def _(eng, mm_pr=mm_pr):
                eng = AD(eng)
                eng.wait_ge(s["mm"].h, mm_pr + 1)
                eng.tensor_copy(rows[0:1, 0:2], psD[0:1, 0:2])
                eng.tensor_tensor(rows[0:1, 2:3], rows[0:1, 0:1], rows[0:1, 1:2],
                                  ALU.mult)
                eng.tensor_scalar(rows[0:1, 2:3], rows[0:1, 2:3], -1.02, -0.5,
                                  ALU.mult, ALU.add).then_inc(s["dv"].h, 1)
                s["dv"].n += 1
            dv_row = s["dv"].n
            mm_bc = s["mm"].n

            @blk.tensor
            def _(eng, dv_row=dv_row):
                eng.wait_ge(s["dv"].h, dv_row)
                eng.matmul(psD[:, 512:513], sb["c_row1f"][0:1, :],
                           rows[0:1, 2:3], start=True, stop=True).then_inc(s["mm"].h, 1)
                s["mm"].n += 1

            @blk.vector
            def _(eng, mm_bc=mm_bc):
                eng = AD(eng)
                eng.wait_ge(s["mm"].h, mm_bc + 1)
                eng.tensor_copy(negmb[:, 0:1], psD[:, 512:513]).then_inc(s["dv"].h, 1)
                s["dv"].n += 1
            dv_nmb = s["dv"].n

            def attn_iter(idx, score_fn, Mq, W, exp_dst, acc_ap, av_lhs, out_ap):
                slot = idx % 2
                mm0 = s["mm"].n

                @blk.tensor
                def _(eng, mm0=mm0, sc_need=s["sc"].n, idx=idx):
                    if idx >= 2:
                        eng.wait_ge(s["sc"].h, sc_need)
                    score_fn(eng, slot)
                    s["mm"].n += 1
                mm_s = s["mm"].n

                @blk.scalar
                def _(eng, mm_s=mm_s, Mq=Mq, W=W, slot=slot, exp_dst=exp_dst,
                      acc_ap=acc_ap, dv_need=s["dv"].n, idx=idx):
                    eng = AD(eng)
                    eng.wait_ge(s["mm"].h, mm_s)
                    if idx >= 2:
                        eng.wait_ge(s["dv"].h, dv_need)
                    eng.activation(pbuf[0:Mq, slot * 256:slot * 256 + W],
                                   psS[0:Mq, slot * 512:slot * 512 + W],
                                   AF.Exp, bias=exp_dst,
                                   accum_out=acc_ap).then_inc(s["sc"].h, 1)
                    s["sc"].n += 1
                sc_e = s["sc"].n

                @blk.tensor
                def _(eng, sc_e=sc_e, Mq=Mq, W=W, slot=slot, dv_need=s["dv"].n,
                      idx=idx):
                    eng.wait_ge(s["sc"].h, sc_e)
                    if idx >= 1:
                        eng.wait_ge(s["dv"].h, dv_need)
                    w1_ = min(W, 128)
                    w2 = W - w1_
                    inst = eng.transpose(psT1[0:w1_, 0:Mq],
                                         pbuf[0:Mq, slot * 256:slot * 256 + w1_],
                                         sb["c_ident"][0:Mq, 0:Mq])
                    if w2 > 0:
                        inst = eng.transpose(psT2[0:w2, 0:Mq],
                                             pbuf[0:Mq, slot * 256 + 128:slot * 256 + W],
                                             sb["c_ident"][0:Mq, 0:Mq])
                    inst.then_inc(s["mm"].h, 1)
                    s["mm"].n += 1
                mm_t = s["mm"].n

                @blk.vector
                def _(eng, mm_t=mm_t, Mq=Mq, W=W):
                    eng = AD(eng)
                    eng.wait_ge(s["mm"].h, mm_t)
                    w1_ = min(W, 128)
                    w2 = W - w1_
                    eng.tensor_copy(pts[0:w1_, 0:Mq], psT1[0:w1_, 0:Mq])
                    if w2 > 0:
                        eng.tensor_copy(pts[0:w2, 256:256 + Mq], psT2[0:w2, 0:Mq])
                    eng.drain().then_inc(s["dv"].h, 1)
                    s["dv"].n += 1
                dv_pt = s["dv"].n

                @blk.tensor
                def _(eng, dv_pt=dv_pt, Mq=Mq, W=W, av_lhs=av_lhs, idx=idx,
                      dv_o=s["dv"].n):
                    eng.wait_ge(s["dv"].h, dv_pt)
                    oslot = idx % 2
                    w1_ = min(W, 128)
                    w2 = W - w1_
                    inst = eng.matmul(psO[0:64, oslot * 512:oslot * 512 + Mq],
                                      av_lhs[0], pts[0:w1_, 0:Mq],
                                      start=True, stop=(w2 <= 0))
                    if w2 > 0:
                        inst = eng.matmul(psO[0:64, oslot * 512:oslot * 512 + Mq],
                                          av_lhs[1], pts[0:w2, 256:256 + Mq],
                                          start=False, stop=True)
                    inst.then_inc(s["mm"].h, 1)
                    s["mm"].n += 1
                mm_av = s["mm"].n

                @blk.vector
                def _(eng, mm_av=mm_av, Mq=Mq, out_ap=out_ap, idx=idx):
                    eng = AD(eng)
                    eng.wait_ge(s["mm"].h, mm_av)
                    oslot = idx % 2
                    eng.tensor_copy(out_ap,
                                    psO[0:64, oslot * 512:oslot * 512 + Mq]).then_inc(s["dv"].h, 1)
                    s["dv"].n += 1

            idx = 0
            for t in range(9):
                T0 = 128 * t
                Mq = 128 if t < 8 else 16
                W = Mq + 32
                for h in range(16):
                    rbase = (h % 2) * 64
                    chq = h // 2
                    msk = mskn0 if (use_v0 and t == 0) else mskn

                    def score_n(eng, slot, T0=T0, Mq=Mq, W=W, h=h, rbase=rbase,
                                chq=chq, msk=msk, first=(idx == 0),
                                dv_nmb=dv_nmb):
                        if first:
                            eng.wait_ge(s["dv"].h, dv_nmb)
                        eng.matmul(psS[0:Mq, slot * 512:slot * 512 + W],
                                   qh[rbase:rbase + 64, chq * NQH + T0:chq * NQH + T0 + Mq],
                                   kh[rbase:rbase + 64, chq * NKH + 1504 + T0:chq * NKH + 1504 + T0 + W],
                                   start=True, stop=False, tile_position=(rbase, 0))
                        eng.matmul(psS[0:Mq, slot * 512:slot * 512 + W],
                                   sb["c_ident"][:, 0:Mq],
                                   msk[:, h * W_NEAR:h * W_NEAR + W],
                                   start=False, stop=True,
                                   tile_position=(0, 0)).then_inc(s["mm"].h, 1)

                    av = [vn[0:min(W, 128), t * 1024 + h * 64:t * 1024 + h * 64 + 64]]
                    if W > 128:
                        av.append(vn[0:W - 128, (t + 1) * 1024 + h * 64:(t + 1) * 1024 + h * 64 + 64])
                    attn_iter(idx, score_n, Mq, W,
                              negmb[0:Mq, 0:1],
                              sums_n[0:Mq, t * 16 + h:t * 16 + h + 1],
                              av,
                              von[rbase:rbase + 64, chq * NQH + T0:chq * NQH + T0 + Mq])
                    idx += 1

            for r in range(16):
                gpr0 = s["gp"].n

                @blk.gpsimd
                def _(eng, r=r, QH0=QH0, dv_need=s["dv"].n):
                    eng.wait_ge(s["dv"].h, dv_need)
                    sl = r % 2
                    eng.wait_ge(s["gp"].h, s["gp"].n)
                    eng.dma_start(
                        out=vr1[:, sl * 1024:sl * 1024 + 1024],
                        in_=io["v_scr"][QH0 + r:QH0 + r + 16 * 127 + 1:16, :]).then_inc(s["gp"].h, 16)
                    s["gp"].n += 16
                    eng.wait_ge(s["gp"].h, s["gp"].n)
                    eng.dma_start(
                        out=vr2[0:33, sl * 1024:sl * 1024 + 1024],
                        in_=io["v_scr"][QH0 + r + 16 * 128:QH0 + r + 16 * 160 + 1:16, :]).then_inc(s["gp"].h, 16)
                    s["gp"].n += 16
                gpr = s["gp"].n

                msk = mskf0 if use_v0 else mskf
                for h in range(16):
                    rbase = (h % 2) * 64
                    chq = h // 2
                    sl = r % 2

                    def score_f(eng, slot, r=r, h=h, rbase=rbase, chq=chq,
                                msk=msk, gpr=gpr, first=(h == 0)):
                        if first:
                            eng.wait_ge(s["gp"].h, gpr)
                        eng.matmul(psS[0:SUBQ, slot * 512:slot * 512 + W_FARH],
                                   qh[rbase:rbase + 64, chq * NQH + r:chq * NQH + r + 16 * (SUBQ - 1) + 1:16],
                                   kh[rbase:rbase + 64, chq * NKH + r:chq * NKH + r + 16 * (W_FARH - 1) + 1:16],
                                   start=True, stop=False, tile_position=(rbase, 0))
                        eng.matmul(psS[0:SUBQ, slot * 512:slot * 512 + W_FARH],
                                   sb["c_ident"][:, 0:SUBQ],
                                   msk[:, h * W_FARM:h * W_FARM + W_FARH],
                                   start=False, stop=True,
                                   tile_position=(0, 0)).then_inc(s["mm"].h, 1)

                    av = [vr1[:, sl * 1024 + h * 64:sl * 1024 + h * 64 + 64],
                          vr2[0:33, sl * 1024 + h * 64:sl * 1024 + h * 64 + 64]]
                    attn_iter(idx, score_f, SUBQ, W_FARH,
                              negmb[0:SUBQ, 0:1],
                              sums_f[0:SUBQ, r * 16 + h:r * 16 + h + 1],
                              av,
                              vof[rbase:rbase + 64, chq * NQH + r:chq * NQH + r + 16 * (SUBQ - 1) + 1:16])
                    idx += 1

            dv_att = s["dv"].n

            for t in range(9):
                T0 = 128 * t
                Mq = 128 if t < 8 else 16
                mmx = s["mm"].n

                @blk.tensor
                def _(eng, t=t, Mq=Mq, dv_att=dv_att, dv_need=s["dv"].n):
                    eng.wait_ge(s["dv"].h, dv_att)
                    if t >= 1:
                        eng.wait_ge(s["dv"].h, dv_need)
                    eng.transpose(psD[0:16, 0:Mq], sums_n[0:Mq, t * 16:(t + 1) * 16],
                                  sb["c_identf"][0:Mq, 0:Mq]).then_inc(s["mm"].h, 1)
                    s["mm"].n += 1

                @blk.vector
                def _(eng, t=t, T0=T0, Mq=Mq, mmx=mmx):
                    eng = AD(eng)
                    eng.wait_ge(s["mm"].h, mmx + 1)
                    eng.tensor_copy(denr[0:16, T0:T0 + Mq],
                                    psD[0:16, 0:Mq]).then_inc(s["dv"].h, 1)
                    s["dv"].n += 1

            for r in range(16):
                mmx = s["mm"].n

                @blk.tensor
                def _(eng, r=r, dv_need=s["dv"].n):
                    eng.wait_ge(s["dv"].h, dv_need)
                    eng.transpose(psD[0:16, 0:SUBQ], sums_f[0:SUBQ, r * 16:(r + 1) * 16],
                                  sb["c_identf"][0:SUBQ, 0:SUBQ]).then_inc(s["mm"].h, 1)
                    s["mm"].n += 1

                @blk.vector
                def _(eng, r=r, mmx=mmx):
                    eng = AD(eng)
                    eng.wait_ge(s["mm"].h, mmx + 1)
                    eng.tensor_copy(denf[0:16, r:r + 16 * (SUBQ - 1) + 1:16],
                                    psD[0:16, 0:SUBQ]).then_inc(s["dv"].h, 1)
                    s["dv"].n += 1

            @blk.vector
            def _(eng):
                eng = AD(eng)
                eng.tensor_tensor(denr[:, :], denr[:, :], denf[:, :], ALU.add)
                eng.tensor_scalar(denr[:, :], denr[:, :], 1e-30, None, ALU.add)
                eng.reciprocal(denr[:, :], denr[:, :]).then_inc(s["dv"].h, 1)
                s["dv"].n += 1
            dv_den = s["dv"].n

            subs = [(0, 512), (512, 512), (1024, 16)]
            for c in range(8):
                for si, (sb0, sw) in enumerate(subs):
                    gidx = getattr(kb, "_spill_gidx", 0)
                    p = "spA" if gidx % 2 == 0 else "spB"
                    kb._spill_gidx = gidx + 1
                    tgt = s[p].n + 16
                    mmx = s["mm"].n

                    @blk.tensor
                    def _(eng, c=c, sb0=sb0, sw=sw, dv_den=dv_den, si=si,
                          dv_need=s["dv"].n):
                        eng.wait_ge(s["dv"].h, dv_den)
                        if c + si > 0:
                            eng.wait_ge(s["dv"].h, dv_need)
                        eng.matmul(psD[:, 0:sw], sb["c_selh"][:, c * 128:(c + 1) * 128],
                                   denr[0:16, sb0:sb0 + sw],
                                   start=True, stop=True).then_inc(s["mm"].h, 1)
                        s["mm"].n += 1

                    @blk.vector
                    def _(eng, c=c, sb0=sb0, sw=sw, mmx=mmx, p=p, tgt=tgt,
                          QH0=QH0):
                        eng = AD(eng)
                        eng.wait_ge(s["mm"].h, mmx + 1)
                        eng.wait_ge(s[p].h, tgt - 16)
                        slot = (getattr(kb, "_aog_slot", 0)) % 2
                        kb._aog_slot = getattr(kb, "_aog_slot", 0) + 1
                        eng.tensor_tensor(cmb[:, 0:sw],
                                          von[:, c * NQH + sb0:c * NQH + sb0 + sw],
                                          vof[:, c * NQH + sb0:c * NQH + sb0 + sw],
                                          ALU.add)
                        eng.tensor_tensor(cmb2[:, 0:sw], cmb[:, 0:sw],
                                          psD[:, 0:sw], ALU.mult)
                        eng.tensor_tensor(aost[:, slot * 512:slot * 512 + sw],
                                          cmb2[:, 0:sw],
                                          gh[:, c * NQH + sb0:c * NQH + sb0 + sw],
                                          ALU.mult)
                        eng.drain().then_inc(s["dv"].h, 1)
                        s["dv"].n += 1
                    dv_cb = s["dv"].n

                    @blk.sync
                    def _(eng, c=c, sb0=sb0, sw=sw, dv_cb=dv_cb, p=p, tgt=tgt,
                          QH0=QH0):
                        eng.wait_ge(s["dv"].h, dv_cb)
                        eng.wait_ge(s[p].h, tgt - 16)
                        slot = (getattr(kb, "_aog_slot_sp", 0)) % 2
                        kb._aog_slot_sp = getattr(kb, "_aog_slot_sp", 0) + 1
                        eng.dma_start(
                            out=io["ao_scr"][:, c * NQ + QH0 + sb0:c * NQ + QH0 + sb0 + sw],
                            in_=aost[:, slot * 512:slot * 512 + sw]).then_inc(s[p].h, 16)
                        s[p].n += 16

        spfA, spfB = s["spA"].n, s["spB"].n

        @blk.sync
        def _(eng):
            eng.wait_ge(s["spA"].h, spfA)
            eng.wait_ge(s["spB"].h, spfB)



def matmul_fm_groups(kb, blk, tag, wname, wkc, woc, src_scr, src_cols,
                     dst_scr, resid_scr, resid_off, scale_half=False,
                     resid2=None, resid2_off=0):
    nc, s, sb, io = kb.nc, kb.s, kb.sb, kb.io
    import contextlib
    with contextlib.ExitStack() as _es:
        ent = _es.enter_context
        wsb = ent(nc.sbuf_tensor(tag + "_w", [128, wkc * woc * 128], BF16))
        ast_ = ent(nc.sbuf_tensor(tag + "_a", [128, 2 * wkc * 512], BF16))
        xs = ent(nc.sbuf_tensor(tag + "_xs", [128, 4 * D], F32))
        xs2 = ent(nc.sbuf_tensor(tag + "_xs2", [128, 4 * D], F32)) if resid2 is not None else None
        dfm = ent(nc.sbuf_tensor(tag + "_d", [128, woc * 512], BF16))
        xo = ent(nc.sbuf_tensor(tag + "_xo", [128, D], F32))
        psP = ent(nc.psum_tensor(tag + "_pP", [128, 1024], F32))
        psT = ent(nc.psum_tensor(tag + "_pT", [128, 1024], BF16))

        gp0 = s["gp"].n

        fence = {k: s[k].n for k in ["mm", "ev", "dv", "sc"]}

        @blk.gpsimd
        def _(eng):
            for k, v in fence.items():
                eng.wait_ge(s[k].h, v)
            eng.wait_ge(s["gp"].h, s["gp"].n)
            eng.dma_start(
                out=wsb.ap().rearrange("p (c d) -> p c d", c=wkc),
                in_=io[wname].rearrange("(c p) d -> p c d", p=128)).then_inc(s["gp"].h, 16)
            s["gp"].n += 16
        gpw = s["gp"].n

        groups = []
        t = 0
        while t < src_cols:
            groups.append((t, min(512, src_cols - t)))
            t += 512

        for gi, (a0, alen) in enumerate(groups):
            ntl = (alen + 127) // 128
            sl = gi % 2
            sem = s["xldA"] if sl == 0 else s["xldB"]
            x0 = sem.n

            @blk.sync
            def _(eng, a0=a0, alen=alen, ntl=ntl, sl=sl, sem=sem,
                  dv_need=getattr(kb, "_" + tag + "_dvprev", 0)):
                eng.wait_ge(s["dv"].h, dv_need)
                eng.dma_start(
                    out=AP(ast_, sl * wkc * 512, [[2 * wkc * 512, 128], [512, wkc], [1, alen]]),
                    in_=src_scr.rearrange("p (c t) -> p c t", c=wkc)[:, :, a0:a0 + alen],
                ).then_inc(sem.h, 16)
                sem.n += 16
                for i in range(ntl):
                    r0 = resid_off + a0 + i * 128
                    w = min(128, src_cols - (a0 + i * 128))
                    eng.dma_start(out=xs[0:w, i * D:i * D + D],
                                  in_=resid_scr[r0:r0 + w, :]).then_inc(sem.h, 16)
                    sem.n += 16
                    if resid2 is not None:
                        r2 = resid2_off + a0 + i * 128
                        eng.dma_start(out=xs2[0:w, i * D:i * D + D],
                                      in_=resid2[r2:r2 + w, :]).then_inc(sem.h, 16)
                        sem.n += 16
            x1_ = sem.n
            mm0 = s["mm"].n
            dv0 = s["dv"].n

            @blk.tensor
            def _(eng, a0=a0, alen=alen, sl=sl, sem=sem, x1_=x1_, gpw=gpw,
                  dv0=dv0):
                eng.wait_ge(sem.h, x1_)
                eng.wait_ge(s["gp"].h, gpw)
                for oc in range(woc):
                    slot = oc % 2
                    if oc >= 2:
                        eng.wait_ge(s["dv"].h, dv0 + (oc - 2) + 1)
                    inst = None
                    for kc in range(wkc):
                        inst = eng.matmul(
                            psP[:, slot * 512:slot * 512 + alen],
                            wsb[:, kc * woc * 128 + oc * 128:kc * woc * 128 + oc * 128 + 128],
                            ast_[:, sl * wkc * 512 + kc * 512:sl * wkc * 512 + kc * 512 + alen],
                            start=(kc == 0), stop=(kc == wkc - 1))
                    inst.then_inc(s["mm"].h, 1)
                    s["mm"].n += 1

            @blk.vector
            def _(eng, alen=alen, mm0=mm0):
                eng = AD(eng)
                for oc in range(woc):
                    slot = oc % 2
                    eng.wait_ge(s["mm"].h, mm0 + oc + 1)
                    eng.tensor_copy(dfm[:, oc * 512:oc * 512 + alen],
                                    psP[:, slot * 512:slot * 512 + alen]).then_inc(s["dv"].h, 1)
                    s["dv"].n += 1
            dv_dfm = s["dv"].n
            mmT = s["mm"].n

            @blk.tensor
            def _(eng, alen=alen, ntl=ntl, dv_dfm=dv_dfm):
                eng.wait_ge(s["dv"].h, dv_dfm)
                for i in range(ntl):
                    w = min(128, alen - i * 128)
                    if i >= 1:
                        eng.wait_ge(s["dv"].h, dv_dfm + i)
                    inst = None
                    for ch in range(8):
                        inst = eng.transpose(psT[0:w, ch * 128:(ch + 1) * 128],
                                             dfm[:, ch * 512 + i * 128:ch * 512 + i * 128 + w],
                                             sb["c_ident"][:, :])
                    inst.then_inc(s["mm"].h, 1)
                    s["mm"].n += 1

            @blk.vector
            def _(eng, a0=a0, alen=alen, ntl=ntl, mmT=mmT,
                  spn=(s["spA"].n, s["spB"].n)):
                eng = AD(eng)
                for i in range(ntl):
                    w = min(128, alen - i * 128)
                    eng.wait_ge(s["mm"].h, mmT + i + 1)
                    gidx = kb._spill_gidx
                    p = "spA" if gidx % 2 == 0 else "spB"
                    eng.wait_ge(s["spA"].h, s["spA"].n)
                    eng.wait_ge(s["spB"].h, s["spB"].n)
                    if scale_half:
                        eng.scalar_tensor_tensor(xo[0:w, :], psT[0:w, 0:1024], 0.5,
                                                 xs[0:w, i * D:i * D + D],
                                                 ALU.mult, ALU.add)
                    else:
                        eng.tensor_tensor(xo[0:w, :], psT[0:w, 0:1024],
                                          xs[0:w, i * D:i * D + D], ALU.add)
                    if resid2 is not None:
                        eng.tensor_tensor(xo[0:w, :], xo[0:w, :],
                                          xs2[0:w, i * D:i * D + D], ALU.add)
                    eng.drain().then_inc(s["dv"].h, 1)
                    s["dv"].n += 1
                    dv_x = s["dv"].n

                    kb._mf_spills.append((a0 + i * 128, w, p, s[p].n + 16, dv_x))
                    kb._spill_gidx += 1
                    s[p].n += 16

            @blk.sync
            def _(eng, dst_scr=dst_scr):
                while kb._mf_emit < len(kb._mf_spills):
                    r0, w, p, tgt, dv_x = kb._mf_spills[kb._mf_emit]
                    kb._mf_emit += 1
                    eng.wait_ge(s["dv"].h, dv_x)
                    eng.wait_ge(s[p].h, tgt - 16)
                    eng.dma_start(out=dst_scr[r0:r0 + w, :],
                                  in_=xo[0:w, :]).then_inc(s[p].h, 16)
            setattr(kb, "_" + tag + "_dvprev", s["dv"].n)

        spfA, spfB = s["spA"].n, s["spB"].n

        @blk.sync
        def _(eng):
            eng.wait_ge(s["spA"].h, spfA)
            eng.wait_ge(s["spB"].h, spfB)


def phase_s2d(kb, blk):
    kb._mf_spills = []
    kb._mf_emit = 0
    kb._spill_gidx = getattr(kb, "_spill_gidx", 0)
    matmul_fm_groups(kb, blk, "s2d", "w_out", 8, 8, kb.io["ao_scr"], NQ,
                     kb.io["x2_scr"], kb.io["z0_scr"], 0)



def phase_s2ef(kb, blk):
    nc, s, sb, io = kb.nc, kb.s, kb.sb, kb.io
    import contextlib
    with contextlib.ExitStack() as _es:
        ent = _es.enter_context
        wfg = ent(nc.sbuf_tensor("ef_wfg", [128, 8 * DS], BF16))
        wfu = ent(nc.sbuf_tensor("ef_wfu", [128, 8 * DS], BF16))
        xs = ent(nc.sbuf_tensor("ef_xs", [128, 2 * D], F32))
        cl = ent(nc.sbuf_tensor("ef_cl", [128, 8], F32))
        dmp = ent(nc.sbuf_tensor("ef_dmp", [128, D], F32))
        xnt = ent(nc.sbuf_tensor("ef_xnt", [128, D], BF16))
        xnfm = ent(nc.sbuf_tensor("ef_xnfm", [128, 8 * 512], BF16))
        sg = ent(nc.sbuf_tensor("ef_sg", [128, 512], BF16))
        ug = ent(nc.sbuf_tensor("ef_ug", [128, 512], BF16))
        gst = ent(nc.sbuf_tensor("ef_gst", [128, 2 * 512], BF16))
        psT = ent(nc.psum_tensor("ef_pT", [128, 1024], BF16))
        psG = ent(nc.psum_tensor("ef_pG", [128, 512], F32))
        psU = ent(nc.psum_tensor("ef_pU", [128, 512], F32))

        gp0 = s["gp"].n

        @blk.gpsimd
        def _(eng):
            eng.wait_ge(s["gp"].h, s["gp"].n)
            eng.dma_start(
                out=wfg.ap().rearrange("p (c d) -> p c d", c=8),
                in_=io["w_fg"].rearrange("(c p) d -> p c d", p=128)).then_inc(s["gp"].h, 16)
            s["gp"].n += 16
            eng.wait_ge(s["gp"].h, s["gp"].n)
            eng.dma_start(
                out=wfu.ap().rearrange("p (c d) -> p c d", c=8),
                in_=io["w_fu"].rearrange("(c p) d -> p c d", p=128)).then_inc(s["gp"].h, 16)
            s["gp"].n += 16
        gpw = s["gp"].n

        groups = []
        t = 0
        while t < NQ:
            groups.append((t, min(512, NQ - t)))
            t += 512

        for a0, alen in groups:
            ntl = (alen + 127) // 128
            xA0, xB0 = s["xldA"].n, s["xldB"].n
            dv0, sc0, mm0, ev0 = (s[k].n for k in ["dv", "sc", "mm", "ev"])

            @blk.sync
            def _(eng, a0=a0, ntl=ntl, dv0=dv0):
                for i in range(ntl):
                    if i >= 2:
                        eng.wait_ge(s["dv"].h, dv0 + 2 * (i - 2) + 2)
                    sem = s["xldA"] if i % 2 == 0 else s["xldB"]
                    r0 = a0 + i * 128
                    w = min(128, NQ - r0)
                    eng.dma_start(out=xs[0:w, (i % 2) * D:(i % 2) * D + D],
                                  in_=io["x2_scr"][r0:r0 + w, :]).then_inc(sem.h, 16)
                    sem.n += 16

            def evac_fm(eng, j):
                eng.tensor_copy(
                    AP(xnfm, j * 128, [[8 * 512, 128], [512, 8], [1, 128]]),
                    psT.ap().rearrange("p (c t) -> p c t", c=8)).then_inc(s["ev"].h, 1)
                s["ev"].n += 1

            @blk.vector
            def _(eng, ntl=ntl, xA0=xA0, xB0=xB0, sc0=sc0, mm0=mm0):
                eng = AD(eng)
                for i in range(ntl):
                    if i % 2 == 0:
                        eng.wait_ge(s["xldA"].h, xA0 + 16 * (i // 2 + 1))
                    else:
                        eng.wait_ge(s["xldB"].h, xB0 + 16 * (i // 2 + 1))
                    xt = xs[:, (i % 2) * D:(i % 2) * D + D]
                    eng.scalar_tensor_tensor(dmp[:, :], xt, 1.0, xt, ALU.bypass,
                                             ALU.mult, accum_out=cl[:, 0:1])
                    eng.tensor_scalar(cl[:, 1:2], cl[:, 0:1], 1.0 / D, RMS_EPS,
                                      ALU.mult, ALU.add).then_inc(s["dv"].h, 1)
                    s["dv"].n += 1
                    eng.wait_ge(s["sc"].h, sc0 + i + 1)
                    eng.wait_ge(s["mm"].h, mm0 + i)
                    if i >= 1:
                        evac_fm(eng, i - 1)
                    eng.reciprocal(cl[:, 3:4], cl[:, 2:3])
                    eng.tensor_scalar(dmp[:, :], xt, cl[:, 3:4], None, ALU.mult)
                    eng.tensor_tensor(xnt[:, :], dmp[:, :], sb["g2b"][:, :],
                                      ALU.mult).then_inc(s["dv"].h, 1)
                    s["dv"].n += 1
                eng.wait_ge(s["mm"].h, mm0 + ntl)
                evac_fm(eng, ntl - 1)

            @blk.scalar
            def _(eng, ntl=ntl, dv0=dv0):
                eng = AD(eng)
                for i in range(ntl):
                    eng.wait_ge(s["dv"].h, dv0 + 2 * i + 1)
                    eng.activation(cl[:, 2:3], cl[:, 1:2], AF.Sqrt).then_inc(s["sc"].h, 1)
                    s["sc"].n += 1

            @blk.tensor
            def _(eng, ntl=ntl, dv0=dv0, ev0=ev0):
                for i in range(ntl):
                    eng.wait_ge(s["dv"].h, dv0 + 2 * i + 2)
                    if i >= 1:
                        eng.wait_ge(s["ev"].h, ev0 + i)
                    inst = None
                    for ch in range(8):
                        inst = eng.transpose(psT[:, ch * 128:(ch + 1) * 128],
                                             xnt[:, ch * 128:(ch + 1) * 128],
                                             sb["c_ident"][:, :])
                    inst.then_inc(s["mm"].h, 1)
                    s["mm"].n += 1
            evF = s["ev"].n
            mmP = s["mm"].n
            scP = s["sc"].n
            dvP = s["dv"].n

            @blk.tensor
            def _(eng, alen=alen, evF=evF, gpw=gpw, scP=scP, dvP=dvP):
                eng.wait_ge(s["ev"].h, evF)
                eng.wait_ge(s["gp"].h, gpw)
                for oc in range(22):
                    if oc >= 1:
                        eng.wait_ge(s["sc"].h, scP + oc)
                        eng.wait_ge(s["dv"].h, dvP + 2 * oc - 1)
                    inst = None
                    for kc in range(8):
                        inst = eng.matmul(psG[:, 0:alen],
                                          wfg[:, kc * DS + oc * 128:kc * DS + oc * 128 + 128],
                                          xnfm[:, kc * 512:kc * 512 + alen],
                                          start=(kc == 0), stop=(kc == 7))
                    inst.then_inc(s["mm"].h, 1)
                    s["mm"].n += 1
                    inst = None
                    for kc in range(8):
                        inst = eng.matmul(psU[:, 0:alen],
                                          wfu[:, kc * DS + oc * 128:kc * DS + oc * 128 + 128],
                                          xnfm[:, kc * 512:kc * 512 + alen],
                                          start=(kc == 0), stop=(kc == 7))
                    inst.then_inc(s["mm"].h, 1)
                    s["mm"].n += 1

            @blk.scalar
            def _(eng, alen=alen, mmP=mmP, dvP=dvP):
                eng = AD(eng)
                for oc in range(22):
                    eng.wait_ge(s["mm"].h, mmP + 2 * oc + 1)
                    if oc >= 1:
                        eng.wait_ge(s["dv"].h, dvP + 2 * oc)
                    eng.activation(sg[:, 0:alen], psG[:, 0:alen],
                                   AF.Silu).then_inc(s["sc"].h, 1)
                    s["sc"].n += 1

            @blk.vector
            def _(eng, a0=a0, alen=alen, mmP=mmP, scP=scP):
                eng = AD(eng)
                for oc in range(22):
                    eng.wait_ge(s["mm"].h, mmP + 2 * oc + 2)
                    eng.wait_ge(s["sc"].h, scP + oc + 1)
                    eng.tensor_copy(ug[:, 0:alen], psU[:, 0:alen])
                    eng.drain().then_inc(s["dv"].h, 1)
                    s["dv"].n += 1
                    gidx = kb._spill_gidx
                    p = "spA" if gidx % 2 == 0 else "spB"
                    eng.wait_ge(s[p].h, s[p].n)
                    eng.tensor_tensor(gst[:, (oc % 2) * 512:(oc % 2) * 512 + alen],
                                      sg[:, 0:alen], ug[:, 0:alen],
                                      ALU.mult).then_inc(s["dv"].h, 1)
                    s["dv"].n += 1
                    kb._mf_spills.append((a0, alen, oc, p, s[p].n + 16, s["dv"].n))
                    kb._spill_gidx += 1
                    s[p].n += 16

            @blk.sync
            def _(eng):
                while kb._mf_emit < len(kb._mf_spills):
                    a0_, alen_, oc, p, tgt, dv_x = kb._mf_spills[kb._mf_emit]
                    kb._mf_emit += 1
                    eng.wait_ge(s["dv"].h, dv_x)
                    eng.wait_ge(s[p].h, tgt - 16)
                    eng.dma_start(
                        out=io["gt_scr"][:, oc * NQ + a0_:oc * NQ + a0_ + alen_],
                        in_=gst[:, (oc % 2) * 512:(oc % 2) * 512 + alen_]).then_inc(s[p].h, 16)

        spfA, spfB = s["spA"].n, s["spB"].n

        @blk.sync
        def _(eng):
            eng.wait_ge(s["spA"].h, spfA)
            eng.wait_ge(s["spB"].h, spfB)

    kb._mf_spills = []
    kb._mf_emit = 0
    matmul_fm_groups(kb, blk, "ef2", "w_fd", 22, 8, kb.io["gt_scr"], NQ,
                     kb.io["x2b_scr"], kb.io["x2_scr"], 0,
                     resid2=kb.io["x1_scr"], resid2_off=KVQ)



def phase_s3(kb, blk):
    nc, s, io, sb = kb.nc, kb.s, kb.io, kb.sb
    kb.sb["w1q"] = kb.sbuf("w1q_b", [128, 8 * DH], FP8)
    kb.sb["w2q"] = kb.sbuf("w2q_b", [128, 32 * D], FP8)
    wld0 = s["wld"].n

    @blk.sync
    def _(eng):
        eng.dma_start(out=kb.sb["w1q"][:, :], in_=io["w1q_scr"][:, :]).then_inc(s["wld"].h, 16)
        eng.dma_start(out=kb.sb["w2q"][:, :], in_=io["w2q_scr"][:, :]).then_inc(s["wld"].h, 16)
        s["wld"].n += 32
    wldw = s["wld"].n

    R0 = 14

    def xin(eng, i, dst, sem):
        r0 = R0 + i * 128
        w = min(128, NQ - r0)
        eng.dma_start(out=dst[0:w, :],
                      in_=io["x2b_scr"][r0:r0 + w, :]).then_inc(sem.h, 16)

    def resid(eng, tok0, w, dst):
        eng.dma_start(out=dst[0:w, :],
                      in_=io["x2b_scr"][R0 + tok0:R0 + tok0 + w, :]).then_inc(s["xld"].h, 16)

    def outd(eng, tok0, w, src):
        eng.dma_start(out=io["xout"][tok0 - 2:tok0 - 2 + w, :],
                      in_=src[0:w, :]).then_inc(s["sp"].h, 16)

    groups = [(2 + GLEN * k, GLEN) for k in range(2048 // GLEN)]
    edge_fix = [(groups[0][0], 0, sb["c_edgel"][:, 0:1]),
                (groups[-1][0], GLEN + 1, sb["c_edger"][:, 0:1])]
    emit_bitffn(kb, blk, "s3", 17, groups, xin, resid, outd,
                edge_fix=edge_fix, pre_wait=[("wld", wldw)])


PHASES = ["p0", "s1", "s2ab", "s2c", "s2d", "s2ef", "s3"]


def build_program(debug=(), upto=None):
    AD.reset()
    nc = bass.Bass("TRN2", target_bir_lowering=False)
    kb = KB(nc, debug, upto)
    declare_io(kb)
    fns = dict(p0=phase_p0, s1=phase_s1, s2ab=phase_s2ab, s2c=phase_s2c,
               s2d=phase_s2d, s2ef=phase_s2ef, s3=phase_s3)
    kb.phase_marks = []
    with nc.Block() as blk:
        for ph in PHASES:
            fns[ph](kb, blk)
            kb.barrier(blk)
            kb.phase_marks.append((ph, len(list(nc.all_instructions()))))
            if ph == "s1":
                kb.free("w2q")
                kb.free("w1q")
            if upto == ph:
                break
    return nc, kb


def kernel(**inputs):
    nc, kb = build_program()
    in_maps = [host_core_inputs(c, inputs) for c in range(NCORE)]
    res = run_bass_kernel_spmd(nc, in_maps, core_ids=list(range(NCORE)))
    out = np.zeros((B, N, D), dtype=np.float32)
    for c in range(NCORE):
        b, j = c // 4, c % 4
        out[b, j * CHUNK:(j + 1) * CHUNK] = res.results[c]["xout"]
    return out
